# version 05
# baseline (speedup 1.0000x reference)
import numpy as np
import ml_dtypes
import concourse.bass as bass
import concourse.mybir as mybir
from concourse.bass_utils import run_bass_kernel_spmd

F32 = mybir.dt.float32
BF16 = mybir.dt.bfloat16
U8 = mybir.dt.uint8
AX = mybir.AxisListType
ALU = mybir.AluOpType
ACT = mybir.ActivationFunctionType

NCORES = 8
T = 8192
HID = 2048
Tc = T // NCORES
C = 128
NCH = Tc // C
H = 16
NSL = HID // 128
DH = 8192
SCALE = float(128) ** -0.5

ENGINES = ("sync", "tensor", "vector", "scalar", "gpsimd")
STAGES = "all"
WANT_DBG = ()


class Buf:
    __slots__ = ("ap", "w", "r", "rng", "fresh")

    def __init__(self, ap, rng=None):
        self.ap = ap
        self.w = None
        self.r = {}
        self.rng = rng
        self.fresh = True

    def add_r(self, tok):
        s, v = tok
        if self.r.get(s, -1) < v:
            self.r[s] = v

    def r_toks(self):
        return [(s, v) for s, v in self.r.items()]

    def __getitem__(self, key):
        return self.ap[key]


class Bank(Buf):
    __slots__ = ("bf",)

    def __init__(self, ap):
        super().__init__(ap)
        self.bf = ap.bitcast(BF16)


class Plan:
    LANES = {"ld": 4, "lw": 6, "st": 4}

    def __init__(self, nc):
        self.nc = nc
        self.plan = []
        self.sems = {}
        self.cnt = {}
        for s in ("ts", "vs", "ss", "gs", "cs"):
            self.sems[s] = nc.alloc_semaphore(s)
            self.cnt[s] = 0
        self.rr = {}
        for q, n in self.LANES.items():
            self.rr[q] = 0
            for i in range(n):
                name = f"{q}{i}"
                self.sems[name] = nc.alloc_semaphore(name)
                self.cnt[name] = 0

    def lane(self, q):
        i = self.rr[q] % self.LANES[q]
        self.rr[q] += 1
        name = f"{q}{i}"
        pre = (name, self.cnt[name]) if self.cnt[name] else None
        return name, pre

    def emit(self, eng, fn, waits, inc, inc_by):
        tok = None
        if inc is not None:
            self.cnt[inc] += inc_by
            tok = (inc, self.cnt[inc])
        self.plan.append((eng, fn, tuple(w for w in waits if w is not None),
                          (inc, inc_by) if inc else None))
        return tok

    def realize(self, block):
        wm = {e: {} for e in ENGINES}

        def mk(eng_name):
            def body(eng):
                for (e, fn, waits, inc) in self.plan:
                    if e != eng_name:
                        continue
                    for (s, v) in waits:
                        if wm[e].get(s, 0) < v:
                            eng.wait_ge(self.sems[s], v)
                            wm[e][s] = v
                    ins = fn(eng)
                    if inc is not None:
                        s, by = inc
                        ins.then_inc(self.sems[s], by)
            return body

        block.sync(mk("sync"))
        block.tensor(mk("tensor"))
        block.vector(mk("vector"))
        block.scalar(mk("scalar"))
        block.gpsimd(mk("gpsimd"))


def bc_in(ap, n):
    return bass.AP(ap.tensor, ap.offset, [list(p) for p in ap.ap] + [[0, n]])


def bc_mid(ap, n):
    pr = [list(p) for p in ap.ap]
    return bass.AP(ap.tensor, ap.offset, [pr[0], [0, n]] + pr[1:])


class Builder:
    SEM_OF = {"sync": "ld", "gpsimd_dma": "lw", "tensor": "ts", "vector": "vs",
              "scalar": "ss", "gpsimd": "gs"}

    def __init__(self):
        nc = bass.Bass("TRN2", target_bir_lowering=False, debug=False)
        self.nc = nc
        self.P = Plan(nc)
        self.dbg = {}

        I = {}
        def inp(name, shape, dtype=F32):
            I[name] = nc.dram_tensor(name, shape, dtype, kind="ExternalInput").ap()
        self.I = I
        inp("hT_ext", [HID, Tc + 3]); inp("h_nat", [Tc, HID])
        for w in ("Wq", "Wk", "Wv", "Ww", "Wo"):
            inp(w, [HID, HID])
        for w in ("cq", "ck", "cv"):
            inp(w, [HID, 4])
        inp("Wf1", [HID, 128]); inp("Wf2", [128, HID]); inp("Wb", [HID, H])
        inp("Wg1", [HID, 128]); inp("Wg2", [128, HID])
        inp("bg2", [HID]); inp("normw", [128])
        inp("wgT", [HID, DH]); inp("wuT", [HID, DH]); inp("wdT", [DH, HID])
        inp("Mmat", [9, 8])
        self.out = nc.dram_tensor("out", [Tc, HID], F32, kind="ExternalOutput").ap()

        self.pub1 = nc.dram_tensor("pub1", [128, H * 256], BF16).ap()
        self.pub2 = nc.dram_tensor("pub2", [128, H], F32).ap()
        self.agg1 = nc.dram_tensor("agg1", [NCORES * 128, H * 256], BF16, addr_space="Shared").ap()
        self.agg2 = nc.dram_tensor("agg2", [NCORES * 128, H], F32, addr_space="Shared").ap()
        self.o_scr = nc.dram_tensor("o_scr", [HID, Tc], BF16).ap()

        def sb(name, shape, dtype):
            return Buf(nc.alloc_sbuf_tensor(name, shape, dtype).ap())
        self.trilS = sb("trilS", [128, 128], F32)
        self.tril1 = sb("tril1", [128, 128], BF16)
        self.identb = sb("identb", [128, 128], BF16)
        self.identf = sb("identf", [16, 16], F32)
        self.diagmask = sb("diagmask", [128, 128], BF16)
        self.U2n = sb("U2n", [128, 128], F32)
        self.U3n = sb("U3n", [128, 128], F32)
        self.onescol = sb("onescol", [128, 1], F32)
        self.halfcol = sb("halfcol", [128, 1], F32)
        self.colpair = sb("colpair", [128, 2], F32)
        self.ones1p = sb("ones1p", [1, 128], F32)
        self.cwt = sb("cwt", [128, NSL, 16], F32)
        self.Wf2b = sb("Wf2b", [128, HID], BF16)
        self.Wg2b = sb("Wg2b", [128, HID], BF16)
        self.Wbb = sb("Wbb", [128, NSL, H], BF16)
        self.bg2T = sb("bg2T", [128, HID], BF16)
        self.nwcol = sb("nwcol", [128, 1], F32)
        self.MmatS = sb("MmatS", [9, 8], F32)
        self.f1c = sb("f1c", [128, Tc], BF16)
        self.G1c = sb("G1c", [128, Tc], BF16)
        self.SkTs = sb("SkTs", [128, H, 128], BF16)
        self.Svs = sb("Svs", [128, H, 128], BF16)
        self.Btn = sb("Btn", [128, H], F32)
        self.ecol = sb("ecol", [128, 2 * H], F32)
        self.beta_tm = sb("beta_tm", [128, NCH, H], F32)
        self.smx = sb("smx", [128, 16], F32)
        self.rst = sb("rst", [128, 16], F32)
        self.lnc = sb("lnc", [128, 8], F32)
        self.cpool = sb("cpool", [128, 4], F32)

        self.alias_reg = []
        self.AR = 182272
        self.arena = nc.alloc_sbuf_tensor("arena", [128, self.AR], U8)
        self.sbuf_slack = nc.sbuf_bytes_remaining
        assert self.sbuf_slack >= 0, f"sbuf over by {-self.sbuf_slack}"

        a = self.ab
        self.kc = a(0, [NSL, Tc], BF16)
        self.vc = a(32768, [NCH, HID], BF16)
        self.wc = a(65536, [NCH, HID], BF16)
        self.qc = a(98304, [NSL, Tc], BF16)
        self.hTb = a(131072, [NSL, Tc + 3], BF16)
        self.wsl = [a(164096, [NSL, 128], BF16), a(168192, [NSL, 128], BF16)]
        self.cvo = [a(172288, [264], F32), a(173376, [264], F32)]
        self.svo = [a(174464, [264], BF16), a(175040, [264], BF16)]
        self.betaR = a(175616, [1024], F32)
        self.bgrow = a(0, [HID], F32)
        self.Uh = a(8192, [128], F32)
        self.sq2 = a(131072, [HID], BF16)
        self.spA = a(135168, [HID], F32)
        self.escrA = a(143360, [HID], BF16)
        self.bwdA = a(147456, [HID], BF16)
        self.ktmA = a(151552, [H, 128], BF16)
        self.diagA = a(155648, [H, 128], BF16)
        self.aggS = a(131072, [NCORES, 4, 256], BF16)
        self.augB = a(139264, [HID], F32)
        self.WcolS = a(147456, [H, 8], F32)
        self.diagC = a(155648, [NCORES, 128], BF16)
        self.spT = a(131072, [HID], F32)
        self.expBp = a(139264, [HID], BF16)
        self.escr = a(143360, [HID], BF16)
        self.bwR = a(147456, [HID], BF16)
        self.bwd = a(151552, [HID], BF16)
        self.gsig = a(155648, [HID], BF16)
        self.diagB = a(159744, [H, 128], BF16)
        self.ktm = a(163840, [H, 128], BF16)
        self.oTst = a(167936, [4, 128], BF16)
        self.gPM = a(168960, [4, 128], BF16)
        self.gRT = a(169984, [4, 128], BF16)
        self.gSR = a(171008, [4, 128], BF16)
        self.gAB = a(172032, [4, 128], BF16)
        self.gAT = a(173056, [4, 128], BF16)
        self.gP2 = a(174080, [4, 128], BF16)
        self.gOF = a(175104, [4, 128], BF16)
        self.sgrp = a(176128, [512], F32)
        self.ogrp = a(178176, [512], F32)
        self.svr = a(180224, [4, 128], BF16)
        self.diagR = a(181248, [4, 128], BF16)
        self.oT = a(0, [NSL, Tc], BF16)
        self.r1a = a(32768, [NCH, HID], F32)
        self.xT = a(98304, [NSL, Tc], BF16)
        self.wo_b = [a(131072 + i * 4096, [HID], BF16) for i in range(8)]
        self.sqscr = a(163840, [HID], BF16)
        self.xs = [a(167936, [HID], F32)]
        self.gb = a(131072, [8, Tc], BF16)
        self.ffg = [a(0, [NSL, 128], BF16), a(4096, [NSL, 128], BF16)]
        self.ffu = [a(8192, [NSL, 128], BF16), a(12288, [NSL, 128], BF16)]
        self.sgf = a(16384, [Tc], F32)
        self.wd_all = ([a(20480 + i * 4096, [HID], BF16) for i in range(3)] +
                       [a(147456 + i * 4096, [HID], BF16) for i in range(4)] +
                       [a(163840, [HID], BF16)])
        self.sqscr2 = a(131072, [HID], BF16)

        self.banks = [Bank(nc.alloc_psum_tensor(f"pb{i}", [128, 512], F32).ap())
                      for i in range(8)]

    def ab(self, off, shape, dtype):
        esz = mybir.dt.size(dtype)
        n = int(np.prod(shape))
        assert off + n * esz <= self.AR, f"arena overflow at {off}+{n*esz}"
        ap = self.arena.ap()[:, off:off + n * esz].bitcast(dtype)
        if len(shape) > 1:
            dims = " ".join(f"d{i}" for i in range(len(shape)))
            kw = {f"d{i}": int(s) for i, s in enumerate(shape)}
            ap = ap.rearrange(f"p ({dims}) -> p {dims}", **kw)
        b = Buf(ap, rng=(off, off + n * esz))
        self.alias_reg.append(b)
        return b

    def _write_waits(self, b, waits):
        if b.w is not None:
            waits.append(b.w)
        waits.extend(b.r_toks())
        if b.fresh and b.rng is not None:
            lo, hi = b.rng
            for o in self.alias_reg:
                if o is b or o.rng is None:
                    continue
                if o.rng[0] < hi and lo < o.rng[1]:
                    if o.w is not None:
                        waits.append(o.w)
                    waits.extend(o.r_toks())
            b.fresh = False

    def op(self, eng, fn, reads=(), writes=(), extra=(), inc=None, inc_by=1):
        if inc is None:
            inc = self.SEM_OF[eng]
        waits = list(extra)
        for b in reads:
            if b.w is not None:
                waits.append(b.w)
        for b in writes:
            self._write_waits(b, waits)
        tok = self.P.emit(eng, fn, waits, inc, inc_by)
        for b in reads:
            b.add_r(tok)
        for b in writes:
            b.w = tok
            b.r = {}
        return tok

    def dma(self, eng, out_ap, in_ap, reads=(), writes=(), extra=(),
            slow_ok=False):
        q = "lw" if eng == "gpsimd" else "ld"
        sem, pre = self.P.lane(q)
        waits = list(extra)
        if pre is not None:
            waits.append(pre)
        for b in reads:
            if b.w is not None:
                waits.append(b.w)
        for b in writes:
            self._write_waits(b, waits)
        tok = self.P.emit(eng, lambda e: e.dma_start(
            out=out_ap, in_=in_ap, allow_slow_non_contiguous=slow_ok),
            waits, sem, 16)
        for b in reads:
            b.add_r(tok)
        for b in writes:
            b.w = tok
            b.r = {}
        return tok

    def dma_store(self, out_ap, in_ap, bufs, extra=(), eng="sync"):
        sem, pre = self.P.lane("st")
        waits = list(extra)
        if pre is not None:
            waits.append(pre)
        for b in bufs:
            if b.w is not None:
                waits.append(b.w)
        tok = self.P.emit(eng, lambda e: e.dma_start(out=out_ap, in_=in_ap),
                          waits, sem, 16)
        for b in bufs:
            b.add_r(tok)
        return tok

    def T_(self, fn, **kw):
        return self.op("tensor", fn, **kw)

    def V_(self, fn, **kw):
        return self.op("vector", fn, **kw)

    def S_(self, fn, **kw):
        return self.op("scalar", fn, **kw)

    def G_(self, fn, **kw):
        return self.op("gpsimd", fn, **kw)

    def tap(self, name, buf, shape, dtype):
        if name not in WANT_DBG:
            return
        d = self.nc.dram_tensor("dbg_" + name, shape, dtype,
                                kind="ExternalOutput").ap()
        self.dbg[name] = d
        self.dma_store(d[:, :] if len(shape) == 2 else d, buf.ap, [buf])

    def build(self):
        self.p0_constants()
        self.a1_projections()
        self.a2_aux()
        if STAGES != "A1":
            self.a3_l2norm()
            self.a4_states()
        if STAGES in ("A5", "B", "B3", "all"):
            self.a5_combine()
        if STAGES in ("B", "B3", "all"):
            self.b_scan()
        if STAGES in ("B3", "all"):
            self.b3_wo_ln1()
        if STAGES == "all":
            self.b4_ffn()
        self.finish()
        with self.nc.Block() as block:
            self.P.realize(block)
        return self.nc

    def p0_constants(self):
        G_, V_, S_, T_ = self.G_, self.V_, self.S_, self.T_
        I = self.I

        def mask(buf, fill_first, pat, cmp, fill, cmul, base=0):
            G_(lambda g: g.memset(buf.ap[:, :], fill_first), writes=[buf])
            G_(lambda g: g.affine_select(buf.ap[:, :], buf.ap[:, :], pattern=pat,
                                         compare_op=cmp, fill=fill,
                                         channel_multiplier=cmul, base=base),
               writes=[buf])

        mask(self.trilS, SCALE, [[1, 128]], ALU.is_ge, 0.0, -1)
        mask(self.tril1, 1.0, [[1, 128]], ALU.is_ge, 0.0, -1)
        for t_ in (self.identb, self.diagmask):
            mask(t_, 0.0, [[-1, 128]], ALU.not_equal, 1.0, 1)
        mask(self.identf, 0.0, [[-1, 16]], ALU.not_equal, 1.0, 1)
        mask(self.U2n, 1.0, [[1, 128]], ALU.is_ge, 0.0, -1)
        mask(self.Uh, 1.0, [[0, 128]], ALU.is_ge, 0.0, -1, base=63)
        self.V_(lambda v: v.tensor_sub(self.U2n.ap[:, :], self.U2n.ap[:, :],
                                       self.Uh.ap[:, :]),
                reads=[self.Uh], writes=[self.U2n])
        mask(self.U3n, 1.0, [[-1, 128]], ALU.is_ge, 0.0, 1, base=-1)
        G_(lambda g: g.memset(self.onescol.ap[:, :], 1.0), writes=[self.onescol])
        mask(self.halfcol, 1.0, [[0, 1]], ALU.is_ge, 0.0, -1, base=63)
        G_(lambda g: g.memset(self.ones1p.ap[:, :], 1.0), writes=[self.ones1p])
        self.V_(lambda v: v.tensor_copy(self.colpair.ap[:, 0:1],
                                        self.onescol.ap[:, :]),
                reads=[self.onescol], writes=[self.colpair])
        self.V_(lambda v: v.tensor_copy(self.colpair.ap[:, 1:2],
                                        self.halfcol.ap[:, :]),
                reads=[self.halfcol], writes=[self.colpair])
        for i, val in enumerate((1e-6, 1.0, 1e-5)):
            G_(lambda g, i=i, val=val: g.memset(self.cpool.ap[:, i:i + 1], val),
               writes=[self.cpool])

        for i, cwn in enumerate(("cq", "ck", "cv", "cv")):
            self.dma("sync", self.cwt.ap[:, :, 4 * i:4 * i + 4],
                     I[cwn].rearrange("(s p) t -> p s t", p=128),
                     writes=[self.cwt])
        self.dma("gpsimd", self.Wf2b.ap[:, :], I["Wf2"][:, :], writes=[self.Wf2b])
        self.dma("gpsimd", self.Wg2b.ap[:, :], I["Wg2"][:, :], writes=[self.Wg2b])
        self.dma("gpsimd", self.Wbb.ap[:, :, :],
                 I["Wb"].rearrange("(s p) h -> p s h", p=128), writes=[self.Wbb])
        self.dma("sync", self.nwcol.ap[:, :],
                 I["normw"].rearrange("(p o) -> p o", o=1), writes=[self.nwcol])
        self.dma("sync", self.MmatS.ap[:, :], I["Mmat"][:, :], writes=[self.MmatS])
        self.dma("sync", self.bgrow.ap[0:1, :],
                 I["bg2"].rearrange("(o f) -> o f", o=1), writes=[self.bgrow])
        for j in range(4):
            bk = self.banks[4 + j]
            self.T_(lambda t, j=j, bk=bk: t.matmul(
                bk.ap[:, :], self.ones1p.ap[:, :],
                self.bgrow.ap[0:1, j * 512:(j + 1) * 512], start=True, stop=True),
                reads=[self.ones1p, self.bgrow], writes=[bk])
            self.V_(lambda v, j=j, bk=bk: v.tensor_copy(
                self.bg2T.ap[:, j * 512:(j + 1) * 512], bk.ap[:, :]),
                reads=[bk], writes=[self.bg2T])
        self.dma("gpsimd", self.hTb.ap[:, :, :],
                 I["hT_ext"].rearrange("(s p) t -> p s t", p=128),
                 writes=[self.hTb])

    def a1_projections(self):
        T_, V_, S_ = self.T_, self.V_, self.S_
        I = self.I
        streams = [("q", "Wq", 0), ("k", "Wk", 1), ("v", "Wv", 2), ("w", "Ww", 3)]
        for sname, wn, sidx in streams:
            for fs in range(NSL):
                par = (sidx * NSL + fs) % 2
                wb = self.wsl[par]
                self.dma("gpsimd", wb.ap[:, :, :],
                         I[wn][:, fs * 128:(fs + 1) * 128].rearrange(
                             "(a p) c -> p a c", p=128), writes=[wb])
                for g in range(4):
                    t0 = g * 256
                    pb = self.banks[g]
                    for a in range(NSL):
                        self.T_(lambda t, a=a, pb=pb, t0=t0, wb=wb: t.matmul(
                            pb.ap[:, 0:259], wb.ap[:, a, :],
                            self.hTb.ap[:, a, t0:t0 + 259],
                            start=(a == 0), stop=(a == NSL - 1)),
                            reads=[wb, self.hTb], writes=[pb])
                    cp = g % 2
                    co = self.cvo[cp]
                    cb = sidx * 4
                    self.V_(lambda v, pb=pb, co=co, fs=fs, cb=cb: v.tensor_scalar(
                        co.ap[:, 0:256], pb.ap[:, 0:256],
                        self.cwt.ap[:, fs, cb:cb + 1], None, ALU.mult),
                        reads=[pb, self.cwt], writes=[co])
                    for i in range(1, 4):
                        self.V_(lambda v, pb=pb, co=co, fs=fs, cb=cb, i=i:
                                v.scalar_tensor_tensor(
                                    co.ap[:, 0:256], pb.ap[:, i:256 + i],
                                    self.cwt.ap[:, fs, cb + i:cb + i + 1],
                                    co.ap[:, 0:256], ALU.mult, ALU.add),
                                reads=[pb, self.cwt], writes=[co])
                    if sidx == 0:
                        self.S_(lambda s, co=co: s.activation(
                            co.ap[:, 0:256], co.ap[:, 0:256], ACT.Silu),
                            writes=[co])
                        self.V_(lambda v, co=co, fs=fs, t0=t0: v.tensor_scalar(
                            self.qc.ap[:, fs, t0:t0 + 256], co.ap[:, 0:256],
                            SCALE, None, ALU.mult), reads=[co],
                            writes=[self.qc])
                    elif sidx == 1:
                        self.S_(lambda s, co=co, fs=fs, t0=t0: s.activation(
                            self.kc.ap[:, fs, t0:t0 + 256], co.ap[:, 0:256],
                            ACT.Silu), reads=[co], writes=[self.kc])
                    else:
                        so = self.svo[cp]
                        self.S_(lambda s, co=co, so=so: s.activation(
                            so.ap[:, 0:256], co.ap[:, 0:256], ACT.Silu),
                            reads=[co], writes=[so])
                        dst = self.vc if sidx == 2 else self.wc
                        tb = self.banks[4 + cp]
                        for hf in range(2):
                            self.T_(lambda t, tb=tb, so=so, hf=hf: t.transpose(
                                tb.bf[:, hf * 128:(hf + 1) * 128],
                                so.ap[:, hf * 128:hf * 128 + 128],
                                self.identb.ap[:, :]),
                                reads=[so, self.identb],
                                writes=[tb])
                        self.V_(lambda v, tb=tb, dst=dst, g=g, fs=fs: v.tensor_copy(
                            dst.ap[:, 2 * g:2 * g + 2, fs * 128:(fs + 1) * 128],
                            tb.bf[:, 0:256].rearrange("p (a b) -> p a b", a=2)),
                            reads=[tb], writes=[dst])
        self.tap("qc", self.qc, [128, NSL, Tc], BF16)
        self.tap("kc", self.kc, [128, NSL, Tc], BF16)
        self.tap("vc", self.vc, [128, NCH, HID], BF16)
        self.tap("wcr", self.wc, [128, NCH, HID], BF16)

    def a2_aux(self):
        T_, V_, S_ = self.T_, self.V_, self.S_
        I = self.I
        for (dst, wn) in ((self.f1c, "Wf1"), (self.G1c, "Wg1")):
            wb = self.wsl[0]
            self.dma("gpsimd", wb.ap[:, :, :],
                     I[wn].rearrange("(a p) c -> p a c", p=128), writes=[wb])
            for half in range(2):
                pb = self.banks[4 + half]
                for a in range(NSL):
                    self.T_(lambda t, a=a, pb=pb, half=half, wb=wb: t.matmul(
                        pb.ap[:, :], wb.ap[:, a, :],
                        self.hTb.ap[:, a, 3 + half * 512:3 + (half + 1) * 512],
                        start=(a == 0), stop=(a == NSL - 1)),
                        reads=[wb, self.hTb], writes=[pb])
                self.V_(lambda v, pb=pb, dst=dst, half=half: v.tensor_copy(
                    dst.ap[:, half * 512:(half + 1) * 512], pb.ap[:, :]),
                    reads=[pb], writes=[dst])
        pb = self.banks[6]
        for half in range(2):
            for a in range(NSL):
                self.T_(lambda t, a=a, half=half, pb=pb: t.matmul(
                    pb.ap[0:16, 0:512], self.Wbb.ap[:, a, :],
                    self.hTb.ap[:, a, 3 + half * 512:3 + (half + 1) * 512],
                    start=(a == 0), stop=(a == NSL - 1)),
                    reads=[self.Wbb, self.hTb], writes=[pb])
            self.V_(lambda v, pb=pb, half=half: v.tensor_copy(
                self.betaR.ap[0:16, half * 512:(half + 1) * 512],
                pb.ap[0:16, 0:512]),
                reads=[pb], writes=[self.betaR])
        for ch in range(NCH):
            pb7 = self.banks[7]
            self.T_(lambda t, ch=ch, pb7=pb7: t.transpose(
                pb7.ap[:, 0:16], self.betaR.ap[0:16, ch * 128:(ch + 1) * 128],
                self.identf.ap[:, :]),
                reads=[self.betaR, self.identf], writes=[pb7])
            self.S_(lambda s, ch=ch, pb7=pb7: s.activation(
                self.beta_tm.ap[:, ch, :], pb7.ap[:, 0:16], ACT.Exp, scale=-1.0),
                reads=[pb7], writes=[self.beta_tm])
            self.V_(lambda v, ch=ch: v.tensor_scalar(
                self.beta_tm.ap[:, ch, :], self.beta_tm.ap[:, ch, :], 1.0, None,
                ALU.add), writes=[self.beta_tm])
            self.V_(lambda v, ch=ch: v.reciprocal(
                self.beta_tm.ap[:, ch, :], self.beta_tm.ap[:, ch, :]),
                writes=[self.beta_tm])
        self.tap("f1c", self.f1c, [128, Tc], BF16)
        self.tap("G1c", self.G1c, [128, Tc], BF16)
        self.tap("beta", self.beta_tm, [128, NCH, H], F32)

    def a3_l2norm(self):
        for ch in range(NCH):
            self.V_(lambda v, ch=ch: v.tensor_mul(
                self.sq2.ap[:, :], self.wc.ap[:, ch, :], self.wc.ap[:, ch, :]),
                reads=[self.wc], writes=[self.sq2])
            self.V_(lambda v: v.tensor_reduce(
                self.rst.ap[:, :], self.sq2.ap[:, :].rearrange(
                    "p (h m) -> p h m", h=H), AX.X, ALU.add),
                reads=[self.sq2], writes=[self.rst])
            self.S_(lambda s: s.activation(self.rst.ap[:, :], self.rst.ap[:, :],
                                           ACT.Ln, bias=self.cpool.ap[:, 0:1]),
                    reads=[self.cpool], writes=[self.rst])
            self.S_(lambda s: s.activation(self.rst.ap[:, :], self.rst.ap[:, :],
                                           ACT.Exp, scale=-0.5), writes=[self.rst])
            self.V_(lambda v, ch=ch: v.tensor_mul(
                self.rst.ap[:, :], self.rst.ap[:, :], self.beta_tm.ap[:, ch, :]),
                reads=[self.beta_tm], writes=[self.rst])
            self.V_(lambda v, ch=ch: v.tensor_tensor(
                self.wc.ap[:, ch, :].rearrange("p (h m) -> p h m", h=H),
                self.wc.ap[:, ch, :].rearrange("p (h m) -> p h m", h=H),
                bc_in(self.rst.ap[:, :], 128), ALU.mult),
                reads=[self.rst], writes=[self.wc])
        self.tap("wcn", self.wc, [128, NCH, HID], BF16)

    def emit_sp(self, sp, ch):
        for j in range(4):
            pb = self.banks[j]
            self.T_(lambda t, j=j, ch=ch, pb=pb: t.matmul(
                pb.ap[:, :], self.f1c.ap[:, ch * 128:(ch + 1) * 128],
                self.Wf2b.ap[:, j * 512:(j + 1) * 512], start=True, stop=True),
                reads=[self.f1c, self.Wf2b], writes=[pb])
            self.S_(lambda s, j=j, pb=pb, sp=sp: s.activation(
                sp.ap[:, j * 512:(j + 1) * 512], pb.ap[:, :], ACT.Exp, scale=-1.0),
                reads=[pb], writes=[sp])
            self.S_(lambda s, j=j, sp=sp: s.activation(
                sp.ap[:, j * 512:(j + 1) * 512], sp.ap[:, j * 512:(j + 1) * 512],
                ACT.Ln, bias=self.cpool.ap[:, 1:2]),
                reads=[self.cpool], writes=[sp])

    def emit_cumexp(self, U, sp, dst_scales):
        for j in range(4):
            pb = self.banks[j]
            self.T_(lambda t, j=j, pb=pb, U=U, sp=sp: t.matmul(
                pb.ap[:, :], U.ap[:, :], sp.ap[:, j * 512:(j + 1) * 512],
                start=True, stop=True), reads=[U, sp], writes=[pb])
            for (dst, sc) in dst_scales:
                self.S_(lambda s, j=j, pb=pb, dst=dst, sc=sc: s.activation(
                    dst.ap[:, j * 512:(j + 1) * 512], pb.ap[:, :], ACT.Exp,
                    scale=sc), reads=[pb], writes=[dst])

    def emit_cols(self, sp, col, dst_ap, bank_i=5):
        pb = self.banks[bank_i]
        for h in range(H):
            self.T_(lambda t, h=h, pb=pb, sp=sp, col=col: t.matmul(
                pb.ap[:, h:h + 1], sp.ap[:, h * 128:(h + 1) * 128], col.ap[:, :],
                start=True, stop=True),
                reads=[sp, col], writes=[pb])
        return self.S_(lambda s, pb=pb, dst_ap=dst_ap: s.activation(
            dst_ap, pb.ap[:, 0:H], ACT.Exp, scale=-1.0),
            reads=[pb, self.ecol], writes=[self.ecol])

    def emit_cols2(self, sp, bank_i=5):
        pb = self.banks[bank_i]
        for h in range(H):
            self.T_(lambda t, h=h, pb=pb, sp=sp: t.matmul(
                pb.ap[:, 2 * h:2 * h + 2], sp.ap[:, h * 128:(h + 1) * 128],
                self.colpair.ap[:, :], start=True, stop=True),
                reads=[sp, self.colpair], writes=[pb])
        ec = self.ecol.ap[:, 0:2 * H].rearrange("p (a b) -> p b a", a=2) \
            if False else None
        pv = pb.ap[:, 0:2 * H].rearrange("p (h w) -> p w h", h=H)
        dv = self.ecol.ap[:, 0:2 * H].rearrange("p (w h) -> p w h", w=2)
        self.S_(lambda s: s.activation(dv, pv, ACT.Exp, scale=-1.0),
                reads=[pb], writes=[self.ecol])

    def emit_diag(self, dst, cols_ap, n=H, reads=()):
        for h in range(n):
            self.V_(lambda v, h=h, dst=dst, cols_ap=cols_ap: v.tensor_scalar(
                dst.ap[:, h, :], self.diagmask.ap[:, :],
                cols_ap[:, h:h + 1], None, ALU.mult),
                reads=[self.diagmask] + list(reads),
                writes=[dst])

    def emit_ktm(self, dst, ch, bank0=6):
        for g8 in range(2):
            pbT = self.banks[bank0 + g8]
            for hi in range(8):
                h = g8 * 8 + hi
                self.T_(lambda t, h=h, ch=ch, pbT=pbT, hi=hi: t.transpose(
                    pbT.bf[:, hi * 128:(hi + 1) * 128],
                    self.kc.ap[:, h, ch * 128:(ch + 1) * 128],
                    self.identb.ap[:, :]),
                    reads=[self.kc, self.identb], writes=[pbT])
            self.V_(lambda v, g8=g8, pbT=pbT, dst=dst: v.tensor_copy(
                dst.ap[:, g8 * 8:(g8 + 1) * 8, :],
                pbT.bf[:, 0:1024].rearrange("p (a b) -> p a b", a=8)),
                reads=[pbT], writes=[dst])

    def emit_states(self, ktm, bwdb, diag, ch, bank0=0):
        for g8 in range(2):
            evac_w = []
            for hi in range(8):
                h = g8 * 8 + hi
                bsk = self.banks[bank0 + hi // 4]
                bsv = self.banks[bank0 + 2 + hi // 4]
                sl = (hi % 4) * 128
                self.T_(lambda t, h=h, bsk=bsk, sl=sl, bwdb=bwdb, ktm=ktm:
                        t.matmul(bsk.ap[:, sl:sl + 128],
                                 bwdb.ap[:, h * 128:(h + 1) * 128],
                                 ktm.ap[:, h, :], start=True,
                                 stop=False),
                        reads=[bwdb, ktm], writes=[bsk])
                self.T_(lambda t, h=h, bsk=bsk, sl=sl, diag=diag: t.matmul(
                    bsk.ap[:, sl:sl + 128], diag.ap[:, h, :],
                    self.SkTs.ap[:, h, :], start=False, stop=True),
                    reads=[diag, self.SkTs], writes=[bsk])
                self.T_(lambda t, h=h, bsv=bsv, sl=sl, ch=ch, bwdb=bwdb: t.matmul(
                    bsv.ap[:, sl:sl + 128], bwdb.ap[:, h * 128:(h + 1) * 128],
                    self.vc.ap[:, ch, h * 128:(h + 1) * 128],
                    start=True, stop=False),
                    reads=[bwdb, self.vc], writes=[bsv])
                tok = self.T_(lambda t, h=h, bsv=bsv, sl=sl, diag=diag: t.matmul(
                    bsv.ap[:, sl:sl + 128], diag.ap[:, h, :],
                    self.Svs.ap[:, h, :], start=False, stop=True),
                    reads=[diag, self.Svs], writes=[bsv])
                evac_w.append(tok)
            for bi in range(2):
                bsk = self.banks[bank0 + bi]
                bsv = self.banks[bank0 + 2 + bi]
                hlo = g8 * 8 + bi * 4
                self.V_(lambda v, bsk=bsk, hlo=hlo: v.tensor_copy(
                    self.SkTs.ap[:, hlo:hlo + 4, :],
                    bsk.ap[:, :].rearrange("p (a b) -> p a b", a=4)),
                    reads=[bsk], writes=[self.SkTs], extra=evac_w)
                self.V_(lambda v, bsv=bsv, hlo=hlo: v.tensor_copy(
                    self.Svs.ap[:, hlo:hlo + 4, :],
                    bsv.ap[:, :].rearrange("p (a b) -> p a b", a=4)),
                    reads=[bsv], writes=[self.Svs], extra=evac_w)

    def a4_states(self):
        self.V_(lambda v: v.memset(self.SkTs.ap[:, :, :], 0.0), writes=[self.SkTs])
        self.V_(lambda v: v.memset(self.Svs.ap[:, :, :], 0.0), writes=[self.Svs])
        self.V_(lambda v: v.memset(self.Btn.ap[:, :], 0.0), writes=[self.Btn])
        for ch in range(NCH):
            self.emit_sp(self.spA, ch)
            pb5 = self.banks[5]
            for h in range(H):
                self.T_(lambda t, h=h, pb5=pb5: t.matmul(
                    pb5.ap[:, h:h + 1], self.spA.ap[:, h * 128:(h + 1) * 128],
                    self.onescol.ap[:, :], start=True, stop=True),
                    reads=[self.spA, self.onescol],
                    writes=[pb5])
            self.S_(lambda s, pb5=pb5: s.activation(
                self.ecol.ap[:, 0:H], pb5.ap[:, 0:H], ACT.Exp, scale=-1.0),
                reads=[pb5], writes=[self.ecol])
            self.V_(lambda v, pb5=pb5: v.tensor_add(
                self.Btn.ap[:, :], self.Btn.ap[:, :], pb5.ap[:, 0:H]),
                reads=[pb5], writes=[self.Btn])
            self.emit_cumexp(self.U3n, self.spA, [(self.escrA, -1.0)])
            self.V_(lambda v, ch=ch: v.tensor_mul(
                self.bwdA.ap[:, :], self.wc.ap[:, ch, :], self.escrA.ap[:, :]),
                reads=[self.wc, self.escrA], writes=[self.bwdA])
            self.emit_diag(self.diagA, self.ecol.ap[:, 0:H], reads=[self.ecol])
            self.emit_ktm(self.ktmA, ch)
            self.emit_states(self.ktmA, self.bwdA, self.diagA, ch)
        self.tap("SkT_loc", self.SkTs, [128, H, 128], BF16)
        self.tap("Sv_loc", self.Svs, [128, H, 128], BF16)
        self.tap("Btn", self.Btn, [128, H], F32)

    def a5_combine(self):
        V_, S_, T_ = self.V_, self.S_, self.T_
        pubv = self.pub1.rearrange("p (h two d) -> p h two d", h=H, two=2)
        t_p1a = self.dma_store(pubv[:, :, 0, :], self.SkTs.ap[:, :, :],
                               [self.SkTs])
        t_p1b = self.dma_store(pubv[:, :, 1, :], self.Svs.ap[:, :, :],
                               [self.Svs])
        t_p2 = self.dma_store(self.pub2[:, :], self.Btn.ap[:, :], [self.Btn])
        t_c1 = self.P.emit("gpsimd", lambda g: g.collective_compute(
            "AllGather", ALU.bypass, replica_groups=[list(range(NCORES))],
            ins=[self.pub1[:, :]], outs=[self.agg1[:, :]]),
            [t_p1a, t_p1b], "cs", 1)
        t_c2 = self.P.emit("gpsimd", lambda g: g.collective_compute(
            "AllGather", ALU.bypass, replica_groups=[list(range(NCORES))],
            ins=[self.pub2[:, :]], outs=[self.agg2[:, :]]),
            [t_p2], "cs", 1)
        augv = self.augB.ap.rearrange("p (m h) -> p m h", m=128)
        self.V_(lambda v: v.memset(self.augB.ap[0:9, :], 1.0), writes=[self.augB])
        self.dma("sync", augv[0:8, :, :],
                 self.agg2.rearrange("(c m) h -> c m h", m=128),
                 writes=[self.augB], extra=[t_c2])
        pb = self.banks[4]
        for h in range(H):
            self.T_(lambda t, h=h, pb=pb, augv=augv: t.matmul(
                pb.ap[:, (h % 8) * 8:(h % 8) * 8 + 8],
                augv[0:9, :, h:h + 1],
                self.MmatS.ap[:, :], start=True, stop=True),
                reads=[self.augB, self.MmatS],
                writes=[pb])
            if h % 8 == 7:
                hlo = h - 7
                self.S_(lambda s, pb=pb, hlo=hlo: s.activation(
                    self.WcolS.ap[:, hlo:hlo + 8, :].rearrange("p a b -> p (a b)"),
                    pb.ap[:, 0:64], ACT.Exp, scale=-1.0),
                    reads=[pb], writes=[self.WcolS])
        agg1v = self.agg1.rearrange("(c p) x -> p c x", p=128).rearrange(
            "p c (h d) -> p c h d", h=H)
        for hg in range(4):
            self.dma("sync", self.aggS.ap[:, :, :, :],
                     agg1v[:, :, hg * 4:(hg + 1) * 4, :],
                     writes=[self.aggS], extra=[t_c1])
            for hi in range(4):
                h = hg * 4 + hi
                self.emit_diag(self.diagC, self.WcolS.ap[:, h, :], n=NCORES,
                               reads=[self.WcolS])
                bsk = self.banks[0]
                bsv = self.banks[1]
                for c in range(NCORES):
                    self.T_(lambda t, hi=hi, c=c, bsk=bsk: t.matmul(
                        bsk.ap[:, 0:128], self.diagC.ap[:, c, :],
                        self.aggS.ap[:, c, hi, 0:128], start=(c == 0),
                        stop=(c == NCORES - 1)),
                        reads=[self.diagC, self.aggS],
                        writes=[bsk])
                    self.T_(lambda t, hi=hi, c=c, bsv=bsv: t.matmul(
                        bsv.ap[:, 0:128], self.diagC.ap[:, c, :],
                        self.aggS.ap[:, c, hi, 128:256], start=(c == 0),
                        stop=(c == NCORES - 1)),
                        reads=[self.diagC, self.aggS],
                        writes=[bsv])
                self.V_(lambda v, h=h, bsk=bsk: v.tensor_copy(
                    self.SkTs.ap[:, h, :], bsk.ap[:, 0:128]),
                    reads=[bsk], writes=[self.SkTs])
                self.V_(lambda v, h=h, bsv=bsv: v.tensor_copy(
                    self.Svs.ap[:, h, :], bsv.ap[:, 0:128]),
                    reads=[bsv], writes=[self.Svs])
        self.tap("SkT_in", self.SkTs, [128, H, 128], BF16)
        self.tap("Sv_in", self.Svs, [128, H, 128], BF16)

    def b_scan(self):
        T_, V_, S_ = self.T_, self.V_, self.S_
        for ch in range(NCH):
            self.emit_sp(self.spT, ch)
            self.emit_cumexp(self.U2n, self.spT,
                             [(self.expBp, -1.0), (self.escr, 1.0)])
            self.V_(lambda v, ch=ch: v.tensor_mul(
                self.bwR.ap[:, :], self.wc.ap[:, ch, :], self.escr.ap[:, :]),
                reads=[self.wc, self.escr], writes=[self.bwR])
            self.emit_cumexp(self.U3n, self.spT, [(self.escr, -1.0)])
            self.V_(lambda v, ch=ch: v.tensor_mul(
                self.bwd.ap[:, :], self.wc.ap[:, ch, :], self.escr.ap[:, :]),
                reads=[self.wc, self.escr], writes=[self.bwd])
            self.emit_cols2(self.spT)
            for j in range(4):
                pb = self.banks[j]
                self.T_(lambda t, j=j, ch=ch, pb=pb: t.matmul(
                    pb.ap[:, :], self.G1c.ap[:, ch * 128:(ch + 1) * 128],
                    self.Wg2b.ap[:, j * 512:(j + 1) * 512], start=True,
                    stop=True), reads=[self.G1c, self.Wg2b], writes=[pb])
                self.V_(lambda v, j=j, pb=pb: v.tensor_tensor(
                    self.sgrp.ap[:, :], pb.ap[:, :],
                    self.bg2T.ap[:, j * 512:(j + 1) * 512], ALU.add),
                    reads=[pb, self.bg2T], writes=[self.sgrp])
                self.S_(lambda s: s.activation(self.sgrp.ap[:, :],
                                               self.sgrp.ap[:, :], ACT.Exp,
                                               scale=-1.0), writes=[self.sgrp])
                self.V_(lambda v: v.tensor_scalar(self.sgrp.ap[:, :],
                                                  self.sgrp.ap[:, :], 1.0, None,
                                                  ALU.add), writes=[self.sgrp])
                self.V_(lambda v: v.reciprocal(self.sgrp.ap[:, :],
                                               self.sgrp.ap[:, :]),
                        writes=[self.sgrp])
                self.V_(lambda v, j=j: v.tensor_copy(
                    self.gsig.ap[:, j * 512:(j + 1) * 512], self.sgrp.ap[:, :]),
                    reads=[self.sgrp], writes=[self.gsig])
            self.emit_diag(self.diagB, self.ecol.ap[:, 0:H], reads=[self.ecol])
            for g4 in range(4):
                hlo = g4 * 4
                bPT = self.banks[4]
                bS = self.banks[5]
                bTR = self.banks[6]
                bO = self.banks[7]
                self.emit_diag(self.diagR, self.ecol.ap[:, H + hlo:H + hlo + 4],
                               n=4, reads=[self.ecol])
                for hi in range(4):
                    h = hlo + hi
                    self.T_(lambda t, h=h, hi=hi, bPT=bPT: t.matmul(
                        bPT.ap[:, hi * 128:(hi + 1) * 128],
                        self.diagR.ap[:, hi, :], self.SkTs.ap[:, h, :],
                        start=True, stop=True),
                        reads=[self.diagR, self.SkTs],
                        writes=[bPT])
                self.V_(lambda v, bPT=bPT: v.tensor_copy(
                    self.gSR.ap[:, :, :],
                    bPT.ap[:, :].rearrange("p (a b) -> p a b", a=4)),
                    reads=[bPT], writes=[self.gSR])
                for hi in range(4):
                    h = hlo + hi
                    self.T_(lambda t, h=h, hi=hi, ch=ch, bPT=bPT: t.matmul(
                        bPT.ap[:, hi * 128:(hi + 1) * 128],
                        self.kc.ap[:, h, ch * 128:(ch + 1) * 128],
                        self.qc.ap[:, h, ch * 128:(ch + 1) * 128],
                        start=True, stop=True),
                        reads=[self.kc, self.qc],
                        writes=[bPT])
                self.V_(lambda v, bPT=bPT: v.tensor_tensor(
                    self.gPM.ap[:, :, :],
                    bPT.ap[:, :].rearrange("p (a b) -> p a b", a=4),
                    bc_mid(self.trilS.ap[:, :], 4), ALU.mult),
                    reads=[bPT, self.trilS], writes=[self.gPM])
                for hi in range(4):
                    h = hlo + hi
                    self.T_(lambda t, h=h, hi=hi, bTR=bTR: t.transpose(
                        bTR.bf[:, 512 + hi * 128:512 + (hi + 1) * 128],
                        self.bwR.ap[:, h * 128:(h + 1) * 128],
                        self.identb.ap[:, :]),
                        reads=[self.bwR, self.identb],
                        writes=[bTR])
                self.V_(lambda v, bTR=bTR: v.tensor_copy(
                    self.gRT.ap[:, :, :],
                    bTR.bf[:, 512:1024].rearrange("p (a b) -> p a b", a=4)),
                    reads=[bTR], writes=[self.gRT])
                for hi in range(4):
                    h = hlo + hi
                    self.T_(lambda t, h=h, hi=hi, bS=bS: t.matmul(
                        bS.ap[:, hi * 128:(hi + 1) * 128], self.gPM.ap[:, hi, :],
                        self.bwR.ap[:, h * 128:(h + 1) * 128],
                        start=True, stop=False),
                        reads=[self.gPM, self.bwR],
                        writes=[bS])
                    self.T_(lambda t, h=h, hi=hi, ch=ch, bS=bS: t.matmul(
                        bS.ap[:, hi * 128:(hi + 1) * 128],
                        self.qc.ap[:, h, ch * 128:(ch + 1) * 128],
                        self.gSR.ap[:, hi, :], start=False, stop=True),
                        reads=[self.qc, self.gSR], writes=[bS])
                self.V_(lambda v, bS=bS, hlo=hlo: v.tensor_tensor(
                    self.sgrp.ap[:, :], bS.ap[:, :],
                    self.expBp.ap[:, hlo * 128:(hlo + 4) * 128], ALU.mult),
                    reads=[bS, self.expBp], writes=[self.sgrp])
                self.V_(lambda v: v.tensor_reduce(
                    self.smx.ap[:, 0:4], self.sgrp.ap[:, :].rearrange(
                        "p (a b) -> p a b", a=4), AX.X, ALU.max, negate=True),
                    reads=[self.sgrp], writes=[self.smx])
                for hi in range(4):
                    self.S_(lambda s, hi=hi: s.activation(
                        self.sgrp.ap[:, hi * 128:(hi + 1) * 128],
                        self.sgrp.ap[:, hi * 128:(hi + 1) * 128], ACT.Exp,
                        bias=self.smx.ap[:, hi:hi + 1]),
                        reads=[self.smx], writes=[self.sgrp])
                self.V_(lambda v: v.tensor_reduce(
                    self.smx.ap[:, 4:8], self.sgrp.ap[:, :].rearrange(
                        "p (a b) -> p a b", a=4), AX.X, ALU.add),
                    reads=[self.sgrp], writes=[self.smx])
                self.V_(lambda v: v.reciprocal(self.smx.ap[:, 4:8],
                                               self.smx.ap[:, 4:8]),
                        writes=[self.smx])
                self.V_(lambda v: v.tensor_tensor(
                    self.sgrp.ap[:, :].rearrange("p (a b) -> p a b", a=4),
                    self.sgrp.ap[:, :].rearrange("p (a b) -> p a b", a=4),
                    bc_in(self.smx.ap[:, 4:8], 128), ALU.mult),
                    reads=[self.smx], writes=[self.sgrp])
                self.V_(lambda v, hlo=hlo: v.tensor_tensor(
                    self.gAB.ap[:, :, :].rearrange("p a b -> p (a b)"),
                    self.sgrp.ap[:, :],
                    self.expBp.ap[:, hlo * 128:(hlo + 4) * 128], ALU.mult),
                    reads=[self.sgrp, self.expBp], writes=[self.gAB])
                for hi in range(4):
                    self.T_(lambda t, hi=hi, bTR=bTR: t.transpose(
                        bTR.bf[:, hi * 128:(hi + 1) * 128], self.gAB.ap[:, hi, :],
                        self.identb.ap[:, :]),
                        reads=[self.gAB, self.identb],
                        writes=[bTR])
                self.V_(lambda v, bTR=bTR: v.tensor_copy(
                    self.gAT.ap[:, :, :],
                    bTR.bf[:, 0:512].rearrange("p (a b) -> p a b", a=4)),
                    reads=[bTR], writes=[self.gAT])
                for hi in range(4):
                    self.T_(lambda t, hi=hi, bPT=bPT: t.matmul(
                        bPT.ap[:, hi * 128:(hi + 1) * 128], self.gRT.ap[:, hi, :],
                        self.gAT.ap[:, hi, :], start=True, stop=True),
                        reads=[self.gRT, self.gAT],
                        writes=[bPT])
                self.V_(lambda v, bPT=bPT: v.tensor_tensor(
                    self.gP2.ap[:, :, :],
                    bPT.ap[:, :].rearrange("p (a b) -> p a b", a=4),
                    bc_mid(self.tril1.ap[:, :], 4), ALU.mult),
                    reads=[bPT, self.tril1], writes=[self.gP2])
                self.V_(lambda v, hlo=hlo: v.tensor_tensor(
                    self.svr.ap[:, :, :], self.Svs.ap[:, hlo:hlo + 4, :],
                    bc_in(self.ecol.ap[:, H + hlo:H + hlo + 4], 128), ALU.mult),
                    reads=[self.Svs, self.ecol], writes=[self.svr])
                for hi in range(4):
                    h = hlo + hi
                    self.T_(lambda t, h=h, hi=hi, ch=ch, bO=bO: t.matmul(
                        bO.ap[:, hi * 128:(hi + 1) * 128], self.gP2.ap[:, hi, :],
                        self.vc.ap[:, ch, h * 128:(h + 1) * 128],
                        start=True, stop=False),
                        reads=[self.gP2, self.vc],
                        writes=[bO])
                    self.T_(lambda t, hi=hi, bO=bO: t.matmul(
                        bO.ap[:, hi * 128:(hi + 1) * 128], self.gAT.ap[:, hi, :],
                        self.svr.ap[:, hi, :], start=False, stop=True),
                        reads=[self.gAT, self.svr], writes=[bO])
                self.S_(lambda s, bO=bO: s.activation(
                    self.ogrp.ap[:, :], bO.ap[:, :], ACT.Square),
                    reads=[bO], writes=[self.ogrp])
                self.V_(lambda v: v.tensor_reduce(
                    self.rst.ap[:, 0:4], self.ogrp.ap[:, :].rearrange(
                        "p (a b) -> p a b", a=4), AX.X, ALU.add),
                    reads=[self.ogrp], writes=[self.rst])
                self.S_(lambda s: s.activation(self.rst.ap[:, 0:4],
                                               self.rst.ap[:, 0:4], ACT.Ln,
                                               scale=1.0 / 128.0,
                                               bias=self.cpool.ap[:, 2:3]),
                        reads=[self.cpool], writes=[self.rst])
                self.S_(lambda s: s.activation(self.rst.ap[:, 0:4],
                                               self.rst.ap[:, 0:4], ACT.Exp,
                                               scale=-0.5), writes=[self.rst])
                self.V_(lambda v, bO=bO: v.tensor_tensor(
                    self.ogrp.ap[:, :].rearrange("p (a b) -> p a b", a=4),
                    bO.ap[:, :].rearrange("p (a b) -> p a b", a=4),
                    bc_in(self.rst.ap[:, 0:4], 128), ALU.mult),
                    reads=[bO, self.rst], writes=[self.ogrp])
                self.V_(lambda v, hlo=hlo: v.tensor_tensor(
                    self.gOF.ap[:, :, :].rearrange("p a b -> p (a b)"),
                    self.ogrp.ap[:, :],
                    self.gsig.ap[:, hlo * 128:(hlo + 4) * 128], ALU.mult),
                    reads=[self.ogrp, self.gsig], writes=[self.gOF])
                for hi in range(4):
                    self.T_(lambda t, hi=hi, bTR=bTR: t.transpose(
                        bTR.bf[:, 512 + hi * 128:512 + (hi + 1) * 128],
                        self.gOF.ap[:, hi, :], self.identb.ap[:, :]),
                        reads=[self.gOF, self.identb],
                        writes=[bTR])
                self.V_(lambda v, bTR=bTR: v.tensor_copy(
                    self.oTst.ap[:, :, :],
                    bTR.bf[:, 512:1024].rearrange("p (a b) -> p a b", a=4)),
                    reads=[bTR], writes=[self.oTst])
                self.dma_store(
                    self.o_scr.rearrange("(s p) t -> p s t", p=128)[
                        :, hlo:hlo + 4, ch * 128:(ch + 1) * 128],
                    self.oTst.ap[:, :, :], [self.oTst])
            self.emit_ktm(self.ktm, ch, bank0=4)
            self.emit_states(self.ktm, self.bwd, self.diagB, ch, bank0=4)

    def b3_wo_ln1(self):
        T_, V_, S_ = self.T_, self.V_, self.S_
        I = self.I
        st_toks = [(f"st{i}", self.P.cnt[f"st{i}"])
                   for i in range(self.P.LANES["st"]) if self.P.cnt[f"st{i}"]]
        self.dma("sync", self.oT.ap[:, :, :],
                 self.o_scr.rearrange("(s p) t -> p s t", p=128),
                 writes=[self.oT], extra=st_toks)
        for band in range(2):
            for i in range(8):
                sl = band * 8 + i
                wb = self.wo_b[i]
                self.dma("gpsimd", wb.ap[:, :],
                         I["Wo"][sl * 128:(sl + 1) * 128, :], writes=[wb])
                self.V_(lambda v, wb=wb: v.tensor_scalar(
                    wb.ap[:, :], wb.ap[:, :], self.nwcol.ap[:, 0:1], None,
                    ALU.mult), reads=[self.nwcol], writes=[wb])
            for tc in range(NCH):
                par = tc % 2
                if band == 0:
                    xb = self.xs[0]
                    self.dma("sync", xb.ap[:, :],
                             I["h_nat"][tc * 128:(tc + 1) * 128, :], writes=[xb])
                for j in range(4):
                    pb = self.banks[par * 4 + j]
                    for i in range(8):
                        sl = band * 8 + i
                        wb = self.wo_b[i]
                        self.T_(lambda t, i=i, j=j, tc=tc, sl=sl, pb=pb, wb=wb:
                                t.matmul(pb.ap[:, :],
                                         self.oT.ap[:, sl, tc * 128:(tc + 1) * 128],
                                         wb.ap[:, j * 512:(j + 1) * 512],
                                         start=(i == 0), stop=(i == 7)),
                                reads=[self.oT, wb],
                                writes=[pb])
                for j in range(4):
                    pb = self.banks[par * 4 + j]
                    if band == 0:
                        xb = self.xs[0]
                        self.V_(lambda v, j=j, tc=tc, pb=pb, xb=xb: v.tensor_tensor(
                            self.r1a.ap[:, tc, j * 512:(j + 1) * 512], pb.ap[:, :],
                            xb.ap[:, j * 512:(j + 1) * 512], ALU.add),
                            reads=[pb, xb], writes=[self.r1a])
                    else:
                        self.V_(lambda v, j=j, tc=tc, pb=pb: v.tensor_tensor(
                            self.r1a.ap[:, tc, j * 512:(j + 1) * 512],
                            self.r1a.ap[:, tc, j * 512:(j + 1) * 512],
                            pb.ap[:, :], ALU.add),
                            reads=[pb], writes=[self.r1a])
        for tc in range(NCH):
            self.emit_ln(tc, to_xT=True)
        self.tap("x1", self.r1a, [128, NCH, HID], F32)

    def emit_ln(self, tc, to_xT):
        V_, S_, T_ = self.V_, self.S_, self.T_
        acc = self.r1a
        sq = self.sqscr if to_xT else self.sqscr2
        self.V_(lambda v, tc=tc: v.tensor_reduce(
            self.lnc.ap[:, 0:1], acc.ap[:, tc, :], AX.X, ALU.add),
            reads=[acc], writes=[self.lnc])
        self.V_(lambda v: v.tensor_scalar(self.lnc.ap[:, 0:1], self.lnc.ap[:, 0:1],
                                          1.0 / HID, None, ALU.mult),
                writes=[self.lnc])
        self.S_(lambda s, tc=tc, sq=sq: s.activation(
            sq.ap[:, :], acc.ap[:, tc, :], ACT.Square,
            accum_out=self.lnc.ap[:, 1:2]),
            reads=[acc], writes=[sq, self.lnc])
        self.V_(lambda v: v.tensor_scalar(self.lnc.ap[:, 1:2], self.lnc.ap[:, 1:2],
                                          1.0 / HID, None, ALU.mult),
                writes=[self.lnc])
        self.V_(lambda v: v.tensor_mul(self.lnc.ap[:, 2:3], self.lnc.ap[:, 0:1],
                                       self.lnc.ap[:, 0:1]), writes=[self.lnc])
        self.V_(lambda v: v.tensor_sub(self.lnc.ap[:, 1:2], self.lnc.ap[:, 1:2],
                                       self.lnc.ap[:, 2:3]), writes=[self.lnc])
        self.S_(lambda s: s.activation(self.lnc.ap[:, 3:4], self.lnc.ap[:, 1:2],
                                       ACT.Ln, bias=self.cpool.ap[:, 2:3]),
                reads=[self.cpool], writes=[self.lnc])
        self.S_(lambda s: s.activation(self.lnc.ap[:, 3:4], self.lnc.ap[:, 3:4],
                                       ACT.Exp, scale=-0.5), writes=[self.lnc])
        self.V_(lambda v, tc=tc: v.tensor_scalar(
            acc.ap[:, tc, :], acc.ap[:, tc, :], self.lnc.ap[:, 0:1], None,
            ALU.subtract), reads=[self.lnc], writes=[acc])
        self.V_(lambda v, tc=tc: v.tensor_scalar(
            acc.ap[:, tc, :], acc.ap[:, tc, :], self.lnc.ap[:, 3:4], None,
            ALU.mult), reads=[self.lnc], writes=[acc])
        if not to_xT:
            return
        self.V_(lambda v, tc=tc, sq=sq: v.tensor_copy(sq.ap[:, :],
                                                      acc.ap[:, tc, :]),
                reads=[acc], writes=[sq])
        for g8 in range(2):
            pbT = self.banks[4 + g8]
            for hi in range(8):
                sl = g8 * 8 + hi
                self.T_(lambda t, sl=sl, hi=hi, pbT=pbT, sq=sq: t.transpose(
                    pbT.bf[:, hi * 128:(hi + 1) * 128],
                    sq.ap[:, sl * 128:(sl + 1) * 128], self.identb.ap[:, :]),
                    reads=[sq, self.identb], writes=[pbT])
            self.V_(lambda v, g8=g8, tc=tc, pbT=pbT: v.tensor_copy(
                self.xT.ap[:, g8 * 8:(g8 + 1) * 8, tc * 128:(tc + 1) * 128],
                pbT.bf[:, 0:1024].rearrange("p (a b) -> p a b", a=8)),
                reads=[pbT], writes=[self.xT])

    def b4_ffn(self):
        T_, V_, S_ = self.T_, self.V_, self.S_
        I = self.I
        for band in range(4):
            for halfband in range(2):
                for i in range(8):
                    hidx = band * 16 + halfband * 8 + i
                    wd = self.wd_all[i]
                    self.dma("gpsimd", wd.ap[:, :],
                             I["wdT"][hidx * 128:(hidx + 1) * 128, :],
                             writes=[wd])
                for i in range(8):
                    hs = halfband * 8 + i
                    hidx = band * 16 + hs
                    par = i % 2
                    fg = self.ffg[par]
                    fu = self.ffu[par]
                    self.dma("gpsimd", fg.ap[:, :, :],
                             I["wgT"][:, hidx * 128:(hidx + 1) * 128].rearrange(
                                 "(a p) c -> p a c", p=128), writes=[fg])
                    self.dma("gpsimd", fu.ap[:, :, :],
                             I["wuT"][:, hidx * 128:(hidx + 1) * 128].rearrange(
                                 "(a p) c -> p a c", p=128), writes=[fu])
                    bg = self.banks[4 + par]
                    bu = self.banks[6 + par]
                    for half in range(2):
                        for (bk, fw) in ((bg, fg), (bu, fu)):
                            for a in range(NSL):
                                self.T_(lambda t, a=a, half=half, bk=bk, fw=fw:
                                        t.matmul(bk.ap[:, :], fw.ap[:, a, :],
                                                 self.xT.ap[:, a, half * 512:
                                                            (half + 1) * 512],
                                                 start=(a == 0),
                                                 stop=(a == NSL - 1)),
                                        reads=[fw, self.xT],
                                        writes=[bk])
                        self.S_(lambda s, bg=bg, half=half: s.activation(
                            self.sgf.ap[:, half * 512:(half + 1) * 512],
                            bg.ap[:, :], ACT.Silu),
                            reads=[bg], writes=[self.sgf])
                        self.V_(lambda v, bu=bu, i=i, half=half: v.tensor_tensor(
                            self.gb.ap[:, i, half * 512:(half + 1) * 512],
                            bu.ap[:, :],
                            self.sgf.ap[:, half * 512:(half + 1) * 512],
                            ALU.mult),
                            reads=[bu, self.sgf], writes=[self.gb])
                unit = 0
                for tc in range(NCH):
                    for jh in range(2):
                        pair = unit % 2
                        unit += 1
                        for jj in range(2):
                            j = jh * 2 + jj
                            pb = self.banks[pair * 2 + jj]
                            for i in range(8):
                                wd = self.wd_all[i]
                                self.T_(lambda t, i=i, j=j, tc=tc, pb=pb,
                                        wd=wd: t.matmul(
                                    pb.ap[:, :],
                                    self.gb.ap[:, i, tc * 128:(tc + 1) * 128],
                                    wd.ap[:, j * 512:(j + 1) * 512],
                                    start=(i == 0), stop=(i == 7)),
                                    reads=[self.gb, wd],
                                    writes=[pb])
                        for jj in range(2):
                            j = jh * 2 + jj
                            pb = self.banks[pair * 2 + jj]
                            self.V_(lambda v, j=j, tc=tc, pb=pb: v.tensor_tensor(
                                self.r1a.ap[:, tc, j * 512:(j + 1) * 512],
                                self.r1a.ap[:, tc, j * 512:(j + 1) * 512],
                                pb.ap[:, :], ALU.add),
                                reads=[pb], writes=[self.r1a])
        for tc in range(NCH):
            self.emit_ln(tc, to_xT=False)
            self.dma_store(self.out[tc * 128:(tc + 1) * 128, :],
                           self.r1a.ap[:, tc, :], [self.r1a])

    def finish(self):
        toks = [(f"st{i}", self.P.cnt[f"st{i}"])
                for i in range(self.P.LANES["st"]) if self.P.cnt[f"st{i}"]]
        if toks:
            self.P.emit("gpsimd", lambda g: g.engine_nop(), toks, "gs", 1)


def build_host_inputs(inputs):
    h = np.ascontiguousarray(np.asarray(inputs["L0_attn_hidden_states"],
                                        np.float32)[0])
    hT_full = np.ascontiguousarray(h.T)
    per_core = []
    for c in range(NCORES):
        lo = c * Tc
        hT_ext = np.zeros((HID, Tc + 3), np.float32)
        if c == 0:
            hT_ext[:, 3:] = hT_full[:, 0:Tc]
        else:
            hT_ext[:, :] = hT_full[:, lo - 3:lo + Tc]
        Mm = np.zeros((9, 8), np.float32)
        for cp in range(8):
            if cp < c:
                for cm in range(cp + 1, c):
                    Mm[cm, cp] = 1.0
            else:
                Mm[8, cp] = 1e5
        per_core.append({
            "hT_ext": hT_ext,
            "h_nat": np.ascontiguousarray(h[lo:lo + Tc]),
            "Mmat": Mm,
        })
    f32 = lambda k: np.ascontiguousarray(np.asarray(inputs[k], np.float32))
    shared = {
        "Wq": f32("L0_attn_Wq"), "Wk": f32("L0_attn_Wk"),
        "Wv": f32("L0_attn_Wv"), "Ww": f32("L0_attn_Ww"),
        "Wo": f32("L0_attn_Wo"),
        "cq": f32("L0_attn_cq"), "ck": f32("L0_attn_ck"),
        "cv": f32("L0_attn_cv"),
        "Wf1": f32("L0_attn_Wf1"), "Wf2": f32("L0_attn_Wf2"),
        "Wb": f32("L0_attn_Wb"),
        "Wg1": f32("L0_attn_Wg1"), "Wg2": f32("L0_attn_Wg2"),
        "bg2": f32("L0_attn_bg2"), "normw": f32("L0_attn_norm_w"),
        "wgT": np.ascontiguousarray(np.asarray(inputs["L0_ffn_w_gate"],
                                               np.float32).T),
        "wuT": np.ascontiguousarray(np.asarray(inputs["L0_ffn_w_up"],
                                               np.float32).T),
        "wdT": np.ascontiguousarray(np.asarray(inputs["L0_ffn_w_down"],
                                               np.float32).T),
    }
    return [{**shared, **pc} for pc in per_core]


_BUILD_CACHE = {}


def kernel(**inputs):
    in_maps = build_host_inputs(inputs)
    key = (STAGES, WANT_DBG)
    if key not in _BUILD_CACHE:
        b = Builder()
        nc = b.build()
        _BUILD_CACHE[key] = (nc, b)
    nc, b = _BUILD_CACHE[key]
    res = run_bass_kernel_spmd(nc, in_maps, core_ids=list(range(NCORES)))
    out = np.concatenate([r["out"] for r in res.results], axis=0)
    return out[None]


# version 06
# speedup vs baseline: 1.0122x; 1.0122x over previous
import numpy as np
import ml_dtypes
import concourse.bass as bass
import concourse.mybir as mybir
from concourse.bass_utils import run_bass_kernel_spmd

F32 = mybir.dt.float32
BF16 = mybir.dt.bfloat16
U8 = mybir.dt.uint8
AX = mybir.AxisListType
ALU = mybir.AluOpType
ACT = mybir.ActivationFunctionType

NCORES = 8
T = 8192
HID = 2048
Tc = T // NCORES
C = 128
NCH = Tc // C
H = 16
NSL = HID // 128
DH = 8192
SCALE = float(128) ** -0.5

ENGINES = ("sync", "tensor", "vector", "scalar", "gpsimd")
STAGES = "all"
WANT_DBG = ()


class Buf:
    __slots__ = ("ap", "w", "r", "rng", "fresh")

    def __init__(self, ap, rng=None):
        self.ap = ap
        self.w = None
        self.r = {}
        self.rng = rng
        self.fresh = True

    def add_r(self, tok):
        s, v = tok
        if self.r.get(s, -1) < v:
            self.r[s] = v

    def r_toks(self):
        return [(s, v) for s, v in self.r.items()]

    def __getitem__(self, key):
        return self.ap[key]


class Bank(Buf):
    __slots__ = ("bf",)

    def __init__(self, ap):
        super().__init__(ap)
        self.bf = ap.bitcast(BF16)


class Plan:
    LANES = {"ld": 4, "lw": 6, "st": 4}

    def __init__(self, nc):
        self.nc = nc
        self.plan = []
        self.sems = {}
        self.cnt = {}
        for s in ("ts", "vs", "ss", "gs", "cs"):
            self.sems[s] = nc.alloc_semaphore(s)
            self.cnt[s] = 0
        self.rr = {}
        for q, n in self.LANES.items():
            self.rr[q] = 0
            for i in range(n):
                name = f"{q}{i}"
                self.sems[name] = nc.alloc_semaphore(name)
                self.cnt[name] = 0

    def lane(self, q):
        i = self.rr[q] % self.LANES[q]
        self.rr[q] += 1
        name = f"{q}{i}"
        pre = (name, self.cnt[name]) if self.cnt[name] else None
        return name, pre

    def emit(self, eng, fn, waits, inc, inc_by):
        tok = None
        if inc is not None:
            self.cnt[inc] += inc_by
            tok = (inc, self.cnt[inc])
        self.plan.append((eng, fn, tuple(w for w in waits if w is not None),
                          (inc, inc_by) if inc else None))
        return tok

    def realize(self, block):
        wm = {e: {} for e in ENGINES}

        def mk(eng_name):
            def body(eng):
                for (e, fn, waits, inc) in self.plan:
                    if e != eng_name:
                        continue
                    for (s, v) in waits:
                        if wm[e].get(s, 0) < v:
                            eng.wait_ge(self.sems[s], v)
                            wm[e][s] = v
                    ins = fn(eng)
                    if inc is not None:
                        s, by = inc
                        ins.then_inc(self.sems[s], by)
            return body

        block.sync(mk("sync"))
        block.tensor(mk("tensor"))
        block.vector(mk("vector"))
        block.scalar(mk("scalar"))
        block.gpsimd(mk("gpsimd"))


def bc_in(ap, n):
    return bass.AP(ap.tensor, ap.offset, [list(p) for p in ap.ap] + [[0, n]])


def bc_mid(ap, n):
    pr = [list(p) for p in ap.ap]
    return bass.AP(ap.tensor, ap.offset, [pr[0], [0, n]] + pr[1:])


class Builder:
    SEM_OF = {"sync": "ld", "gpsimd_dma": "lw", "tensor": "ts", "vector": "vs",
              "scalar": "ss", "gpsimd": "gs"}

    def __init__(self):
        nc = bass.Bass("TRN2", target_bir_lowering=False, debug=False)
        self.nc = nc
        self.P = Plan(nc)
        self.dbg = {}

        I = {}
        def inp(name, shape, dtype=F32):
            I[name] = nc.dram_tensor(name, shape, dtype, kind="ExternalInput").ap()
        self.I = I
        inp("hT_ext", [HID, Tc + 3]); inp("h_nat", [Tc, HID])
        for w in ("Wq", "Wk", "Wv", "Ww", "Wo"):
            inp(w, [HID, HID])
        for w in ("cq", "ck", "cv"):
            inp(w, [HID, 4])
        inp("Wf1", [HID, 128]); inp("Wf2", [128, HID]); inp("Wb", [HID, H])
        inp("Wg1", [HID, 128]); inp("Wg2", [128, HID])
        inp("bg2", [HID]); inp("normw", [128])
        inp("wgT", [HID, DH]); inp("wuT", [HID, DH]); inp("wdT", [DH, HID])
        inp("Mmat", [9, 8])
        self.out = nc.dram_tensor("out", [Tc, HID], F32, kind="ExternalOutput").ap()

        self.pub1 = nc.dram_tensor("pub1", [128, H * 256], BF16).ap()
        self.pub2 = nc.dram_tensor("pub2", [128, H], F32).ap()
        self.agg1 = nc.dram_tensor("agg1", [NCORES * 128, H * 256], BF16, addr_space="Shared").ap()
        self.agg2 = nc.dram_tensor("agg2", [NCORES * 128, H], F32, addr_space="Shared").ap()
        self.o_scr = nc.dram_tensor("o_scr", [HID, Tc], BF16).ap()

        def sb(name, shape, dtype):
            return Buf(nc.alloc_sbuf_tensor(name, shape, dtype).ap())
        self.trilS = sb("trilS", [128, 128], F32)
        self.tril1 = sb("tril1", [128, 128], BF16)
        self.identb = sb("identb", [128, 128], BF16)
        self.identf = sb("identf", [16, 16], F32)
        self.U2n = sb("U2n", [128, 128], F32)
        self.U3n = sb("U3n", [128, 128], F32)
        self.onescol = sb("onescol", [128, 1], F32)
        self.halfcol = sb("halfcol", [128, 1], F32)
        self.colpair = sb("colpair", [128, 2], F32)
        self.ones1p = sb("ones1p", [1, 128], F32)
        self.Wf2b = sb("Wf2b", [128, HID], BF16)
        self.Wg2b = sb("Wg2b", [128, HID], BF16)
        self.bg2T = sb("bg2T", [128, HID], BF16)
        self.nwcol = sb("nwcol", [128, 1], F32)
        self.MmatS = sb("MmatS", [9, 8], F32)
        self.f1c = sb("f1c", [128, Tc], BF16)
        self.G1c = sb("G1c", [128, Tc], BF16)
        self.SkTs = sb("SkTs", [128, H, 128], BF16)
        self.Svs = sb("Svs", [128, H, 128], BF16)
        self.Btn = sb("Btn", [128, H], F32)
        self.ecol = sb("ecol", [128, 2 * H], F32)
        self.beta_tm = sb("beta_tm", [128, NCH, H], F32)
        self.smx = sb("smx", [128, 16], F32)
        self.rst = sb("rst", [128, 16], F32)
        self.lnc = sb("lnc", [128, 8], F32)
        self.cpool = sb("cpool", [128, 4], F32)

        self.alias_reg = []
        self.AR = 184320
        self.arena = nc.alloc_sbuf_tensor("arena", [128, self.AR], U8)
        self.sbuf_slack = nc.sbuf_bytes_remaining
        assert self.sbuf_slack >= 0, f"sbuf over by {-self.sbuf_slack}"

        a = self.ab
        self.kc = a(0, [NSL, Tc], BF16)
        self.vc = a(32768, [NCH, HID], BF16)
        self.wc = a(65536, [NCH, HID], BF16)
        self.qc = a(98304, [NSL, Tc], BF16)
        self.hTb = a(131072, [NSL, Tc + 3], BF16)
        self.wsl = [a(164096, [NSL, 128], BF16), a(168192, [NSL, 128], BF16)]
        self.cvo = [a(172288, [264], F32), a(173376, [264], F32)]
        self.svo = [a(174464, [264], BF16), a(175040, [264], BF16)]
        self.betaR = a(175616, [1024], F32)
        self.bgrow = a(0, [HID], F32)
        self.Uh = a(8192, [128], F32)
        self.cwt = a(179712, [NSL, 16], F32)
        self.Wbb = a(180736, [NSL, H], BF16)
        self.sq2 = a(131072, [HID], BF16)
        self.spA = a(135168, [HID], F32)
        self.escrA = a(143360, [HID], BF16)
        self.bwdA = a(147456, [HID], BF16)
        self.ktmA = a(151552, [H, 128], BF16)
        self.diagA = a(155648, [H, 128], BF16)
        self.aggS = a(131072, [NCORES, 4, 256], BF16)
        self.augB = a(139264, [HID], F32)
        self.WcolS = a(147456, [H, 8], F32)
        self.diagC = a(155648, [NCORES, 128], BF16)
        self.spT = a(131072, [HID], F32)
        self.expBp = a(139264, [HID], BF16)
        self.escr = a(143360, [HID], BF16)
        self.bwR = a(147456, [HID], BF16)
        self.bwd = a(151552, [HID], BF16)
        self.gsig = a(155648, [HID], BF16)
        self.diagB = a(159744, [H, 128], BF16)
        self.ktm = a(163840, [H, 128], BF16)
        self.oTst = a(167936, [4, 128], BF16)
        self.gPM = a(168960, [4, 128], BF16)
        self.gRT = a(169984, [4, 128], BF16)
        self.gSR = a(171008, [4, 128], BF16)
        self.gAB = a(172032, [4, 128], BF16)
        self.gAT = a(173056, [4, 128], BF16)
        self.gP2 = a(174080, [4, 128], BF16)
        self.gOF = a(175104, [4, 128], BF16)
        self.sgrp = a(176128, [512], F32)
        self.ogrp = a(178176, [512], F32)
        self.svr = a(180224, [4, 128], BF16)
        self.diagR = a(181248, [4, 128], BF16)
        self.oT = a(0, [NSL, Tc], BF16)
        self.r1a = a(32768, [NCH, HID], F32)
        self.xT = a(98304, [NSL, Tc], BF16)
        self.wo_b = [a(131072 + i * 4096, [HID], BF16) for i in range(8)]
        self.sqscr = a(163840, [HID], BF16)
        self.xs = [a(167936, [HID], F32)]
        self.gb = [a(131072, [8, Tc], BF16), a(147456, [8, Tc], BF16)]
        self.ffg = [a(0, [NSL, 128], BF16), a(4096, [NSL, 128], BF16)]
        self.ffu = [a(8192, [NSL, 128], BF16), a(12288, [NSL, 128], BF16)]
        self.sgf = a(16384, [Tc], F32)
        self.wd_all = ([a(20480 + i * 4096, [HID], BF16) for i in range(3)] +
                       [a(163840 + i * 4096, [HID], BF16) for i in range(5)])
        self.sqscr2 = a(131072, [HID], BF16)

        self.banks = [Bank(nc.alloc_psum_tensor(f"pb{i}", [128, 512], F32).ap())
                      for i in range(8)]

    def ab(self, off, shape, dtype):
        esz = mybir.dt.size(dtype)
        n = int(np.prod(shape))
        assert off + n * esz <= self.AR, f"arena overflow at {off}+{n*esz}"
        ap = self.arena.ap()[:, off:off + n * esz].bitcast(dtype)
        if len(shape) > 1:
            dims = " ".join(f"d{i}" for i in range(len(shape)))
            kw = {f"d{i}": int(s) for i, s in enumerate(shape)}
            ap = ap.rearrange(f"p ({dims}) -> p {dims}", **kw)
        b = Buf(ap, rng=(off, off + n * esz))
        self.alias_reg.append(b)
        return b

    def _write_waits(self, b, waits):
        if b.w is not None:
            waits.append(b.w)
        waits.extend(b.r_toks())
        if b.fresh and b.rng is not None:
            lo, hi = b.rng
            for o in self.alias_reg:
                if o is b or o.rng is None:
                    continue
                if o.rng[0] < hi and lo < o.rng[1]:
                    if o.w is not None:
                        waits.append(o.w)
                    waits.extend(o.r_toks())
            b.fresh = False

    def op(self, eng, fn, reads=(), writes=(), extra=(), inc=None, inc_by=1):
        if inc is None:
            inc = self.SEM_OF[eng]
        waits = list(extra)
        for b in reads:
            if b.w is not None:
                waits.append(b.w)
        for b in writes:
            self._write_waits(b, waits)
        tok = self.P.emit(eng, fn, waits, inc, inc_by)
        for b in reads:
            b.add_r(tok)
        for b in writes:
            b.w = tok
            b.r = {}
        return tok

    def dma(self, eng, out_ap, in_ap, reads=(), writes=(), extra=(),
            slow_ok=False):
        q = "lw" if eng == "gpsimd" else "ld"
        sem, pre = self.P.lane(q)
        waits = list(extra)
        if pre is not None:
            waits.append(pre)
        for b in reads:
            if b.w is not None:
                waits.append(b.w)
        for b in writes:
            self._write_waits(b, waits)
        tok = self.P.emit(eng, lambda e: e.dma_start(
            out=out_ap, in_=in_ap, allow_slow_non_contiguous=slow_ok),
            waits, sem, 16)
        for b in reads:
            b.add_r(tok)
        for b in writes:
            b.w = tok
            b.r = {}
        return tok

    def dma_store(self, out_ap, in_ap, bufs, extra=(), eng="sync"):
        sem, pre = self.P.lane("st")
        waits = list(extra)
        if pre is not None:
            waits.append(pre)
        for b in bufs:
            if b.w is not None:
                waits.append(b.w)
        tok = self.P.emit(eng, lambda e: e.dma_start(out=out_ap, in_=in_ap),
                          waits, sem, 16)
        for b in bufs:
            b.add_r(tok)
        return tok

    def T_(self, fn, **kw):
        return self.op("tensor", fn, **kw)

    def V_(self, fn, **kw):
        return self.op("vector", fn, **kw)

    def S_(self, fn, **kw):
        return self.op("scalar", fn, **kw)

    def G_(self, fn, **kw):
        return self.op("gpsimd", fn, **kw)

    def tap(self, name, buf, shape, dtype):
        if name not in WANT_DBG:
            return
        d = self.nc.dram_tensor("dbg_" + name, shape, dtype,
                                kind="ExternalOutput").ap()
        self.dbg[name] = d
        self.dma_store(d[:, :] if len(shape) == 2 else d, buf.ap, [buf])

    def build(self):
        self.p0_constants()
        self.a1_projections()
        self.a2_aux()
        if STAGES != "A1":
            self.a3_l2norm()
            self.a4_states()
        if STAGES in ("A5", "B", "B3", "all"):
            self.a5_combine()
        if STAGES in ("B", "B3", "all"):
            self.b_scan()
        if STAGES in ("B3", "all"):
            self.b3_wo_ln1()
        if STAGES == "all":
            self.b4_ffn()
        self.finish()
        with self.nc.Block() as block:
            self.P.realize(block)
        return self.nc

    def p0_constants(self):
        G_, V_, S_, T_ = self.G_, self.V_, self.S_, self.T_
        I = self.I

        def mask(buf, fill_first, pat, cmp, fill, cmul, base=0):
            G_(lambda g: g.memset(buf.ap[:, :], fill_first), writes=[buf])
            G_(lambda g: g.affine_select(buf.ap[:, :], buf.ap[:, :], pattern=pat,
                                         compare_op=cmp, fill=fill,
                                         channel_multiplier=cmul, base=base),
               writes=[buf])

        mask(self.trilS, SCALE, [[1, 128]], ALU.is_ge, 0.0, -1)
        mask(self.tril1, 1.0, [[1, 128]], ALU.is_ge, 0.0, -1)
        for t_ in (self.identb,):
            mask(t_, 0.0, [[-1, 128]], ALU.not_equal, 1.0, 1)
        mask(self.identf, 0.0, [[-1, 16]], ALU.not_equal, 1.0, 1)
        mask(self.U2n, 1.0, [[1, 128]], ALU.is_ge, 0.0, -1)
        mask(self.Uh, 1.0, [[0, 128]], ALU.is_ge, 0.0, -1, base=63)
        self.V_(lambda v: v.tensor_sub(self.U2n.ap[:, :], self.U2n.ap[:, :],
                                       self.Uh.ap[:, :]),
                reads=[self.Uh], writes=[self.U2n])
        mask(self.U3n, 1.0, [[-1, 128]], ALU.is_ge, 0.0, 1, base=-1)
        G_(lambda g: g.memset(self.onescol.ap[:, :], 1.0), writes=[self.onescol])
        mask(self.halfcol, 1.0, [[0, 1]], ALU.is_ge, 0.0, -1, base=63)
        G_(lambda g: g.memset(self.ones1p.ap[:, :], 1.0), writes=[self.ones1p])
        self.V_(lambda v: v.tensor_copy(self.colpair.ap[:, 0:1],
                                        self.onescol.ap[:, :]),
                reads=[self.onescol], writes=[self.colpair])
        self.V_(lambda v: v.tensor_copy(self.colpair.ap[:, 1:2],
                                        self.halfcol.ap[:, :]),
                reads=[self.halfcol], writes=[self.colpair])
        for i, val in enumerate((1e-6, 1.0, 1e-5)):
            G_(lambda g, i=i, val=val: g.memset(self.cpool.ap[:, i:i + 1], val),
               writes=[self.cpool])

        for i, cwn in enumerate(("cq", "ck", "cv", "cv")):
            self.dma("sync", self.cwt.ap[:, :, 4 * i:4 * i + 4],
                     I[cwn].rearrange("(s p) t -> p s t", p=128),
                     writes=[self.cwt])
        self.dma("gpsimd", self.Wf2b.ap[:, :], I["Wf2"][:, :], writes=[self.Wf2b])
        self.dma("gpsimd", self.Wg2b.ap[:, :], I["Wg2"][:, :], writes=[self.Wg2b])
        self.dma("gpsimd", self.Wbb.ap[:, :, :],
                 I["Wb"].rearrange("(s p) h -> p s h", p=128), writes=[self.Wbb])
        self.dma("sync", self.nwcol.ap[:, :],
                 I["normw"].rearrange("(p o) -> p o", o=1), writes=[self.nwcol])
        self.dma("sync", self.MmatS.ap[:, :], I["Mmat"][:, :], writes=[self.MmatS])
        self.dma("sync", self.bgrow.ap[0:1, :],
                 I["bg2"].rearrange("(o f) -> o f", o=1), writes=[self.bgrow])
        for j in range(4):
            bk = self.banks[4 + j]
            self.T_(lambda t, j=j, bk=bk: t.matmul(
                bk.ap[:, :], self.ones1p.ap[:, :],
                self.bgrow.ap[0:1, j * 512:(j + 1) * 512], start=True, stop=True),
                reads=[self.ones1p, self.bgrow], writes=[bk])
            self.V_(lambda v, j=j, bk=bk: v.tensor_copy(
                self.bg2T.ap[:, j * 512:(j + 1) * 512], bk.ap[:, :]),
                reads=[bk], writes=[self.bg2T])
        self.dma("gpsimd", self.hTb.ap[:, :, :],
                 I["hT_ext"].rearrange("(s p) t -> p s t", p=128),
                 writes=[self.hTb])

    def a1_projections(self):
        T_, V_, S_ = self.T_, self.V_, self.S_
        I = self.I
        streams = [("q", "Wq", 0), ("k", "Wk", 1), ("v", "Wv", 2), ("w", "Ww", 3)]
        for sname, wn, sidx in streams:
            for fs in range(NSL):
                par = (sidx * NSL + fs) % 2
                wb = self.wsl[par]
                self.dma("gpsimd", wb.ap[:, :, :],
                         I[wn][:, fs * 128:(fs + 1) * 128].rearrange(
                             "(a p) c -> p a c", p=128), writes=[wb])
                for g in range(4):
                    t0 = g * 256
                    pb = self.banks[g]
                    for a in range(NSL):
                        self.T_(lambda t, a=a, pb=pb, t0=t0, wb=wb: t.matmul(
                            pb.ap[:, 0:259], wb.ap[:, a, :],
                            self.hTb.ap[:, a, t0:t0 + 259],
                            start=(a == 0), stop=(a == NSL - 1)),
                            reads=[wb, self.hTb], writes=[pb])
                    cp = g % 2
                    co = self.cvo[cp]
                    cb = sidx * 4
                    self.V_(lambda v, pb=pb, co=co, fs=fs, cb=cb: v.tensor_scalar(
                        co.ap[:, 0:256], pb.ap[:, 0:256],
                        self.cwt.ap[:, fs, cb:cb + 1], None, ALU.mult),
                        reads=[pb, self.cwt], writes=[co])
                    for i in range(1, 4):
                        self.V_(lambda v, pb=pb, co=co, fs=fs, cb=cb, i=i:
                                v.scalar_tensor_tensor(
                                    co.ap[:, 0:256], pb.ap[:, i:256 + i],
                                    self.cwt.ap[:, fs, cb + i:cb + i + 1],
                                    co.ap[:, 0:256], ALU.mult, ALU.add),
                                reads=[pb, self.cwt], writes=[co])
                    if sidx == 0:
                        self.S_(lambda s, co=co: s.activation(
                            co.ap[:, 0:256], co.ap[:, 0:256], ACT.Silu),
                            writes=[co])
                        self.V_(lambda v, co=co, fs=fs, t0=t0: v.tensor_scalar(
                            self.qc.ap[:, fs, t0:t0 + 256], co.ap[:, 0:256],
                            SCALE, None, ALU.mult), reads=[co],
                            writes=[self.qc])
                    elif sidx == 1:
                        self.S_(lambda s, co=co, fs=fs, t0=t0: s.activation(
                            self.kc.ap[:, fs, t0:t0 + 256], co.ap[:, 0:256],
                            ACT.Silu), reads=[co], writes=[self.kc])
                    else:
                        so = self.svo[cp]
                        self.S_(lambda s, co=co, so=so: s.activation(
                            so.ap[:, 0:256], co.ap[:, 0:256], ACT.Silu),
                            reads=[co], writes=[so])
                        dst = self.vc if sidx == 2 else self.wc
                        tb = self.banks[4 + cp]
                        for hf in range(2):
                            self.T_(lambda t, tb=tb, so=so, hf=hf: t.transpose(
                                tb.bf[:, hf * 128:(hf + 1) * 128],
                                so.ap[:, hf * 128:hf * 128 + 128],
                                self.identb.ap[:, :]),
                                reads=[so, self.identb],
                                writes=[tb])
                        self.V_(lambda v, tb=tb, dst=dst, g=g, fs=fs: v.tensor_copy(
                            dst.ap[:, 2 * g:2 * g + 2, fs * 128:(fs + 1) * 128],
                            tb.bf[:, 0:256].rearrange("p (a b) -> p a b", a=2)),
                            reads=[tb], writes=[dst])
        self.tap("qc", self.qc, [128, NSL, Tc], BF16)
        self.tap("kc", self.kc, [128, NSL, Tc], BF16)
        self.tap("vc", self.vc, [128, NCH, HID], BF16)
        self.tap("wcr", self.wc, [128, NCH, HID], BF16)

    def a2_aux(self):
        T_, V_, S_ = self.T_, self.V_, self.S_
        I = self.I
        for (dst, wn) in ((self.f1c, "Wf1"), (self.G1c, "Wg1")):
            wb = self.wsl[0]
            self.dma("gpsimd", wb.ap[:, :, :],
                     I[wn].rearrange("(a p) c -> p a c", p=128), writes=[wb])
            for half in range(2):
                pb = self.banks[4 + half]
                for a in range(NSL):
                    self.T_(lambda t, a=a, pb=pb, half=half, wb=wb: t.matmul(
                        pb.ap[:, :], wb.ap[:, a, :],
                        self.hTb.ap[:, a, 3 + half * 512:3 + (half + 1) * 512],
                        start=(a == 0), stop=(a == NSL - 1)),
                        reads=[wb, self.hTb], writes=[pb])
                self.V_(lambda v, pb=pb, dst=dst, half=half: v.tensor_copy(
                    dst.ap[:, half * 512:(half + 1) * 512], pb.ap[:, :]),
                    reads=[pb], writes=[dst])
        pb = self.banks[6]
        for half in range(2):
            for a in range(NSL):
                self.T_(lambda t, a=a, half=half, pb=pb: t.matmul(
                    pb.ap[0:16, 0:512], self.Wbb.ap[:, a, :],
                    self.hTb.ap[:, a, 3 + half * 512:3 + (half + 1) * 512],
                    start=(a == 0), stop=(a == NSL - 1)),
                    reads=[self.Wbb, self.hTb], writes=[pb])
            self.V_(lambda v, pb=pb, half=half: v.tensor_copy(
                self.betaR.ap[0:16, half * 512:(half + 1) * 512],
                pb.ap[0:16, 0:512]),
                reads=[pb], writes=[self.betaR])
        for ch in range(NCH):
            pb7 = self.banks[7]
            self.T_(lambda t, ch=ch, pb7=pb7: t.transpose(
                pb7.ap[:, 0:16], self.betaR.ap[0:16, ch * 128:(ch + 1) * 128],
                self.identf.ap[:, :]),
                reads=[self.betaR, self.identf], writes=[pb7])
            self.S_(lambda s, ch=ch, pb7=pb7: s.activation(
                self.beta_tm.ap[:, ch, :], pb7.ap[:, 0:16], ACT.Exp, scale=-1.0),
                reads=[pb7], writes=[self.beta_tm])
            self.V_(lambda v, ch=ch: v.tensor_scalar(
                self.beta_tm.ap[:, ch, :], self.beta_tm.ap[:, ch, :], 1.0, None,
                ALU.add), writes=[self.beta_tm])
            self.V_(lambda v, ch=ch: v.reciprocal(
                self.beta_tm.ap[:, ch, :], self.beta_tm.ap[:, ch, :]),
                writes=[self.beta_tm])
        self.tap("f1c", self.f1c, [128, Tc], BF16)
        self.tap("G1c", self.G1c, [128, Tc], BF16)
        self.tap("beta", self.beta_tm, [128, NCH, H], F32)

    def a3_l2norm(self):
        for ch in range(NCH):
            self.V_(lambda v, ch=ch: v.tensor_mul(
                self.sq2.ap[:, :], self.wc.ap[:, ch, :], self.wc.ap[:, ch, :]),
                reads=[self.wc], writes=[self.sq2])
            self.V_(lambda v: v.tensor_reduce(
                self.rst.ap[:, :], self.sq2.ap[:, :].rearrange(
                    "p (h m) -> p h m", h=H), AX.X, ALU.add),
                reads=[self.sq2], writes=[self.rst])
            self.S_(lambda s: s.activation(self.rst.ap[:, :], self.rst.ap[:, :],
                                           ACT.Ln, bias=self.cpool.ap[:, 0:1]),
                    reads=[self.cpool], writes=[self.rst])
            self.S_(lambda s: s.activation(self.rst.ap[:, :], self.rst.ap[:, :],
                                           ACT.Exp, scale=-0.5), writes=[self.rst])
            self.V_(lambda v, ch=ch: v.tensor_mul(
                self.rst.ap[:, :], self.rst.ap[:, :], self.beta_tm.ap[:, ch, :]),
                reads=[self.beta_tm], writes=[self.rst])
            self.V_(lambda v, ch=ch: v.tensor_tensor(
                self.wc.ap[:, ch, :].rearrange("p (h m) -> p h m", h=H),
                self.wc.ap[:, ch, :].rearrange("p (h m) -> p h m", h=H),
                bc_in(self.rst.ap[:, :], 128), ALU.mult),
                reads=[self.rst], writes=[self.wc])
        self.tap("wcn", self.wc, [128, NCH, HID], BF16)

    def emit_sp(self, sp, ch):
        for j in range(4):
            pb = self.banks[j]
            self.T_(lambda t, j=j, ch=ch, pb=pb: t.matmul(
                pb.ap[:, :], self.f1c.ap[:, ch * 128:(ch + 1) * 128],
                self.Wf2b.ap[:, j * 512:(j + 1) * 512], start=True, stop=True),
                reads=[self.f1c, self.Wf2b], writes=[pb])
            self.S_(lambda s, j=j, pb=pb, sp=sp: s.activation(
                sp.ap[:, j * 512:(j + 1) * 512], pb.ap[:, :], ACT.Exp, scale=-1.0),
                reads=[pb], writes=[sp])
            self.S_(lambda s, j=j, sp=sp: s.activation(
                sp.ap[:, j * 512:(j + 1) * 512], sp.ap[:, j * 512:(j + 1) * 512],
                ACT.Ln, bias=self.cpool.ap[:, 1:2]),
                reads=[self.cpool], writes=[sp])

    def emit_cumexp(self, U, sp, dst_scales):
        for j in range(4):
            pb = self.banks[j]
            self.T_(lambda t, j=j, pb=pb, U=U, sp=sp: t.matmul(
                pb.ap[:, :], U.ap[:, :], sp.ap[:, j * 512:(j + 1) * 512],
                start=True, stop=True), reads=[U, sp], writes=[pb])
            for (dst, sc) in dst_scales:
                self.S_(lambda s, j=j, pb=pb, dst=dst, sc=sc: s.activation(
                    dst.ap[:, j * 512:(j + 1) * 512], pb.ap[:, :], ACT.Exp,
                    scale=sc), reads=[pb], writes=[dst])

    def emit_cols(self, sp, col, dst_ap, bank_i=5):
        pb = self.banks[bank_i]
        for h in range(H):
            self.T_(lambda t, h=h, pb=pb, sp=sp, col=col: t.matmul(
                pb.ap[:, h:h + 1], sp.ap[:, h * 128:(h + 1) * 128], col.ap[:, :],
                start=True, stop=True),
                reads=[sp, col], writes=[pb])
        return self.S_(lambda s, pb=pb, dst_ap=dst_ap: s.activation(
            dst_ap, pb.ap[:, 0:H], ACT.Exp, scale=-1.0),
            reads=[pb, self.ecol], writes=[self.ecol])

    def emit_cols2(self, sp, bank_i=5):
        pb = self.banks[bank_i]
        for h in range(H):
            self.T_(lambda t, h=h, pb=pb, sp=sp: t.matmul(
                pb.ap[:, 2 * h:2 * h + 2], sp.ap[:, h * 128:(h + 1) * 128],
                self.colpair.ap[:, :], start=True, stop=True),
                reads=[sp, self.colpair], writes=[pb])
        ec = self.ecol.ap[:, 0:2 * H].rearrange("p (a b) -> p b a", a=2) \
            if False else None
        pv = pb.ap[:, 0:2 * H].rearrange("p (h w) -> p w h", h=H)
        dv = self.ecol.ap[:, 0:2 * H].rearrange("p (w h) -> p w h", w=2)
        self.S_(lambda s: s.activation(dv, pv, ACT.Exp, scale=-1.0),
                reads=[pb], writes=[self.ecol])

    def emit_diag(self, dst, cols_ap, n=H, reads=()):
        for h in range(n):
            self.V_(lambda v, h=h, dst=dst, cols_ap=cols_ap: v.tensor_scalar(
                dst.ap[:, h, :], self.identb.ap[:, :],
                cols_ap[:, h:h + 1], None, ALU.mult),
                reads=[self.identb] + list(reads),
                writes=[dst])

    def emit_ktm(self, dst, ch, bank0=6):
        for g8 in range(2):
            pbT = self.banks[bank0 + g8]
            for hi in range(8):
                h = g8 * 8 + hi
                self.T_(lambda t, h=h, ch=ch, pbT=pbT, hi=hi: t.transpose(
                    pbT.bf[:, hi * 128:(hi + 1) * 128],
                    self.kc.ap[:, h, ch * 128:(ch + 1) * 128],
                    self.identb.ap[:, :]),
                    reads=[self.kc, self.identb], writes=[pbT])
            self.V_(lambda v, g8=g8, pbT=pbT, dst=dst: v.tensor_copy(
                dst.ap[:, g8 * 8:(g8 + 1) * 8, :],
                pbT.bf[:, 0:1024].rearrange("p (a b) -> p a b", a=8)),
                reads=[pbT], writes=[dst])

    def emit_states(self, ktm, bwdb, diag, ch, bank0=0):
        for g8 in range(2):
            evac_w = []
            for hi in range(8):
                h = g8 * 8 + hi
                bsk = self.banks[bank0 + hi // 4]
                bsv = self.banks[bank0 + 2 + hi // 4]
                sl = (hi % 4) * 128
                self.T_(lambda t, h=h, bsk=bsk, sl=sl, bwdb=bwdb, ktm=ktm:
                        t.matmul(bsk.ap[:, sl:sl + 128],
                                 bwdb.ap[:, h * 128:(h + 1) * 128],
                                 ktm.ap[:, h, :], start=True,
                                 stop=False),
                        reads=[bwdb, ktm], writes=[bsk])
                self.T_(lambda t, h=h, bsk=bsk, sl=sl, diag=diag: t.matmul(
                    bsk.ap[:, sl:sl + 128], diag.ap[:, h, :],
                    self.SkTs.ap[:, h, :], start=False, stop=True),
                    reads=[diag, self.SkTs], writes=[bsk])
                self.T_(lambda t, h=h, bsv=bsv, sl=sl, ch=ch, bwdb=bwdb: t.matmul(
                    bsv.ap[:, sl:sl + 128], bwdb.ap[:, h * 128:(h + 1) * 128],
                    self.vc.ap[:, ch, h * 128:(h + 1) * 128],
                    start=True, stop=False),
                    reads=[bwdb, self.vc], writes=[bsv])
                tok = self.T_(lambda t, h=h, bsv=bsv, sl=sl, diag=diag: t.matmul(
                    bsv.ap[:, sl:sl + 128], diag.ap[:, h, :],
                    self.Svs.ap[:, h, :], start=False, stop=True),
                    reads=[diag, self.Svs], writes=[bsv])
                evac_w.append(tok)
            for bi in range(2):
                bsk = self.banks[bank0 + bi]
                bsv = self.banks[bank0 + 2 + bi]
                hlo = g8 * 8 + bi * 4
                self.V_(lambda v, bsk=bsk, hlo=hlo: v.tensor_copy(
                    self.SkTs.ap[:, hlo:hlo + 4, :],
                    bsk.ap[:, :].rearrange("p (a b) -> p a b", a=4)),
                    reads=[bsk], writes=[self.SkTs], extra=evac_w)
                self.V_(lambda v, bsv=bsv, hlo=hlo: v.tensor_copy(
                    self.Svs.ap[:, hlo:hlo + 4, :],
                    bsv.ap[:, :].rearrange("p (a b) -> p a b", a=4)),
                    reads=[bsv], writes=[self.Svs], extra=evac_w)

    def a4_states(self):
        self.V_(lambda v: v.memset(self.SkTs.ap[:, :, :], 0.0), writes=[self.SkTs])
        self.V_(lambda v: v.memset(self.Svs.ap[:, :, :], 0.0), writes=[self.Svs])
        self.V_(lambda v: v.memset(self.Btn.ap[:, :], 0.0), writes=[self.Btn])
        for ch in range(NCH):
            self.emit_sp(self.spA, ch)
            pb5 = self.banks[5]
            for h in range(H):
                self.T_(lambda t, h=h, pb5=pb5: t.matmul(
                    pb5.ap[:, h:h + 1], self.spA.ap[:, h * 128:(h + 1) * 128],
                    self.onescol.ap[:, :], start=True, stop=True),
                    reads=[self.spA, self.onescol],
                    writes=[pb5])
            self.S_(lambda s, pb5=pb5: s.activation(
                self.ecol.ap[:, 0:H], pb5.ap[:, 0:H], ACT.Exp, scale=-1.0),
                reads=[pb5], writes=[self.ecol])
            self.V_(lambda v, pb5=pb5: v.tensor_add(
                self.Btn.ap[:, :], self.Btn.ap[:, :], pb5.ap[:, 0:H]),
                reads=[pb5], writes=[self.Btn])
            self.emit_cumexp(self.U3n, self.spA, [(self.escrA, -1.0)])
            self.V_(lambda v, ch=ch: v.tensor_mul(
                self.bwdA.ap[:, :], self.wc.ap[:, ch, :], self.escrA.ap[:, :]),
                reads=[self.wc, self.escrA], writes=[self.bwdA])
            self.emit_diag(self.diagA, self.ecol.ap[:, 0:H], reads=[self.ecol])
            self.emit_ktm(self.ktmA, ch)
            self.emit_states(self.ktmA, self.bwdA, self.diagA, ch)
        self.tap("SkT_loc", self.SkTs, [128, H, 128], BF16)
        self.tap("Sv_loc", self.Svs, [128, H, 128], BF16)
        self.tap("Btn", self.Btn, [128, H], F32)

    def a5_combine(self):
        V_, S_, T_ = self.V_, self.S_, self.T_
        pubv = self.pub1.rearrange("p (h two d) -> p h two d", h=H, two=2)
        t_p1a = self.dma_store(pubv[:, :, 0, :], self.SkTs.ap[:, :, :],
                               [self.SkTs])
        t_p1b = self.dma_store(pubv[:, :, 1, :], self.Svs.ap[:, :, :],
                               [self.Svs])
        t_p2 = self.dma_store(self.pub2[:, :], self.Btn.ap[:, :], [self.Btn])
        t_c1 = self.P.emit("gpsimd", lambda g: g.collective_compute(
            "AllGather", ALU.bypass, replica_groups=[list(range(NCORES))],
            ins=[self.pub1[:, :]], outs=[self.agg1[:, :]]),
            [t_p1a, t_p1b], "cs", 1)
        t_c2 = self.P.emit("gpsimd", lambda g: g.collective_compute(
            "AllGather", ALU.bypass, replica_groups=[list(range(NCORES))],
            ins=[self.pub2[:, :]], outs=[self.agg2[:, :]]),
            [t_p2], "cs", 1)
        augv = self.augB.ap.rearrange("p (m h) -> p m h", m=128)
        self.V_(lambda v: v.memset(self.augB.ap[0:9, :], 1.0), writes=[self.augB])
        self.dma("sync", augv[0:8, :, :],
                 self.agg2.rearrange("(c m) h -> c m h", m=128),
                 writes=[self.augB], extra=[t_c2])
        pb = self.banks[4]
        for h in range(H):
            self.T_(lambda t, h=h, pb=pb, augv=augv: t.matmul(
                pb.ap[:, (h % 8) * 8:(h % 8) * 8 + 8],
                augv[0:9, :, h:h + 1],
                self.MmatS.ap[:, :], start=True, stop=True),
                reads=[self.augB, self.MmatS],
                writes=[pb])
            if h % 8 == 7:
                hlo = h - 7
                self.S_(lambda s, pb=pb, hlo=hlo: s.activation(
                    self.WcolS.ap[:, hlo:hlo + 8, :].rearrange("p a b -> p (a b)"),
                    pb.ap[:, 0:64], ACT.Exp, scale=-1.0),
                    reads=[pb], writes=[self.WcolS])
        agg1v = self.agg1.rearrange("(c p) x -> p c x", p=128).rearrange(
            "p c (h d) -> p c h d", h=H)
        for hg in range(4):
            self.dma("sync", self.aggS.ap[:, :, :, :],
                     agg1v[:, :, hg * 4:(hg + 1) * 4, :],
                     writes=[self.aggS], extra=[t_c1])
            for hi in range(4):
                h = hg * 4 + hi
                self.emit_diag(self.diagC, self.WcolS.ap[:, h, :], n=NCORES,
                               reads=[self.WcolS])
                bsk = self.banks[0]
                bsv = self.banks[1]
                for c in range(NCORES):
                    self.T_(lambda t, hi=hi, c=c, bsk=bsk: t.matmul(
                        bsk.ap[:, 0:128], self.diagC.ap[:, c, :],
                        self.aggS.ap[:, c, hi, 0:128], start=(c == 0),
                        stop=(c == NCORES - 1)),
                        reads=[self.diagC, self.aggS],
                        writes=[bsk])
                    self.T_(lambda t, hi=hi, c=c, bsv=bsv: t.matmul(
                        bsv.ap[:, 0:128], self.diagC.ap[:, c, :],
                        self.aggS.ap[:, c, hi, 128:256], start=(c == 0),
                        stop=(c == NCORES - 1)),
                        reads=[self.diagC, self.aggS],
                        writes=[bsv])
                self.V_(lambda v, h=h, bsk=bsk: v.tensor_copy(
                    self.SkTs.ap[:, h, :], bsk.ap[:, 0:128]),
                    reads=[bsk], writes=[self.SkTs])
                self.V_(lambda v, h=h, bsv=bsv: v.tensor_copy(
                    self.Svs.ap[:, h, :], bsv.ap[:, 0:128]),
                    reads=[bsv], writes=[self.Svs])
        self.tap("SkT_in", self.SkTs, [128, H, 128], BF16)
        self.tap("Sv_in", self.Svs, [128, H, 128], BF16)

    def b_scan(self):
        T_, V_, S_ = self.T_, self.V_, self.S_
        for ch in range(NCH):
            self.emit_sp(self.spT, ch)
            self.emit_cumexp(self.U2n, self.spT,
                             [(self.expBp, -1.0), (self.escr, 1.0)])
            self.V_(lambda v, ch=ch: v.tensor_mul(
                self.bwR.ap[:, :], self.wc.ap[:, ch, :], self.escr.ap[:, :]),
                reads=[self.wc, self.escr], writes=[self.bwR])
            self.emit_cumexp(self.U3n, self.spT, [(self.escr, -1.0)])
            self.V_(lambda v, ch=ch: v.tensor_mul(
                self.bwd.ap[:, :], self.wc.ap[:, ch, :], self.escr.ap[:, :]),
                reads=[self.wc, self.escr], writes=[self.bwd])
            self.emit_cols2(self.spT)
            for j in range(4):
                pb = self.banks[j]
                self.T_(lambda t, j=j, ch=ch, pb=pb: t.matmul(
                    pb.ap[:, :], self.G1c.ap[:, ch * 128:(ch + 1) * 128],
                    self.Wg2b.ap[:, j * 512:(j + 1) * 512], start=True,
                    stop=True), reads=[self.G1c, self.Wg2b], writes=[pb])
                self.V_(lambda v, j=j, pb=pb: v.tensor_tensor(
                    self.sgrp.ap[:, :], pb.ap[:, :],
                    self.bg2T.ap[:, j * 512:(j + 1) * 512], ALU.add),
                    reads=[pb, self.bg2T], writes=[self.sgrp])
                self.S_(lambda s: s.activation(self.sgrp.ap[:, :],
                                               self.sgrp.ap[:, :], ACT.Exp,
                                               scale=-1.0), writes=[self.sgrp])
                self.V_(lambda v: v.tensor_scalar(self.sgrp.ap[:, :],
                                                  self.sgrp.ap[:, :], 1.0, None,
                                                  ALU.add), writes=[self.sgrp])
                self.V_(lambda v: v.reciprocal(self.sgrp.ap[:, :],
                                               self.sgrp.ap[:, :]),
                        writes=[self.sgrp])
                self.V_(lambda v, j=j: v.tensor_copy(
                    self.gsig.ap[:, j * 512:(j + 1) * 512], self.sgrp.ap[:, :]),
                    reads=[self.sgrp], writes=[self.gsig])
            self.emit_diag(self.diagB, self.ecol.ap[:, 0:H], reads=[self.ecol])
            for g4 in range(4):
                hlo = g4 * 4
                bPT = self.banks[4]
                bS = self.banks[5]
                bTR = self.banks[6]
                bO = self.banks[7]
                self.emit_diag(self.diagR, self.ecol.ap[:, H + hlo:H + hlo + 4],
                               n=4, reads=[self.ecol])
                for hi in range(4):
                    h = hlo + hi
                    self.T_(lambda t, h=h, hi=hi, bPT=bPT: t.matmul(
                        bPT.ap[:, hi * 128:(hi + 1) * 128],
                        self.diagR.ap[:, hi, :], self.SkTs.ap[:, h, :],
                        start=True, stop=True),
                        reads=[self.diagR, self.SkTs],
                        writes=[bPT])
                self.V_(lambda v, bPT=bPT: v.tensor_copy(
                    self.gSR.ap[:, :, :],
                    bPT.ap[:, :].rearrange("p (a b) -> p a b", a=4)),
                    reads=[bPT], writes=[self.gSR])
                for hi in range(4):
                    h = hlo + hi
                    self.T_(lambda t, h=h, hi=hi, ch=ch, bPT=bPT: t.matmul(
                        bPT.ap[:, hi * 128:(hi + 1) * 128],
                        self.kc.ap[:, h, ch * 128:(ch + 1) * 128],
                        self.qc.ap[:, h, ch * 128:(ch + 1) * 128],
                        start=True, stop=True),
                        reads=[self.kc, self.qc],
                        writes=[bPT])
                self.V_(lambda v, bPT=bPT: v.tensor_tensor(
                    self.gPM.ap[:, :, :],
                    bPT.ap[:, :].rearrange("p (a b) -> p a b", a=4),
                    bc_mid(self.trilS.ap[:, :], 4), ALU.mult),
                    reads=[bPT, self.trilS], writes=[self.gPM])
                for hi in range(4):
                    h = hlo + hi
                    self.T_(lambda t, h=h, hi=hi, bTR=bTR: t.transpose(
                        bTR.bf[:, 512 + hi * 128:512 + (hi + 1) * 128],
                        self.bwR.ap[:, h * 128:(h + 1) * 128],
                        self.identb.ap[:, :]),
                        reads=[self.bwR, self.identb],
                        writes=[bTR])
                self.V_(lambda v, bTR=bTR: v.tensor_copy(
                    self.gRT.ap[:, :, :],
                    bTR.bf[:, 512:1024].rearrange("p (a b) -> p a b", a=4)),
                    reads=[bTR], writes=[self.gRT])
                for hi in range(4):
                    h = hlo + hi
                    self.T_(lambda t, h=h, hi=hi, bS=bS: t.matmul(
                        bS.ap[:, hi * 128:(hi + 1) * 128], self.gPM.ap[:, hi, :],
                        self.bwR.ap[:, h * 128:(h + 1) * 128],
                        start=True, stop=False),
                        reads=[self.gPM, self.bwR],
                        writes=[bS])
                    self.T_(lambda t, h=h, hi=hi, ch=ch, bS=bS: t.matmul(
                        bS.ap[:, hi * 128:(hi + 1) * 128],
                        self.qc.ap[:, h, ch * 128:(ch + 1) * 128],
                        self.gSR.ap[:, hi, :], start=False, stop=True),
                        reads=[self.qc, self.gSR], writes=[bS])
                self.V_(lambda v, bS=bS, hlo=hlo: v.tensor_tensor(
                    self.sgrp.ap[:, :], bS.ap[:, :],
                    self.expBp.ap[:, hlo * 128:(hlo + 4) * 128], ALU.mult),
                    reads=[bS, self.expBp], writes=[self.sgrp])
                self.V_(lambda v: v.tensor_reduce(
                    self.smx.ap[:, 0:4], self.sgrp.ap[:, :].rearrange(
                        "p (a b) -> p a b", a=4), AX.X, ALU.max, negate=True),
                    reads=[self.sgrp], writes=[self.smx])
                for hi in range(4):
                    self.S_(lambda s, hi=hi: s.activation(
                        self.sgrp.ap[:, hi * 128:(hi + 1) * 128],
                        self.sgrp.ap[:, hi * 128:(hi + 1) * 128], ACT.Exp,
                        bias=self.smx.ap[:, hi:hi + 1]),
                        reads=[self.smx], writes=[self.sgrp])
                self.V_(lambda v: v.tensor_reduce(
                    self.smx.ap[:, 4:8], self.sgrp.ap[:, :].rearrange(
                        "p (a b) -> p a b", a=4), AX.X, ALU.add),
                    reads=[self.sgrp], writes=[self.smx])
                self.V_(lambda v: v.reciprocal(self.smx.ap[:, 4:8],
                                               self.smx.ap[:, 4:8]),
                        writes=[self.smx])
                self.V_(lambda v: v.tensor_tensor(
                    self.sgrp.ap[:, :].rearrange("p (a b) -> p a b", a=4),
                    self.sgrp.ap[:, :].rearrange("p (a b) -> p a b", a=4),
                    bc_in(self.smx.ap[:, 4:8], 128), ALU.mult),
                    reads=[self.smx], writes=[self.sgrp])
                self.V_(lambda v, hlo=hlo: v.tensor_tensor(
                    self.gAB.ap[:, :, :].rearrange("p a b -> p (a b)"),
                    self.sgrp.ap[:, :],
                    self.expBp.ap[:, hlo * 128:(hlo + 4) * 128], ALU.mult),
                    reads=[self.sgrp, self.expBp], writes=[self.gAB])
                for hi in range(4):
                    self.T_(lambda t, hi=hi, bTR=bTR: t.transpose(
                        bTR.bf[:, hi * 128:(hi + 1) * 128], self.gAB.ap[:, hi, :],
                        self.identb.ap[:, :]),
                        reads=[self.gAB, self.identb],
                        writes=[bTR])
                self.V_(lambda v, bTR=bTR: v.tensor_copy(
                    self.gAT.ap[:, :, :],
                    bTR.bf[:, 0:512].rearrange("p (a b) -> p a b", a=4)),
                    reads=[bTR], writes=[self.gAT])
                for hi in range(4):
                    self.T_(lambda t, hi=hi, bPT=bPT: t.matmul(
                        bPT.ap[:, hi * 128:(hi + 1) * 128], self.gRT.ap[:, hi, :],
                        self.gAT.ap[:, hi, :], start=True, stop=True),
                        reads=[self.gRT, self.gAT],
                        writes=[bPT])
                self.V_(lambda v, bPT=bPT: v.tensor_tensor(
                    self.gP2.ap[:, :, :],
                    bPT.ap[:, :].rearrange("p (a b) -> p a b", a=4),
                    bc_mid(self.tril1.ap[:, :], 4), ALU.mult),
                    reads=[bPT, self.tril1], writes=[self.gP2])
                self.V_(lambda v, hlo=hlo: v.tensor_tensor(
                    self.svr.ap[:, :, :], self.Svs.ap[:, hlo:hlo + 4, :],
                    bc_in(self.ecol.ap[:, H + hlo:H + hlo + 4], 128), ALU.mult),
                    reads=[self.Svs, self.ecol], writes=[self.svr])
                for hi in range(4):
                    h = hlo + hi
                    self.T_(lambda t, h=h, hi=hi, ch=ch, bO=bO: t.matmul(
                        bO.ap[:, hi * 128:(hi + 1) * 128], self.gP2.ap[:, hi, :],
                        self.vc.ap[:, ch, h * 128:(h + 1) * 128],
                        start=True, stop=False),
                        reads=[self.gP2, self.vc],
                        writes=[bO])
                    self.T_(lambda t, hi=hi, bO=bO: t.matmul(
                        bO.ap[:, hi * 128:(hi + 1) * 128], self.gAT.ap[:, hi, :],
                        self.svr.ap[:, hi, :], start=False, stop=True),
                        reads=[self.gAT, self.svr], writes=[bO])
                self.S_(lambda s, bO=bO: s.activation(
                    self.ogrp.ap[:, :], bO.ap[:, :], ACT.Square),
                    reads=[bO], writes=[self.ogrp])
                self.V_(lambda v: v.tensor_reduce(
                    self.rst.ap[:, 0:4], self.ogrp.ap[:, :].rearrange(
                        "p (a b) -> p a b", a=4), AX.X, ALU.add),
                    reads=[self.ogrp], writes=[self.rst])
                self.S_(lambda s: s.activation(self.rst.ap[:, 0:4],
                                               self.rst.ap[:, 0:4], ACT.Ln,
                                               scale=1.0 / 128.0,
                                               bias=self.cpool.ap[:, 2:3]),
                        reads=[self.cpool], writes=[self.rst])
                self.S_(lambda s: s.activation(self.rst.ap[:, 0:4],
                                               self.rst.ap[:, 0:4], ACT.Exp,
                                               scale=-0.5), writes=[self.rst])
                self.V_(lambda v, bO=bO: v.tensor_tensor(
                    self.ogrp.ap[:, :].rearrange("p (a b) -> p a b", a=4),
                    bO.ap[:, :].rearrange("p (a b) -> p a b", a=4),
                    bc_in(self.rst.ap[:, 0:4], 128), ALU.mult),
                    reads=[bO, self.rst], writes=[self.ogrp])
                self.V_(lambda v, hlo=hlo: v.tensor_tensor(
                    self.gOF.ap[:, :, :].rearrange("p a b -> p (a b)"),
                    self.ogrp.ap[:, :],
                    self.gsig.ap[:, hlo * 128:(hlo + 4) * 128], ALU.mult),
                    reads=[self.ogrp, self.gsig], writes=[self.gOF])
                for hi in range(4):
                    self.T_(lambda t, hi=hi, bTR=bTR: t.transpose(
                        bTR.bf[:, 512 + hi * 128:512 + (hi + 1) * 128],
                        self.gOF.ap[:, hi, :], self.identb.ap[:, :]),
                        reads=[self.gOF, self.identb],
                        writes=[bTR])
                self.V_(lambda v, bTR=bTR: v.tensor_copy(
                    self.oTst.ap[:, :, :],
                    bTR.bf[:, 512:1024].rearrange("p (a b) -> p a b", a=4)),
                    reads=[bTR], writes=[self.oTst])
                self.dma_store(
                    self.o_scr.rearrange("(s p) t -> p s t", p=128)[
                        :, hlo:hlo + 4, ch * 128:(ch + 1) * 128],
                    self.oTst.ap[:, :, :], [self.oTst])
            self.emit_ktm(self.ktm, ch, bank0=4)
            self.emit_states(self.ktm, self.bwd, self.diagB, ch, bank0=4)

    def b3_wo_ln1(self):
        T_, V_, S_ = self.T_, self.V_, self.S_
        I = self.I
        st_toks = [(f"st{i}", self.P.cnt[f"st{i}"])
                   for i in range(self.P.LANES["st"]) if self.P.cnt[f"st{i}"]]
        self.dma("sync", self.oT.ap[:, :, :],
                 self.o_scr.rearrange("(s p) t -> p s t", p=128),
                 writes=[self.oT], extra=st_toks)
        for band in range(2):
            for i in range(8):
                sl = band * 8 + i
                wb = self.wo_b[i]
                self.dma("gpsimd", wb.ap[:, :],
                         I["Wo"][sl * 128:(sl + 1) * 128, :], writes=[wb])
                self.V_(lambda v, wb=wb: v.tensor_scalar(
                    wb.ap[:, :], wb.ap[:, :], self.nwcol.ap[:, 0:1], None,
                    ALU.mult), reads=[self.nwcol], writes=[wb])
            for tc in range(NCH):
                par = tc % 2
                if band == 0:
                    xb = self.xs[0]
                    self.dma("sync", xb.ap[:, :],
                             I["h_nat"][tc * 128:(tc + 1) * 128, :], writes=[xb])
                for j in range(4):
                    pb = self.banks[par * 4 + j]
                    for i in range(8):
                        sl = band * 8 + i
                        wb = self.wo_b[i]
                        self.T_(lambda t, i=i, j=j, tc=tc, sl=sl, pb=pb, wb=wb:
                                t.matmul(pb.ap[:, :],
                                         self.oT.ap[:, sl, tc * 128:(tc + 1) * 128],
                                         wb.ap[:, j * 512:(j + 1) * 512],
                                         start=(i == 0), stop=(i == 7)),
                                reads=[self.oT, wb],
                                writes=[pb])
                for j in range(4):
                    pb = self.banks[par * 4 + j]
                    if band == 0:
                        xb = self.xs[0]
                        self.V_(lambda v, j=j, tc=tc, pb=pb, xb=xb: v.tensor_tensor(
                            self.r1a.ap[:, tc, j * 512:(j + 1) * 512], pb.ap[:, :],
                            xb.ap[:, j * 512:(j + 1) * 512], ALU.add),
                            reads=[pb, xb], writes=[self.r1a])
                    else:
                        self.V_(lambda v, j=j, tc=tc, pb=pb: v.tensor_tensor(
                            self.r1a.ap[:, tc, j * 512:(j + 1) * 512],
                            self.r1a.ap[:, tc, j * 512:(j + 1) * 512],
                            pb.ap[:, :], ALU.add),
                            reads=[pb], writes=[self.r1a])
        for tc in range(NCH):
            self.emit_ln(tc, to_xT=True)
        self.tap("x1", self.r1a, [128, NCH, HID], F32)

    def emit_ln(self, tc, to_xT):
        V_, S_, T_ = self.V_, self.S_, self.T_
        acc = self.r1a
        sq = self.sqscr if to_xT else self.sqscr2
        self.V_(lambda v, tc=tc: v.tensor_reduce(
            self.lnc.ap[:, 0:1], acc.ap[:, tc, :], AX.X, ALU.add),
            reads=[acc], writes=[self.lnc])
        self.V_(lambda v: v.tensor_scalar(self.lnc.ap[:, 0:1], self.lnc.ap[:, 0:1],
                                          1.0 / HID, None, ALU.mult),
                writes=[self.lnc])
        self.S_(lambda s, tc=tc, sq=sq: s.activation(
            sq.ap[:, :], acc.ap[:, tc, :], ACT.Square,
            accum_out=self.lnc.ap[:, 1:2]),
            reads=[acc], writes=[sq, self.lnc])
        self.V_(lambda v: v.tensor_scalar(self.lnc.ap[:, 1:2], self.lnc.ap[:, 1:2],
                                          1.0 / HID, None, ALU.mult),
                writes=[self.lnc])
        self.V_(lambda v: v.tensor_mul(self.lnc.ap[:, 2:3], self.lnc.ap[:, 0:1],
                                       self.lnc.ap[:, 0:1]), writes=[self.lnc])
        self.V_(lambda v: v.tensor_sub(self.lnc.ap[:, 1:2], self.lnc.ap[:, 1:2],
                                       self.lnc.ap[:, 2:3]), writes=[self.lnc])
        self.S_(lambda s: s.activation(self.lnc.ap[:, 3:4], self.lnc.ap[:, 1:2],
                                       ACT.Ln, bias=self.cpool.ap[:, 2:3]),
                reads=[self.cpool], writes=[self.lnc])
        self.S_(lambda s: s.activation(self.lnc.ap[:, 3:4], self.lnc.ap[:, 3:4],
                                       ACT.Exp, scale=-0.5), writes=[self.lnc])
        self.V_(lambda v, tc=tc: v.tensor_scalar(
            acc.ap[:, tc, :], acc.ap[:, tc, :], self.lnc.ap[:, 0:1], None,
            ALU.subtract), reads=[self.lnc], writes=[acc])
        self.V_(lambda v, tc=tc: v.tensor_scalar(
            acc.ap[:, tc, :], acc.ap[:, tc, :], self.lnc.ap[:, 3:4], None,
            ALU.mult), reads=[self.lnc], writes=[acc])
        if not to_xT:
            return
        self.V_(lambda v, tc=tc, sq=sq: v.tensor_copy(sq.ap[:, :],
                                                      acc.ap[:, tc, :]),
                reads=[acc], writes=[sq])
        for g8 in range(2):
            pbT = self.banks[4 + g8]
            for hi in range(8):
                sl = g8 * 8 + hi
                self.T_(lambda t, sl=sl, hi=hi, pbT=pbT, sq=sq: t.transpose(
                    pbT.bf[:, hi * 128:(hi + 1) * 128],
                    sq.ap[:, sl * 128:(sl + 1) * 128], self.identb.ap[:, :]),
                    reads=[sq, self.identb], writes=[pbT])
            self.V_(lambda v, g8=g8, tc=tc, pbT=pbT: v.tensor_copy(
                self.xT.ap[:, g8 * 8:(g8 + 1) * 8, tc * 128:(tc + 1) * 128],
                pbT.bf[:, 0:1024].rearrange("p (a b) -> p a b", a=8)),
                reads=[pbT], writes=[self.xT])

    def b4_ffn(self):
        T_, V_, S_ = self.T_, self.V_, self.S_
        I = self.I

        def emit_gateup(hb, hs):
            hidx = hb * 8 + hs
            par = hs % 2
            fg, fu = self.ffg[par], self.ffu[par]
            gbb = self.gb[hb % 2]
            self.dma("gpsimd", fg.ap[:, :, :],
                     I["wgT"][:, hidx * 128:(hidx + 1) * 128].rearrange(
                         "(a p) c -> p a c", p=128), writes=[fg])
            self.dma("gpsimd", fu.ap[:, :, :],
                     I["wuT"][:, hidx * 128:(hidx + 1) * 128].rearrange(
                         "(a p) c -> p a c", p=128), writes=[fu])
            bg = self.banks[4 + par]
            bu = self.banks[6 + par]
            for half in range(2):
                for (bk, fw) in ((bg, fg), (bu, fu)):
                    for a in range(NSL):
                        self.T_(lambda t, a=a, half=half, bk=bk, fw=fw: t.matmul(
                            bk.ap[:, :], fw.ap[:, a, :],
                            self.xT.ap[:, a, half * 512:(half + 1) * 512],
                            start=(a == 0), stop=(a == NSL - 1)),
                            reads=[fw, self.xT], writes=[bk])
                self.S_(lambda s, bg=bg, half=half: s.activation(
                    self.sgf.ap[:, half * 512:(half + 1) * 512], bg.ap[:, :],
                    ACT.Silu), reads=[bg], writes=[self.sgf])
                self.V_(lambda v, bu=bu, hs=hs, half=half, gbb=gbb:
                        v.tensor_tensor(
                    gbb.ap[:, hs, half * 512:(half + 1) * 512], bu.ap[:, :],
                    self.sgf.ap[:, half * 512:(half + 1) * 512], ALU.mult),
                    reads=[bu, self.sgf], writes=[gbb])

        def emit_down_unit(hb, unit):
            tc, jh = unit // 2, unit % 2
            pair = unit % 2
            gbb = self.gb[hb % 2]
            for jj in range(2):
                j = jh * 2 + jj
                pb = self.banks[pair * 2 + jj]
                for i in range(8):
                    wd = self.wd_all[i]
                    self.T_(lambda t, i=i, j=j, tc=tc, pb=pb, wd=wd, gbb=gbb:
                            t.matmul(
                        pb.ap[:, :], gbb.ap[:, i, tc * 128:(tc + 1) * 128],
                        wd.ap[:, j * 512:(j + 1) * 512],
                        start=(i == 0), stop=(i == 7)),
                        reads=[gbb, wd], writes=[pb])
            for jj in range(2):
                j = jh * 2 + jj
                pb = self.banks[pair * 2 + jj]
                self.V_(lambda v, j=j, tc=tc, pb=pb: v.tensor_tensor(
                    self.r1a.ap[:, tc, j * 512:(j + 1) * 512],
                    self.r1a.ap[:, tc, j * 512:(j + 1) * 512],
                    pb.ap[:, :], ALU.add),
                    reads=[pb], writes=[self.r1a])

        def emit_wd_loads(hb):
            for i in range(8):
                hidx = hb * 8 + i
                wd = self.wd_all[i]
                self.dma("gpsimd", wd.ap[:, :],
                         I["wdT"][hidx * 128:(hidx + 1) * 128, :], writes=[wd])

        emit_wd_loads(0)
        for hs in range(8):
            emit_gateup(0, hs)
        for hb in range(1, 8):
            for hs in range(8):
                emit_down_unit(hb - 1, 2 * hs)
                emit_down_unit(hb - 1, 2 * hs + 1)
                emit_gateup(hb, hs)
            emit_wd_loads(hb)
        for unit in range(16):
            emit_down_unit(7, unit)
        for tc in range(NCH):
            self.emit_ln(tc, to_xT=False)
            self.dma_store(self.out[tc * 128:(tc + 1) * 128, :],
                           self.r1a.ap[:, tc, :], [self.r1a])

    def finish(self):
        toks = [(f"st{i}", self.P.cnt[f"st{i}"])
                for i in range(self.P.LANES["st"]) if self.P.cnt[f"st{i}"]]
        if toks:
            self.P.emit("gpsimd", lambda g: g.engine_nop(), toks, "gs", 1)


def build_host_inputs(inputs):
    h = np.ascontiguousarray(np.asarray(inputs["L0_attn_hidden_states"],
                                        np.float32)[0])
    hT_full = np.ascontiguousarray(h.T)
    per_core = []
    for c in range(NCORES):
        lo = c * Tc
        hT_ext = np.zeros((HID, Tc + 3), np.float32)
        if c == 0:
            hT_ext[:, 3:] = hT_full[:, 0:Tc]
        else:
            hT_ext[:, :] = hT_full[:, lo - 3:lo + Tc]
        Mm = np.zeros((9, 8), np.float32)
        for cp in range(8):
            if cp < c:
                for cm in range(cp + 1, c):
                    Mm[cm, cp] = 1.0
            else:
                Mm[8, cp] = 1e5
        per_core.append({
            "hT_ext": hT_ext,
            "h_nat": np.ascontiguousarray(h[lo:lo + Tc]),
            "Mmat": Mm,
        })
    f32 = lambda k: np.ascontiguousarray(np.asarray(inputs[k], np.float32))
    shared = {
        "Wq": f32("L0_attn_Wq"), "Wk": f32("L0_attn_Wk"),
        "Wv": f32("L0_attn_Wv"), "Ww": f32("L0_attn_Ww"),
        "Wo": f32("L0_attn_Wo"),
        "cq": f32("L0_attn_cq"), "ck": f32("L0_attn_ck"),
        "cv": f32("L0_attn_cv"),
        "Wf1": f32("L0_attn_Wf1"), "Wf2": f32("L0_attn_Wf2"),
        "Wb": f32("L0_attn_Wb"),
        "Wg1": f32("L0_attn_Wg1"), "Wg2": f32("L0_attn_Wg2"),
        "bg2": f32("L0_attn_bg2"), "normw": f32("L0_attn_norm_w"),
        "wgT": np.ascontiguousarray(np.asarray(inputs["L0_ffn_w_gate"],
                                               np.float32).T),
        "wuT": np.ascontiguousarray(np.asarray(inputs["L0_ffn_w_up"],
                                               np.float32).T),
        "wdT": np.ascontiguousarray(np.asarray(inputs["L0_ffn_w_down"],
                                               np.float32).T),
    }
    return [{**shared, **pc} for pc in per_core]


_BUILD_CACHE = {}


def kernel(**inputs):
    in_maps = build_host_inputs(inputs)
    key = (STAGES, WANT_DBG)
    if key not in _BUILD_CACHE:
        b = Builder()
        nc = b.build()
        _BUILD_CACHE[key] = (nc, b)
    nc, b = _BUILD_CACHE[key]
    res = run_bass_kernel_spmd(nc, in_maps, core_ids=list(range(NCORES)))
    out = np.concatenate([r["out"] for r in res.results], axis=0)
    return out[None]


# version 08
# speedup vs baseline: 1.0146x; 1.0024x over previous
import numpy as np
import ml_dtypes
import concourse.bass as bass
import concourse.mybir as mybir
from concourse.bass_utils import run_bass_kernel_spmd

F32 = mybir.dt.float32
BF16 = mybir.dt.bfloat16
U8 = mybir.dt.uint8
AX = mybir.AxisListType
ALU = mybir.AluOpType
ACT = mybir.ActivationFunctionType

NCORES = 8
T = 8192
HID = 2048
Tc = T // NCORES
C = 128
NCH = Tc // C
H = 16
NSL = HID // 128
DH = 8192
SCALE = float(128) ** -0.5

ENGINES = ("sync", "tensor", "vector", "scalar", "gpsimd")
STAGES = "all"
WANT_DBG = ()


class Buf:
    __slots__ = ("ap", "w", "r", "rng", "fresh")

    def __init__(self, ap, rng=None):
        self.ap = ap
        self.w = None
        self.r = {}
        self.rng = rng
        self.fresh = True

    def add_r(self, tok):
        s, v = tok
        if self.r.get(s, -1) < v:
            self.r[s] = v

    def r_toks(self):
        return [(s, v) for s, v in self.r.items()]

    def __getitem__(self, key):
        return self.ap[key]


class Bank(Buf):
    __slots__ = ("bf",)

    def __init__(self, ap):
        super().__init__(ap)
        self.bf = ap.bitcast(BF16)


class Plan:
    LANES = {"ld": 4, "lw": 6, "st": 4}

    def __init__(self, nc):
        self.nc = nc
        self.plan = []
        self.sems = {}
        self.cnt = {}
        for s in ("ts", "vs", "ss", "gs", "cs"):
            self.sems[s] = nc.alloc_semaphore(s)
            self.cnt[s] = 0
        self.rr = {}
        for q, n in self.LANES.items():
            self.rr[q] = 0
            for i in range(n):
                name = f"{q}{i}"
                self.sems[name] = nc.alloc_semaphore(name)
                self.cnt[name] = 0

    def lane(self, q):
        i = self.rr[q] % self.LANES[q]
        self.rr[q] += 1
        name = f"{q}{i}"
        pre = (name, self.cnt[name]) if self.cnt[name] else None
        return name, pre

    def emit(self, eng, fn, waits, inc, inc_by):
        tok = None
        if inc is not None:
            self.cnt[inc] += inc_by
            tok = (inc, self.cnt[inc])
        self.plan.append((eng, fn, tuple(w for w in waits if w is not None),
                          (inc, inc_by) if inc else None))
        return tok

    def realize(self, block):
        wm = {e: {} for e in ENGINES}

        def mk(eng_name):
            def body(eng):
                for (e, fn, waits, inc) in self.plan:
                    if e != eng_name:
                        continue
                    for (s, v) in waits:
                        if wm[e].get(s, 0) < v:
                            eng.wait_ge(self.sems[s], v)
                            wm[e][s] = v
                    ins = fn(eng)
                    if inc is not None:
                        s, by = inc
                        ins.then_inc(self.sems[s], by)
            return body

        block.sync(mk("sync"))
        block.tensor(mk("tensor"))
        block.vector(mk("vector"))
        block.scalar(mk("scalar"))
        block.gpsimd(mk("gpsimd"))


def bc_in(ap, n):
    return bass.AP(ap.tensor, ap.offset, [list(p) for p in ap.ap] + [[0, n]])


def bc_mid(ap, n):
    pr = [list(p) for p in ap.ap]
    return bass.AP(ap.tensor, ap.offset, [pr[0], [0, n]] + pr[1:])


class Builder:
    SEM_OF = {"sync": "ld", "gpsimd_dma": "lw", "tensor": "ts", "vector": "vs",
              "scalar": "ss", "gpsimd": "gs"}

    def __init__(self):
        nc = bass.Bass("TRN2", target_bir_lowering=False, debug=False)
        self.nc = nc
        self.P = Plan(nc)
        self.dbg = {}

        I = {}
        def inp(name, shape, dtype=F32):
            I[name] = nc.dram_tensor(name, shape, dtype, kind="ExternalInput").ap()
        self.I = I
        inp("hT_ext", [HID, Tc + 3]); inp("h_nat", [Tc, HID])
        for w in ("Wq", "Wk", "Wv", "Ww", "Wo"):
            inp(w, [HID, HID])
        for w in ("cq", "ck", "cv"):
            inp(w, [HID, 4])
        inp("Wf1", [HID, 128]); inp("Wf2", [128, HID]); inp("Wb", [HID, H])
        inp("Wg1", [HID, 128]); inp("Wg2", [128, HID])
        inp("bg2", [HID]); inp("normw", [128])
        inp("wgT", [HID, DH]); inp("wuT", [HID, DH]); inp("wdT", [DH, HID])
        inp("Mmat", [9, 8])
        self.out = nc.dram_tensor("out", [Tc, HID], F32, kind="ExternalOutput").ap()

        self.pub1 = nc.dram_tensor("pub1", [128, H * 256], BF16).ap()
        self.pub2 = nc.dram_tensor("pub2", [128, H], F32).ap()
        self.agg1 = nc.dram_tensor("agg1", [NCORES * 128, H * 256], BF16).ap()
        self.agg2 = nc.dram_tensor("agg2", [NCORES * 128, H], F32).ap()
        self.o_scr = nc.dram_tensor("o_scr", [HID, Tc], BF16).ap()

        def sb(name, shape, dtype):
            return Buf(nc.alloc_sbuf_tensor(name, shape, dtype).ap())
        self.trilS = sb("trilS", [128, 128], F32)
        self.tril1 = sb("tril1", [128, 128], BF16)
        self.identb = sb("identb", [128, 128], BF16)
        self.identf = sb("identf", [16, 16], F32)
        self.U2n = sb("U2n", [128, 128], F32)
        self.U3n = sb("U3n", [128, 128], F32)
        self.onescol = sb("onescol", [128, 1], F32)
        self.halfcol = sb("halfcol", [128, 1], F32)
        self.colpair = sb("colpair", [128, 2], F32)
        self.ones1p = sb("ones1p", [1, 128], F32)
        self.Wf2b = sb("Wf2b", [128, HID], BF16)
        self.Wg2b = sb("Wg2b", [128, HID], BF16)
        self.bg2T = sb("bg2T", [128, HID], BF16)
        self.nwcol = sb("nwcol", [128, 1], F32)
        self.MmatS = sb("MmatS", [9, 8], F32)
        self.f1c = sb("f1c", [128, Tc], BF16)
        self.G1c = sb("G1c", [128, Tc], BF16)
        self.SkTs = sb("SkTs", [128, H, 128], BF16)
        self.Svs = sb("Svs", [128, H, 128], BF16)
        self.Btn = sb("Btn", [128, H], F32)
        self.ecol = sb("ecol", [128, 2 * H], F32)
        self.beta_tm = sb("beta_tm", [128, NCH, H], F32)
        self.smx = sb("smx", [128, 16], F32)
        self.rst = sb("rst", [128, 16], F32)
        self.lnc = sb("lnc", [128, 8], F32)
        self.cpool = sb("cpool", [128, 4], F32)

        self.alias_reg = []
        self.AR = 184320
        self.arena = nc.alloc_sbuf_tensor("arena", [128, self.AR], U8)
        self.sbuf_slack = nc.sbuf_bytes_remaining
        assert self.sbuf_slack >= 0, f"sbuf over by {-self.sbuf_slack}"

        a = self.ab
        self.kc = a(0, [NSL, Tc], BF16)
        self.vc = a(32768, [NCH, HID], BF16)
        self.wc = a(65536, [NCH, HID], BF16)
        self.qc = a(98304, [NSL, Tc], BF16)
        self.hTb = a(131072, [NSL, Tc + 3], BF16)
        self.wsl = [a(164096, [NSL, 128], BF16), a(168192, [NSL, 128], BF16)]
        self.cvo = [a(172288, [264], F32), a(173376, [264], F32)]
        self.svo = [a(174464, [264], BF16), a(175040, [264], BF16)]
        self.betaR = a(175616, [1024], F32)
        self.bgrow = a(0, [HID], F32)
        self.Uh = a(8192, [128], F32)
        self.cwt = a(179712, [NSL, 16], F32)
        self.Wbb = a(180736, [NSL, H], BF16)
        self.sq2 = a(131072, [HID], BF16)
        self.spA = a(135168, [HID], F32)
        self.escrA = a(143360, [HID], BF16)
        self.bwdA = a(147456, [HID], BF16)
        self.ktmA = a(151552, [H, 128], BF16)
        self.diagA = a(155648, [H, 128], BF16)
        self.aggS = a(131072, [NCORES, 4, 256], BF16)
        self.augB = a(139264, [HID], F32)
        self.WcolS = a(147456, [H, 8], F32)
        self.diagC = a(155648, [NCORES, 128], BF16)
        self.spT = a(131072, [HID], F32)
        self.expBp = a(139264, [HID], BF16)
        self.escr = a(143360, [HID], BF16)
        self.bwR = a(147456, [HID], BF16)
        self.bwd = a(151552, [HID], BF16)
        self.gsig = a(155648, [HID], BF16)
        self.diagB = a(159744, [H, 128], BF16)
        self.ktm = a(163840, [H, 128], BF16)
        self.oTst = a(167936, [4, 128], BF16)
        self.gPM = a(168960, [4, 128], BF16)
        self.gRT = a(169984, [4, 128], BF16)
        self.gSR = a(171008, [4, 128], BF16)
        self.gAB = a(172032, [4, 128], BF16)
        self.gAT = a(173056, [4, 128], BF16)
        self.gP2 = a(174080, [4, 128], BF16)
        self.gOF = a(175104, [4, 128], BF16)
        self.sgrp = a(176128, [512], F32)
        self.ogrp = a(178176, [512], F32)
        self.svr = a(180224, [4, 128], BF16)
        self.diagR = a(181248, [4, 128], BF16)
        self.oT = a(0, [NSL, Tc], BF16)
        self.r1a = a(32768, [NCH, HID], F32)
        self.xT = a(98304, [NSL, Tc], BF16)
        self.wo_b = [a(131072 + i * 4096, [HID], BF16) for i in range(8)]
        self.sqscr = a(163840, [HID], BF16)
        self.xs = [a(167936, [HID], F32)]
        self.gb = [a(131072, [8, Tc], BF16), a(147456, [8, Tc], BF16)]
        self.ffg = [a(0, [NSL, 128], BF16), a(4096, [NSL, 128], BF16)]
        self.ffu = [a(8192, [NSL, 128], BF16), a(12288, [NSL, 128], BF16)]
        self.sgf = a(16384, [Tc], F32)
        self.wd_all = ([a(20480 + i * 4096, [HID], BF16) for i in range(3)] +
                       [a(163840 + i * 4096, [HID], BF16) for i in range(5)])
        self.sqscr2 = a(131072, [HID], BF16)

        self.banks = [Bank(nc.alloc_psum_tensor(f"pb{i}", [128, 512], F32).ap())
                      for i in range(8)]

    def ab(self, off, shape, dtype):
        esz = mybir.dt.size(dtype)
        n = int(np.prod(shape))
        assert off + n * esz <= self.AR, f"arena overflow at {off}+{n*esz}"
        ap = self.arena.ap()[:, off:off + n * esz].bitcast(dtype)
        if len(shape) > 1:
            dims = " ".join(f"d{i}" for i in range(len(shape)))
            kw = {f"d{i}": int(s) for i, s in enumerate(shape)}
            ap = ap.rearrange(f"p ({dims}) -> p {dims}", **kw)
        b = Buf(ap, rng=(off, off + n * esz))
        self.alias_reg.append(b)
        return b

    def _write_waits(self, b, waits):
        if b.w is not None:
            waits.append(b.w)
        waits.extend(b.r_toks())
        if b.fresh and b.rng is not None:
            lo, hi = b.rng
            for o in self.alias_reg:
                if o is b or o.rng is None:
                    continue
                if o.rng[0] < hi and lo < o.rng[1]:
                    if o.w is not None:
                        waits.append(o.w)
                    waits.extend(o.r_toks())
            b.fresh = False

    def op(self, eng, fn, reads=(), writes=(), extra=(), inc=None, inc_by=1):
        if inc is None:
            inc = self.SEM_OF[eng]
        waits = list(extra)
        for b in reads:
            if b.w is not None:
                waits.append(b.w)
        for b in writes:
            self._write_waits(b, waits)
        tok = self.P.emit(eng, fn, waits, inc, inc_by)
        for b in reads:
            b.add_r(tok)
        for b in writes:
            b.w = tok
            b.r = {}
        return tok

    def dma(self, eng, out_ap, in_ap, reads=(), writes=(), extra=(),
            slow_ok=False):
        q = "lw" if eng == "gpsimd" else "ld"
        sem, pre = self.P.lane(q)
        waits = list(extra)
        if pre is not None:
            waits.append(pre)
        for b in reads:
            if b.w is not None:
                waits.append(b.w)
        for b in writes:
            self._write_waits(b, waits)
        tok = self.P.emit(eng, lambda e: e.dma_start(
            out=out_ap, in_=in_ap, allow_slow_non_contiguous=slow_ok),
            waits, sem, 16)
        for b in reads:
            b.add_r(tok)
        for b in writes:
            b.w = tok
            b.r = {}
        return tok

    def dma_store(self, out_ap, in_ap, bufs, extra=(), eng="sync"):
        sem, pre = self.P.lane("st")
        waits = list(extra)
        if pre is not None:
            waits.append(pre)
        for b in bufs:
            if b.w is not None:
                waits.append(b.w)
        tok = self.P.emit(eng, lambda e: e.dma_start(out=out_ap, in_=in_ap),
                          waits, sem, 16)
        for b in bufs:
            b.add_r(tok)
        return tok

    def T_(self, fn, **kw):
        return self.op("tensor", fn, **kw)

    def V_(self, fn, **kw):
        return self.op("vector", fn, **kw)

    def S_(self, fn, **kw):
        return self.op("scalar", fn, **kw)

    def G_(self, fn, **kw):
        return self.op("gpsimd", fn, **kw)

    def emit_warmup(self, n=30):
        bk = self.banks[4]
        for i in range(n):
            self.T_(lambda t, bk=bk: t.matmul(
                bk.ap[:, :], self.identb.ap[:, :],
                self.bg2T.ap[:, 0:512], start=True, stop=True),
                reads=[self.identb, self.bg2T],
                writes=[bk] if i == 0 else (), extra=() if i == 0 else [bk.w])

    def tap(self, name, buf, shape, dtype):
        if name not in WANT_DBG:
            return
        d = self.nc.dram_tensor("dbg_" + name, shape, dtype,
                                kind="ExternalOutput").ap()
        self.dbg[name] = d
        self.dma_store(d[:, :] if len(shape) == 2 else d, buf.ap, [buf])

    def build(self):
        self.p0_constants()
        self.a1_projections()
        self.a2_aux()
        if STAGES != "A1":
            self.a3_l2norm()
            self.a4_states()
        if STAGES in ("A5", "B", "B3", "all"):
            self.a5_combine()
        if STAGES in ("B", "B3", "all"):
            self.b_scan()
        if STAGES in ("B3", "all"):
            self.b3_wo_ln1()
        if STAGES == "all":
            self.b4_ffn()
        self.finish()
        with self.nc.Block() as block:
            self.P.realize(block)
        return self.nc

    def p0_constants(self):
        G_, V_, S_, T_ = self.G_, self.V_, self.S_, self.T_
        I = self.I

        def mask(buf, fill_first, pat, cmp, fill, cmul, base=0):
            G_(lambda g: g.memset(buf.ap[:, :], fill_first), writes=[buf])
            G_(lambda g: g.affine_select(buf.ap[:, :], buf.ap[:, :], pattern=pat,
                                         compare_op=cmp, fill=fill,
                                         channel_multiplier=cmul, base=base),
               writes=[buf])

        mask(self.trilS, SCALE, [[1, 128]], ALU.is_ge, 0.0, -1)
        mask(self.tril1, 1.0, [[1, 128]], ALU.is_ge, 0.0, -1)
        for t_ in (self.identb,):
            mask(t_, 0.0, [[-1, 128]], ALU.not_equal, 1.0, 1)
        mask(self.identf, 0.0, [[-1, 16]], ALU.not_equal, 1.0, 1)
        mask(self.U2n, 1.0, [[1, 128]], ALU.is_ge, 0.0, -1)
        mask(self.Uh, 1.0, [[0, 128]], ALU.is_ge, 0.0, -1, base=63)
        self.V_(lambda v: v.tensor_sub(self.U2n.ap[:, :], self.U2n.ap[:, :],
                                       self.Uh.ap[:, :]),
                reads=[self.Uh], writes=[self.U2n])
        mask(self.U3n, 1.0, [[-1, 128]], ALU.is_ge, 0.0, 1, base=-1)
        G_(lambda g: g.memset(self.onescol.ap[:, :], 1.0), writes=[self.onescol])
        mask(self.halfcol, 1.0, [[0, 1]], ALU.is_ge, 0.0, -1, base=63)
        G_(lambda g: g.memset(self.ones1p.ap[:, :], 1.0), writes=[self.ones1p])
        self.V_(lambda v: v.tensor_copy(self.colpair.ap[:, 0:1],
                                        self.onescol.ap[:, :]),
                reads=[self.onescol], writes=[self.colpair])
        self.V_(lambda v: v.tensor_copy(self.colpair.ap[:, 1:2],
                                        self.halfcol.ap[:, :]),
                reads=[self.halfcol], writes=[self.colpair])
        for i, val in enumerate((1e-6, 1.0, 1e-5)):
            G_(lambda g, i=i, val=val: g.memset(self.cpool.ap[:, i:i + 1], val),
               writes=[self.cpool])

        for i, cwn in enumerate(("cq", "ck", "cv", "cv")):
            self.dma("sync", self.cwt.ap[:, :, 4 * i:4 * i + 4],
                     I[cwn].rearrange("(s p) t -> p s t", p=128),
                     writes=[self.cwt])
        self.dma("gpsimd", self.Wf2b.ap[:, :], I["Wf2"][:, :], writes=[self.Wf2b])
        self.dma("gpsimd", self.Wg2b.ap[:, :], I["Wg2"][:, :], writes=[self.Wg2b])
        self.dma("gpsimd", self.Wbb.ap[:, :, :],
                 I["Wb"].rearrange("(s p) h -> p s h", p=128), writes=[self.Wbb])
        self.dma("sync", self.nwcol.ap[:, :],
                 I["normw"].rearrange("(p o) -> p o", o=1), writes=[self.nwcol])
        self.dma("sync", self.MmatS.ap[:, :], I["Mmat"][:, :], writes=[self.MmatS])
        self.dma("sync", self.bgrow.ap[0:1, :],
                 I["bg2"].rearrange("(o f) -> o f", o=1), writes=[self.bgrow])
        for j in range(4):
            bk = self.banks[4 + j]
            self.T_(lambda t, j=j, bk=bk: t.matmul(
                bk.ap[:, :], self.ones1p.ap[:, :],
                self.bgrow.ap[0:1, j * 512:(j + 1) * 512], start=True, stop=True),
                reads=[self.ones1p, self.bgrow], writes=[bk])
            self.V_(lambda v, j=j, bk=bk: v.tensor_copy(
                self.bg2T.ap[:, j * 512:(j + 1) * 512], bk.ap[:, :]),
                reads=[bk], writes=[self.bg2T])
        self.dma("gpsimd", self.hTb.ap[:, :, :],
                 I["hT_ext"].rearrange("(s p) t -> p s t", p=128),
                 writes=[self.hTb])

    def a1_projections(self):
        T_, V_, S_ = self.T_, self.V_, self.S_
        I = self.I
        streams = [("q", "Wq", 0), ("k", "Wk", 1), ("v", "Wv", 2), ("w", "Ww", 3)]
        for sname, wn, sidx in streams:
            for fs in range(NSL):
                par = (sidx * NSL + fs) % 2
                wb = self.wsl[par]
                self.dma("gpsimd", wb.ap[:, :, :],
                         I[wn][:, fs * 128:(fs + 1) * 128].rearrange(
                             "(a p) c -> p a c", p=128), writes=[wb])
                for g in range(4):
                    t0 = g * 256
                    pb = self.banks[g]
                    for a in range(NSL):
                        self.T_(lambda t, a=a, pb=pb, t0=t0, wb=wb: t.matmul(
                            pb.ap[:, 0:259], wb.ap[:, a, :],
                            self.hTb.ap[:, a, t0:t0 + 259],
                            start=(a == 0), stop=(a == NSL - 1)),
                            reads=[wb, self.hTb], writes=[pb])
                    cp = g % 2
                    co = self.cvo[cp]
                    cb = sidx * 4
                    self.V_(lambda v, pb=pb, co=co, fs=fs, cb=cb: v.tensor_scalar(
                        co.ap[:, 0:256], pb.ap[:, 0:256],
                        self.cwt.ap[:, fs, cb:cb + 1], None, ALU.mult),
                        reads=[pb, self.cwt], writes=[co])
                    for i in range(1, 4):
                        self.V_(lambda v, pb=pb, co=co, fs=fs, cb=cb, i=i:
                                v.scalar_tensor_tensor(
                                    co.ap[:, 0:256], pb.ap[:, i:256 + i],
                                    self.cwt.ap[:, fs, cb + i:cb + i + 1],
                                    co.ap[:, 0:256], ALU.mult, ALU.add),
                                reads=[pb, self.cwt], writes=[co])
                    if sidx == 0:
                        self.S_(lambda s, co=co: s.activation(
                            co.ap[:, 0:256], co.ap[:, 0:256], ACT.Silu),
                            writes=[co])
                        self.V_(lambda v, co=co, fs=fs, t0=t0: v.tensor_scalar(
                            self.qc.ap[:, fs, t0:t0 + 256], co.ap[:, 0:256],
                            SCALE, None, ALU.mult), reads=[co],
                            writes=[self.qc])
                    elif sidx == 1:
                        self.S_(lambda s, co=co, fs=fs, t0=t0: s.activation(
                            self.kc.ap[:, fs, t0:t0 + 256], co.ap[:, 0:256],
                            ACT.Silu), reads=[co], writes=[self.kc])
                    else:
                        so = self.svo[cp]
                        self.S_(lambda s, co=co, so=so: s.activation(
                            so.ap[:, 0:256], co.ap[:, 0:256], ACT.Silu),
                            reads=[co], writes=[so])
                        dst = self.vc if sidx == 2 else self.wc
                        tb = self.banks[4 + cp]
                        for hf in range(2):
                            self.T_(lambda t, tb=tb, so=so, hf=hf: t.transpose(
                                tb.bf[:, hf * 128:(hf + 1) * 128],
                                so.ap[:, hf * 128:hf * 128 + 128],
                                self.identb.ap[:, :]),
                                reads=[so, self.identb],
                                writes=[tb])
                        self.V_(lambda v, tb=tb, dst=dst, g=g, fs=fs: v.tensor_copy(
                            dst.ap[:, 2 * g:2 * g + 2, fs * 128:(fs + 1) * 128],
                            tb.bf[:, 0:256].rearrange("p (a b) -> p a b", a=2)),
                            reads=[tb], writes=[dst])
        self.tap("qc", self.qc, [128, NSL, Tc], BF16)
        self.tap("kc", self.kc, [128, NSL, Tc], BF16)
        self.tap("vc", self.vc, [128, NCH, HID], BF16)
        self.tap("wcr", self.wc, [128, NCH, HID], BF16)

    def a2_aux(self):
        T_, V_, S_ = self.T_, self.V_, self.S_
        I = self.I
        for (dst, wn) in ((self.f1c, "Wf1"), (self.G1c, "Wg1")):
            wb = self.wsl[0]
            self.dma("gpsimd", wb.ap[:, :, :],
                     I[wn].rearrange("(a p) c -> p a c", p=128), writes=[wb])
            for half in range(2):
                pb = self.banks[4 + half]
                for a in range(NSL):
                    self.T_(lambda t, a=a, pb=pb, half=half, wb=wb: t.matmul(
                        pb.ap[:, :], wb.ap[:, a, :],
                        self.hTb.ap[:, a, 3 + half * 512:3 + (half + 1) * 512],
                        start=(a == 0), stop=(a == NSL - 1)),
                        reads=[wb, self.hTb], writes=[pb])
                self.V_(lambda v, pb=pb, dst=dst, half=half: v.tensor_copy(
                    dst.ap[:, half * 512:(half + 1) * 512], pb.ap[:, :]),
                    reads=[pb], writes=[dst])
        pb = self.banks[6]
        for half in range(2):
            for a in range(NSL):
                self.T_(lambda t, a=a, half=half, pb=pb: t.matmul(
                    pb.ap[0:16, 0:512], self.Wbb.ap[:, a, :],
                    self.hTb.ap[:, a, 3 + half * 512:3 + (half + 1) * 512],
                    start=(a == 0), stop=(a == NSL - 1)),
                    reads=[self.Wbb, self.hTb], writes=[pb])
            self.V_(lambda v, pb=pb, half=half: v.tensor_copy(
                self.betaR.ap[0:16, half * 512:(half + 1) * 512],
                pb.ap[0:16, 0:512]),
                reads=[pb], writes=[self.betaR])
        for ch in range(NCH):
            pb7 = self.banks[7]
            self.T_(lambda t, ch=ch, pb7=pb7: t.transpose(
                pb7.ap[:, 0:16], self.betaR.ap[0:16, ch * 128:(ch + 1) * 128],
                self.identf.ap[:, :]),
                reads=[self.betaR, self.identf], writes=[pb7])
            self.S_(lambda s, ch=ch, pb7=pb7: s.activation(
                self.beta_tm.ap[:, ch, :], pb7.ap[:, 0:16], ACT.Exp, scale=-1.0),
                reads=[pb7], writes=[self.beta_tm])
            self.V_(lambda v, ch=ch: v.tensor_scalar(
                self.beta_tm.ap[:, ch, :], self.beta_tm.ap[:, ch, :], 1.0, None,
                ALU.add), writes=[self.beta_tm])
            self.V_(lambda v, ch=ch: v.reciprocal(
                self.beta_tm.ap[:, ch, :], self.beta_tm.ap[:, ch, :]),
                writes=[self.beta_tm])
        self.tap("f1c", self.f1c, [128, Tc], BF16)
        self.tap("G1c", self.G1c, [128, Tc], BF16)
        self.tap("beta", self.beta_tm, [128, NCH, H], F32)

    def a3_l2norm(self):
        for ch in range(NCH):
            self.V_(lambda v, ch=ch: v.tensor_mul(
                self.sq2.ap[:, :], self.wc.ap[:, ch, :], self.wc.ap[:, ch, :]),
                reads=[self.wc], writes=[self.sq2])
            self.V_(lambda v: v.tensor_reduce(
                self.rst.ap[:, :], self.sq2.ap[:, :].rearrange(
                    "p (h m) -> p h m", h=H), AX.X, ALU.add),
                reads=[self.sq2], writes=[self.rst])
            self.S_(lambda s: s.activation(self.rst.ap[:, :], self.rst.ap[:, :],
                                           ACT.Ln, bias=self.cpool.ap[:, 0:1]),
                    reads=[self.cpool], writes=[self.rst])
            self.S_(lambda s: s.activation(self.rst.ap[:, :], self.rst.ap[:, :],
                                           ACT.Exp, scale=-0.5), writes=[self.rst])
            self.V_(lambda v, ch=ch: v.tensor_mul(
                self.rst.ap[:, :], self.rst.ap[:, :], self.beta_tm.ap[:, ch, :]),
                reads=[self.beta_tm], writes=[self.rst])
            self.V_(lambda v, ch=ch: v.tensor_tensor(
                self.wc.ap[:, ch, :].rearrange("p (h m) -> p h m", h=H),
                self.wc.ap[:, ch, :].rearrange("p (h m) -> p h m", h=H),
                bc_in(self.rst.ap[:, :], 128), ALU.mult),
                reads=[self.rst], writes=[self.wc])
        self.tap("wcn", self.wc, [128, NCH, HID], BF16)

    def emit_sp(self, sp, ch):
        for j in range(4):
            pb = self.banks[j]
            self.T_(lambda t, j=j, ch=ch, pb=pb: t.matmul(
                pb.ap[:, :], self.f1c.ap[:, ch * 128:(ch + 1) * 128],
                self.Wf2b.ap[:, j * 512:(j + 1) * 512], start=True, stop=True),
                reads=[self.f1c, self.Wf2b], writes=[pb])
            self.S_(lambda s, j=j, pb=pb, sp=sp: s.activation(
                sp.ap[:, j * 512:(j + 1) * 512], pb.ap[:, :], ACT.Exp, scale=-1.0),
                reads=[pb], writes=[sp])
            self.S_(lambda s, j=j, sp=sp: s.activation(
                sp.ap[:, j * 512:(j + 1) * 512], sp.ap[:, j * 512:(j + 1) * 512],
                ACT.Ln, bias=self.cpool.ap[:, 1:2]),
                reads=[self.cpool], writes=[sp])

    def emit_cumexp(self, U, sp, dst_scales):
        for j in range(4):
            pb = self.banks[j]
            self.T_(lambda t, j=j, pb=pb, U=U, sp=sp: t.matmul(
                pb.ap[:, :], U.ap[:, :], sp.ap[:, j * 512:(j + 1) * 512],
                start=True, stop=True), reads=[U, sp], writes=[pb])
            for (dst, sc) in dst_scales:
                self.S_(lambda s, j=j, pb=pb, dst=dst, sc=sc: s.activation(
                    dst.ap[:, j * 512:(j + 1) * 512], pb.ap[:, :], ACT.Exp,
                    scale=sc), reads=[pb], writes=[dst])

    def emit_cols(self, sp, col, dst_ap, bank_i=5):
        pb = self.banks[bank_i]
        for h in range(H):
            self.T_(lambda t, h=h, pb=pb, sp=sp, col=col: t.matmul(
                pb.ap[:, h:h + 1], sp.ap[:, h * 128:(h + 1) * 128], col.ap[:, :],
                start=True, stop=True),
                reads=[sp, col], writes=[pb])
        return self.S_(lambda s, pb=pb, dst_ap=dst_ap: s.activation(
            dst_ap, pb.ap[:, 0:H], ACT.Exp, scale=-1.0),
            reads=[pb, self.ecol], writes=[self.ecol])

    def emit_cols2(self, sp, bank_i=5):
        pb = self.banks[bank_i]
        for h in range(H):
            self.T_(lambda t, h=h, pb=pb, sp=sp: t.matmul(
                pb.ap[:, 2 * h:2 * h + 2], sp.ap[:, h * 128:(h + 1) * 128],
                self.colpair.ap[:, :], start=True, stop=True),
                reads=[sp, self.colpair], writes=[pb])
        ec = self.ecol.ap[:, 0:2 * H].rearrange("p (a b) -> p b a", a=2) \
            if False else None
        pv = pb.ap[:, 0:2 * H].rearrange("p (h w) -> p w h", h=H)
        dv = self.ecol.ap[:, 0:2 * H].rearrange("p (w h) -> p w h", w=2)
        self.S_(lambda s: s.activation(dv, pv, ACT.Exp, scale=-1.0),
                reads=[pb], writes=[self.ecol])

    def emit_diag(self, dst, cols_ap, n=H, reads=()):
        for h in range(n):
            self.V_(lambda v, h=h, dst=dst, cols_ap=cols_ap: v.tensor_scalar(
                dst.ap[:, h, :], self.identb.ap[:, :],
                cols_ap[:, h:h + 1], None, ALU.mult),
                reads=[self.identb] + list(reads),
                writes=[dst])

    def emit_ktm(self, dst, ch, bank0=6):
        for g8 in range(2):
            pbT = self.banks[bank0 + g8]
            for hi in range(8):
                h = g8 * 8 + hi
                self.T_(lambda t, h=h, ch=ch, pbT=pbT, hi=hi: t.transpose(
                    pbT.bf[:, hi * 128:(hi + 1) * 128],
                    self.kc.ap[:, h, ch * 128:(ch + 1) * 128],
                    self.identb.ap[:, :]),
                    reads=[self.kc, self.identb], writes=[pbT])
            self.V_(lambda v, g8=g8, pbT=pbT, dst=dst: v.tensor_copy(
                dst.ap[:, g8 * 8:(g8 + 1) * 8, :],
                pbT.bf[:, 0:1024].rearrange("p (a b) -> p a b", a=8)),
                reads=[pbT], writes=[dst])

    def emit_states(self, ktm, bwdb, diag, ch, bank0=0):
        for g8 in range(2):
            evac_w = []
            for hi in range(8):
                h = g8 * 8 + hi
                bsk = self.banks[bank0 + hi // 4]
                bsv = self.banks[bank0 + 2 + hi // 4]
                sl = (hi % 4) * 128
                self.T_(lambda t, h=h, bsk=bsk, sl=sl, bwdb=bwdb, ktm=ktm:
                        t.matmul(bsk.ap[:, sl:sl + 128],
                                 bwdb.ap[:, h * 128:(h + 1) * 128],
                                 ktm.ap[:, h, :], start=True,
                                 stop=False),
                        reads=[bwdb, ktm], writes=[bsk])
                self.T_(lambda t, h=h, bsk=bsk, sl=sl, diag=diag: t.matmul(
                    bsk.ap[:, sl:sl + 128], diag.ap[:, h, :],
                    self.SkTs.ap[:, h, :], start=False, stop=True),
                    reads=[diag, self.SkTs], writes=[bsk])
                self.T_(lambda t, h=h, bsv=bsv, sl=sl, ch=ch, bwdb=bwdb: t.matmul(
                    bsv.ap[:, sl:sl + 128], bwdb.ap[:, h * 128:(h + 1) * 128],
                    self.vc.ap[:, ch, h * 128:(h + 1) * 128],
                    start=True, stop=False),
                    reads=[bwdb, self.vc], writes=[bsv])
                tok = self.T_(lambda t, h=h, bsv=bsv, sl=sl, diag=diag: t.matmul(
                    bsv.ap[:, sl:sl + 128], diag.ap[:, h, :],
                    self.Svs.ap[:, h, :], start=False, stop=True),
                    reads=[diag, self.Svs], writes=[bsv])
                evac_w.append(tok)
            for bi in range(2):
                bsk = self.banks[bank0 + bi]
                bsv = self.banks[bank0 + 2 + bi]
                hlo = g8 * 8 + bi * 4
                self.V_(lambda v, bsk=bsk, hlo=hlo: v.tensor_copy(
                    self.SkTs.ap[:, hlo:hlo + 4, :],
                    bsk.ap[:, :].rearrange("p (a b) -> p a b", a=4)),
                    reads=[bsk], writes=[self.SkTs], extra=evac_w)
                self.V_(lambda v, bsv=bsv, hlo=hlo: v.tensor_copy(
                    self.Svs.ap[:, hlo:hlo + 4, :],
                    bsv.ap[:, :].rearrange("p (a b) -> p a b", a=4)),
                    reads=[bsv], writes=[self.Svs], extra=evac_w)

    def a4_states(self):
        self.V_(lambda v: v.memset(self.SkTs.ap[:, :, :], 0.0), writes=[self.SkTs])
        self.V_(lambda v: v.memset(self.Svs.ap[:, :, :], 0.0), writes=[self.Svs])
        self.V_(lambda v: v.memset(self.Btn.ap[:, :], 0.0), writes=[self.Btn])
        for ch in range(NCH):
            self.emit_sp(self.spA, ch)
            pb5 = self.banks[5]
            for h in range(H):
                self.T_(lambda t, h=h, pb5=pb5: t.matmul(
                    pb5.ap[:, h:h + 1], self.spA.ap[:, h * 128:(h + 1) * 128],
                    self.onescol.ap[:, :], start=True, stop=True),
                    reads=[self.spA, self.onescol],
                    writes=[pb5])
            self.S_(lambda s, pb5=pb5: s.activation(
                self.ecol.ap[:, 0:H], pb5.ap[:, 0:H], ACT.Exp, scale=-1.0),
                reads=[pb5], writes=[self.ecol])
            self.V_(lambda v, pb5=pb5: v.tensor_add(
                self.Btn.ap[:, :], self.Btn.ap[:, :], pb5.ap[:, 0:H]),
                reads=[pb5], writes=[self.Btn])
            self.emit_cumexp(self.U3n, self.spA, [(self.escrA, -1.0)])
            self.V_(lambda v, ch=ch: v.tensor_mul(
                self.bwdA.ap[:, :], self.wc.ap[:, ch, :], self.escrA.ap[:, :]),
                reads=[self.wc, self.escrA], writes=[self.bwdA])
            self.emit_diag(self.diagA, self.ecol.ap[:, 0:H], reads=[self.ecol])
            self.emit_ktm(self.ktmA, ch)
            self.emit_states(self.ktmA, self.bwdA, self.diagA, ch)
        self.tap("SkT_loc", self.SkTs, [128, H, 128], BF16)
        self.tap("Sv_loc", self.Svs, [128, H, 128], BF16)
        self.tap("Btn", self.Btn, [128, H], F32)

    def a5_combine(self):
        V_, S_, T_ = self.V_, self.S_, self.T_
        pubv = self.pub1.rearrange("p (h two d) -> p h two d", h=H, two=2)
        t_p1a = self.dma_store(pubv[:, :, 0, :], self.SkTs.ap[:, :, :],
                               [self.SkTs])
        t_p1b = self.dma_store(pubv[:, :, 1, :], self.Svs.ap[:, :, :],
                               [self.Svs])
        t_p2 = self.dma_store(self.pub2[:, :], self.Btn.ap[:, :], [self.Btn])
        t_c1 = self.P.emit("gpsimd", lambda g: g.collective_compute(
            "AllGather", ALU.bypass, replica_groups=[list(range(NCORES))],
            ins=[self.pub1[:, :]], outs=[self.agg1[:, :]]),
            [t_p1a, t_p1b], "cs", 1)
        t_c2 = self.P.emit("gpsimd", lambda g: g.collective_compute(
            "AllGather", ALU.bypass, replica_groups=[list(range(NCORES))],
            ins=[self.pub2[:, :]], outs=[self.agg2[:, :]]),
            [t_p2], "cs", 1)
        augv = self.augB.ap.rearrange("p (m h) -> p m h", m=128)
        self.V_(lambda v: v.memset(self.augB.ap[0:9, :], 1.0), writes=[self.augB])
        self.dma("sync", augv[0:8, :, :],
                 self.agg2.rearrange("(c m) h -> c m h", m=128),
                 writes=[self.augB], extra=[t_c2])
        pb = self.banks[4]
        for h in range(H):
            self.T_(lambda t, h=h, pb=pb, augv=augv: t.matmul(
                pb.ap[:, (h % 8) * 8:(h % 8) * 8 + 8],
                augv[0:9, :, h:h + 1],
                self.MmatS.ap[:, :], start=True, stop=True),
                reads=[self.augB, self.MmatS],
                writes=[pb])
            if h % 8 == 7:
                hlo = h - 7
                self.S_(lambda s, pb=pb, hlo=hlo: s.activation(
                    self.WcolS.ap[:, hlo:hlo + 8, :].rearrange("p a b -> p (a b)"),
                    pb.ap[:, 0:64], ACT.Exp, scale=-1.0),
                    reads=[pb], writes=[self.WcolS])
        agg1v = self.agg1.rearrange("(c p) x -> p c x", p=128).rearrange(
            "p c (h d) -> p c h d", h=H)
        for hg in range(4):
            self.dma("sync", self.aggS.ap[:, :, :, :],
                     agg1v[:, :, hg * 4:(hg + 1) * 4, :],
                     writes=[self.aggS], extra=[t_c1])
            for hi in range(4):
                h = hg * 4 + hi
                self.emit_diag(self.diagC, self.WcolS.ap[:, h, :], n=NCORES,
                               reads=[self.WcolS])
                bsk = self.banks[0]
                bsv = self.banks[1]
                for c in range(NCORES):
                    self.T_(lambda t, hi=hi, c=c, bsk=bsk: t.matmul(
                        bsk.ap[:, 0:128], self.diagC.ap[:, c, :],
                        self.aggS.ap[:, c, hi, 0:128], start=(c == 0),
                        stop=(c == NCORES - 1)),
                        reads=[self.diagC, self.aggS],
                        writes=[bsk])
                    self.T_(lambda t, hi=hi, c=c, bsv=bsv: t.matmul(
                        bsv.ap[:, 0:128], self.diagC.ap[:, c, :],
                        self.aggS.ap[:, c, hi, 128:256], start=(c == 0),
                        stop=(c == NCORES - 1)),
                        reads=[self.diagC, self.aggS],
                        writes=[bsv])
                self.V_(lambda v, h=h, bsk=bsk: v.tensor_copy(
                    self.SkTs.ap[:, h, :], bsk.ap[:, 0:128]),
                    reads=[bsk], writes=[self.SkTs])
                self.V_(lambda v, h=h, bsv=bsv: v.tensor_copy(
                    self.Svs.ap[:, h, :], bsv.ap[:, 0:128]),
                    reads=[bsv], writes=[self.Svs])
        self.tap("SkT_in", self.SkTs, [128, H, 128], BF16)
        self.tap("Sv_in", self.Svs, [128, H, 128], BF16)

    def b_scan(self):
        T_, V_, S_ = self.T_, self.V_, self.S_
        for ch in range(NCH):
            self.emit_sp(self.spT, ch)
            self.emit_cumexp(self.U2n, self.spT,
                             [(self.expBp, -1.0), (self.escr, 1.0)])
            self.V_(lambda v, ch=ch: v.tensor_mul(
                self.bwR.ap[:, :], self.wc.ap[:, ch, :], self.escr.ap[:, :]),
                reads=[self.wc, self.escr], writes=[self.bwR])
            self.emit_cumexp(self.U3n, self.spT, [(self.escr, -1.0)])
            self.V_(lambda v, ch=ch: v.tensor_mul(
                self.bwd.ap[:, :], self.wc.ap[:, ch, :], self.escr.ap[:, :]),
                reads=[self.wc, self.escr], writes=[self.bwd])
            self.emit_cols2(self.spT)
            for j in range(4):
                pb = self.banks[j]
                self.T_(lambda t, j=j, ch=ch, pb=pb: t.matmul(
                    pb.ap[:, :], self.G1c.ap[:, ch * 128:(ch + 1) * 128],
                    self.Wg2b.ap[:, j * 512:(j + 1) * 512], start=True,
                    stop=True), reads=[self.G1c, self.Wg2b], writes=[pb])
                self.V_(lambda v, j=j, pb=pb: v.tensor_tensor(
                    self.sgrp.ap[:, :], pb.ap[:, :],
                    self.bg2T.ap[:, j * 512:(j + 1) * 512], ALU.add),
                    reads=[pb, self.bg2T], writes=[self.sgrp])
                self.S_(lambda s: s.activation(self.sgrp.ap[:, :],
                                               self.sgrp.ap[:, :], ACT.Exp,
                                               scale=-1.0), writes=[self.sgrp])
                self.V_(lambda v: v.tensor_scalar(self.sgrp.ap[:, :],
                                                  self.sgrp.ap[:, :], 1.0, None,
                                                  ALU.add), writes=[self.sgrp])
                self.V_(lambda v: v.reciprocal(self.sgrp.ap[:, :],
                                               self.sgrp.ap[:, :]),
                        writes=[self.sgrp])
                self.V_(lambda v, j=j: v.tensor_copy(
                    self.gsig.ap[:, j * 512:(j + 1) * 512], self.sgrp.ap[:, :]),
                    reads=[self.sgrp], writes=[self.gsig])
            self.emit_diag(self.diagB, self.ecol.ap[:, 0:H], reads=[self.ecol])
            for g4 in range(4):
                hlo = g4 * 4
                bPT = self.banks[4]
                bS = self.banks[5]
                bTR = self.banks[6]
                bO = self.banks[7]
                self.emit_diag(self.diagR, self.ecol.ap[:, H + hlo:H + hlo + 4],
                               n=4, reads=[self.ecol])
                for hi in range(4):
                    h = hlo + hi
                    self.T_(lambda t, h=h, hi=hi, bPT=bPT: t.matmul(
                        bPT.ap[:, hi * 128:(hi + 1) * 128],
                        self.diagR.ap[:, hi, :], self.SkTs.ap[:, h, :],
                        start=True, stop=True),
                        reads=[self.diagR, self.SkTs],
                        writes=[bPT])
                self.V_(lambda v, bPT=bPT: v.tensor_copy(
                    self.gSR.ap[:, :, :],
                    bPT.ap[:, :].rearrange("p (a b) -> p a b", a=4)),
                    reads=[bPT], writes=[self.gSR])
                for hi in range(4):
                    h = hlo + hi
                    self.T_(lambda t, h=h, hi=hi, ch=ch, bPT=bPT: t.matmul(
                        bPT.ap[:, hi * 128:(hi + 1) * 128],
                        self.kc.ap[:, h, ch * 128:(ch + 1) * 128],
                        self.qc.ap[:, h, ch * 128:(ch + 1) * 128],
                        start=True, stop=True),
                        reads=[self.kc, self.qc],
                        writes=[bPT])
                self.V_(lambda v, bPT=bPT: v.tensor_tensor(
                    self.gPM.ap[:, :, :],
                    bPT.ap[:, :].rearrange("p (a b) -> p a b", a=4),
                    bc_mid(self.trilS.ap[:, :], 4), ALU.mult),
                    reads=[bPT, self.trilS], writes=[self.gPM])
                for hi in range(4):
                    h = hlo + hi
                    self.T_(lambda t, h=h, hi=hi, bTR=bTR: t.transpose(
                        bTR.bf[:, 512 + hi * 128:512 + (hi + 1) * 128],
                        self.bwR.ap[:, h * 128:(h + 1) * 128],
                        self.identb.ap[:, :]),
                        reads=[self.bwR, self.identb],
                        writes=[bTR])
                self.V_(lambda v, bTR=bTR: v.tensor_copy(
                    self.gRT.ap[:, :, :],
                    bTR.bf[:, 512:1024].rearrange("p (a b) -> p a b", a=4)),
                    reads=[bTR], writes=[self.gRT])
                for hi in range(4):
                    h = hlo + hi
                    self.T_(lambda t, h=h, hi=hi, bS=bS: t.matmul(
                        bS.ap[:, hi * 128:(hi + 1) * 128], self.gPM.ap[:, hi, :],
                        self.bwR.ap[:, h * 128:(h + 1) * 128],
                        start=True, stop=False),
                        reads=[self.gPM, self.bwR],
                        writes=[bS])
                    self.T_(lambda t, h=h, hi=hi, ch=ch, bS=bS: t.matmul(
                        bS.ap[:, hi * 128:(hi + 1) * 128],
                        self.qc.ap[:, h, ch * 128:(ch + 1) * 128],
                        self.gSR.ap[:, hi, :], start=False, stop=True),
                        reads=[self.qc, self.gSR], writes=[bS])
                self.V_(lambda v, bS=bS, hlo=hlo: v.tensor_tensor(
                    self.sgrp.ap[:, :], bS.ap[:, :],
                    self.expBp.ap[:, hlo * 128:(hlo + 4) * 128], ALU.mult),
                    reads=[bS, self.expBp], writes=[self.sgrp])
                self.V_(lambda v: v.tensor_reduce(
                    self.smx.ap[:, 0:4], self.sgrp.ap[:, :].rearrange(
                        "p (a b) -> p a b", a=4), AX.X, ALU.max, negate=True),
                    reads=[self.sgrp], writes=[self.smx])
                for hi in range(4):
                    self.S_(lambda s, hi=hi: s.activation(
                        self.sgrp.ap[:, hi * 128:(hi + 1) * 128],
                        self.sgrp.ap[:, hi * 128:(hi + 1) * 128], ACT.Exp,
                        bias=self.smx.ap[:, hi:hi + 1]),
                        reads=[self.smx], writes=[self.sgrp])
                self.V_(lambda v: v.tensor_reduce(
                    self.smx.ap[:, 4:8], self.sgrp.ap[:, :].rearrange(
                        "p (a b) -> p a b", a=4), AX.X, ALU.add),
                    reads=[self.sgrp], writes=[self.smx])
                self.V_(lambda v: v.reciprocal(self.smx.ap[:, 4:8],
                                               self.smx.ap[:, 4:8]),
                        writes=[self.smx])
                self.V_(lambda v: v.tensor_tensor(
                    self.sgrp.ap[:, :].rearrange("p (a b) -> p a b", a=4),
                    self.sgrp.ap[:, :].rearrange("p (a b) -> p a b", a=4),
                    bc_in(self.smx.ap[:, 4:8], 128), ALU.mult),
                    reads=[self.smx], writes=[self.sgrp])
                self.V_(lambda v, hlo=hlo: v.tensor_tensor(
                    self.gAB.ap[:, :, :].rearrange("p a b -> p (a b)"),
                    self.sgrp.ap[:, :],
                    self.expBp.ap[:, hlo * 128:(hlo + 4) * 128], ALU.mult),
                    reads=[self.sgrp, self.expBp], writes=[self.gAB])
                for hi in range(4):
                    self.T_(lambda t, hi=hi, bTR=bTR: t.transpose(
                        bTR.bf[:, hi * 128:(hi + 1) * 128], self.gAB.ap[:, hi, :],
                        self.identb.ap[:, :]),
                        reads=[self.gAB, self.identb],
                        writes=[bTR])
                self.V_(lambda v, bTR=bTR: v.tensor_copy(
                    self.gAT.ap[:, :, :],
                    bTR.bf[:, 0:512].rearrange("p (a b) -> p a b", a=4)),
                    reads=[bTR], writes=[self.gAT])
                for hi in range(4):
                    self.T_(lambda t, hi=hi, bPT=bPT: t.matmul(
                        bPT.ap[:, hi * 128:(hi + 1) * 128], self.gRT.ap[:, hi, :],
                        self.gAT.ap[:, hi, :], start=True, stop=True),
                        reads=[self.gRT, self.gAT],
                        writes=[bPT])
                self.V_(lambda v, bPT=bPT: v.tensor_tensor(
                    self.gP2.ap[:, :, :],
                    bPT.ap[:, :].rearrange("p (a b) -> p a b", a=4),
                    bc_mid(self.tril1.ap[:, :], 4), ALU.mult),
                    reads=[bPT, self.tril1], writes=[self.gP2])
                self.V_(lambda v, hlo=hlo: v.tensor_tensor(
                    self.svr.ap[:, :, :], self.Svs.ap[:, hlo:hlo + 4, :],
                    bc_in(self.ecol.ap[:, H + hlo:H + hlo + 4], 128), ALU.mult),
                    reads=[self.Svs, self.ecol], writes=[self.svr])
                for hi in range(4):
                    h = hlo + hi
                    self.T_(lambda t, h=h, hi=hi, ch=ch, bO=bO: t.matmul(
                        bO.ap[:, hi * 128:(hi + 1) * 128], self.gP2.ap[:, hi, :],
                        self.vc.ap[:, ch, h * 128:(h + 1) * 128],
                        start=True, stop=False),
                        reads=[self.gP2, self.vc],
                        writes=[bO])
                    self.T_(lambda t, hi=hi, bO=bO: t.matmul(
                        bO.ap[:, hi * 128:(hi + 1) * 128], self.gAT.ap[:, hi, :],
                        self.svr.ap[:, hi, :], start=False, stop=True),
                        reads=[self.gAT, self.svr], writes=[bO])
                self.S_(lambda s, bO=bO: s.activation(
                    self.ogrp.ap[:, :], bO.ap[:, :], ACT.Square),
                    reads=[bO], writes=[self.ogrp])
                self.V_(lambda v: v.tensor_reduce(
                    self.rst.ap[:, 0:4], self.ogrp.ap[:, :].rearrange(
                        "p (a b) -> p a b", a=4), AX.X, ALU.add),
                    reads=[self.ogrp], writes=[self.rst])
                self.S_(lambda s: s.activation(self.rst.ap[:, 0:4],
                                               self.rst.ap[:, 0:4], ACT.Ln,
                                               scale=1.0 / 128.0,
                                               bias=self.cpool.ap[:, 2:3]),
                        reads=[self.cpool], writes=[self.rst])
                self.S_(lambda s: s.activation(self.rst.ap[:, 0:4],
                                               self.rst.ap[:, 0:4], ACT.Exp,
                                               scale=-0.5), writes=[self.rst])
                self.V_(lambda v, bO=bO: v.tensor_tensor(
                    self.ogrp.ap[:, :].rearrange("p (a b) -> p a b", a=4),
                    bO.ap[:, :].rearrange("p (a b) -> p a b", a=4),
                    bc_in(self.rst.ap[:, 0:4], 128), ALU.mult),
                    reads=[bO, self.rst], writes=[self.ogrp])
                self.V_(lambda v, hlo=hlo: v.tensor_tensor(
                    self.gOF.ap[:, :, :].rearrange("p a b -> p (a b)"),
                    self.ogrp.ap[:, :],
                    self.gsig.ap[:, hlo * 128:(hlo + 4) * 128], ALU.mult),
                    reads=[self.ogrp, self.gsig], writes=[self.gOF])
                for hi in range(4):
                    self.T_(lambda t, hi=hi, bTR=bTR: t.transpose(
                        bTR.bf[:, 512 + hi * 128:512 + (hi + 1) * 128],
                        self.gOF.ap[:, hi, :], self.identb.ap[:, :]),
                        reads=[self.gOF, self.identb],
                        writes=[bTR])
                self.V_(lambda v, bTR=bTR: v.tensor_copy(
                    self.oTst.ap[:, :, :],
                    bTR.bf[:, 512:1024].rearrange("p (a b) -> p a b", a=4)),
                    reads=[bTR], writes=[self.oTst])
                self.dma_store(
                    self.o_scr.rearrange("(s p) t -> p s t", p=128)[
                        :, hlo:hlo + 4, ch * 128:(ch + 1) * 128],
                    self.oTst.ap[:, :, :], [self.oTst])
            self.emit_ktm(self.ktm, ch, bank0=4)
            self.emit_states(self.ktm, self.bwd, self.diagB, ch, bank0=4)

    def b3_wo_ln1(self):
        T_, V_, S_ = self.T_, self.V_, self.S_
        I = self.I
        self.emit_warmup()
        st_toks = [(f"st{i}", self.P.cnt[f"st{i}"])
                   for i in range(self.P.LANES["st"]) if self.P.cnt[f"st{i}"]]
        self.dma("sync", self.oT.ap[:, :, :],
                 self.o_scr.rearrange("(s p) t -> p s t", p=128),
                 writes=[self.oT], extra=st_toks)
        for band in range(2):
            for i in range(8):
                sl = band * 8 + i
                wb = self.wo_b[i]
                self.dma("gpsimd", wb.ap[:, :],
                         I["Wo"][sl * 128:(sl + 1) * 128, :], writes=[wb])
                self.V_(lambda v, wb=wb: v.tensor_scalar(
                    wb.ap[:, :], wb.ap[:, :], self.nwcol.ap[:, 0:1], None,
                    ALU.mult), reads=[self.nwcol], writes=[wb])
            for tc in range(NCH):
                par = tc % 2
                if band == 0:
                    xb = self.xs[0]
                    self.dma("sync", xb.ap[:, :],
                             I["h_nat"][tc * 128:(tc + 1) * 128, :], writes=[xb])
                for j in range(4):
                    pb = self.banks[par * 4 + j]
                    for i in range(8):
                        sl = band * 8 + i
                        wb = self.wo_b[i]
                        self.T_(lambda t, i=i, j=j, tc=tc, sl=sl, pb=pb, wb=wb:
                                t.matmul(pb.ap[:, :],
                                         self.oT.ap[:, sl, tc * 128:(tc + 1) * 128],
                                         wb.ap[:, j * 512:(j + 1) * 512],
                                         start=(i == 0), stop=(i == 7)),
                                reads=[self.oT, wb],
                                writes=[pb])
                for j in range(4):
                    pb = self.banks[par * 4 + j]
                    if band == 0:
                        xb = self.xs[0]
                        self.V_(lambda v, j=j, tc=tc, pb=pb, xb=xb: v.tensor_tensor(
                            self.r1a.ap[:, tc, j * 512:(j + 1) * 512], pb.ap[:, :],
                            xb.ap[:, j * 512:(j + 1) * 512], ALU.add),
                            reads=[pb, xb], writes=[self.r1a])
                    else:
                        self.V_(lambda v, j=j, tc=tc, pb=pb: v.tensor_tensor(
                            self.r1a.ap[:, tc, j * 512:(j + 1) * 512],
                            self.r1a.ap[:, tc, j * 512:(j + 1) * 512],
                            pb.ap[:, :], ALU.add),
                            reads=[pb], writes=[self.r1a])
        for tc in range(NCH):
            self.emit_ln(tc, to_xT=True)
        self.tap("x1", self.r1a, [128, NCH, HID], F32)

    def emit_ln(self, tc, to_xT):
        V_, S_, T_ = self.V_, self.S_, self.T_
        acc = self.r1a
        sq = self.sqscr if to_xT else self.sqscr2
        self.V_(lambda v, tc=tc: v.tensor_reduce(
            self.lnc.ap[:, 0:1], acc.ap[:, tc, :], AX.X, ALU.add),
            reads=[acc], writes=[self.lnc])
        self.V_(lambda v: v.tensor_scalar(self.lnc.ap[:, 0:1], self.lnc.ap[:, 0:1],
                                          1.0 / HID, None, ALU.mult),
                writes=[self.lnc])
        self.S_(lambda s, tc=tc, sq=sq: s.activation(
            sq.ap[:, :], acc.ap[:, tc, :], ACT.Square,
            accum_out=self.lnc.ap[:, 1:2]),
            reads=[acc], writes=[sq, self.lnc])
        self.V_(lambda v: v.tensor_scalar(self.lnc.ap[:, 1:2], self.lnc.ap[:, 1:2],
                                          1.0 / HID, None, ALU.mult),
                writes=[self.lnc])
        self.V_(lambda v: v.tensor_mul(self.lnc.ap[:, 2:3], self.lnc.ap[:, 0:1],
                                       self.lnc.ap[:, 0:1]), writes=[self.lnc])
        self.V_(lambda v: v.tensor_sub(self.lnc.ap[:, 1:2], self.lnc.ap[:, 1:2],
                                       self.lnc.ap[:, 2:3]), writes=[self.lnc])
        self.S_(lambda s: s.activation(self.lnc.ap[:, 3:4], self.lnc.ap[:, 1:2],
                                       ACT.Ln, bias=self.cpool.ap[:, 2:3]),
                reads=[self.cpool], writes=[self.lnc])
        self.S_(lambda s: s.activation(self.lnc.ap[:, 3:4], self.lnc.ap[:, 3:4],
                                       ACT.Exp, scale=-0.5), writes=[self.lnc])
        self.V_(lambda v, tc=tc: v.tensor_scalar(
            acc.ap[:, tc, :], acc.ap[:, tc, :], self.lnc.ap[:, 0:1], None,
            ALU.subtract), reads=[self.lnc], writes=[acc])
        self.V_(lambda v, tc=tc: v.tensor_scalar(
            acc.ap[:, tc, :], acc.ap[:, tc, :], self.lnc.ap[:, 3:4], None,
            ALU.mult), reads=[self.lnc], writes=[acc])
        if not to_xT:
            return
        self.V_(lambda v, tc=tc, sq=sq: v.tensor_copy(sq.ap[:, :],
                                                      acc.ap[:, tc, :]),
                reads=[acc], writes=[sq])
        for g8 in range(2):
            pbT = self.banks[4 + g8]
            for hi in range(8):
                sl = g8 * 8 + hi
                self.T_(lambda t, sl=sl, hi=hi, pbT=pbT, sq=sq: t.transpose(
                    pbT.bf[:, hi * 128:(hi + 1) * 128],
                    sq.ap[:, sl * 128:(sl + 1) * 128], self.identb.ap[:, :]),
                    reads=[sq, self.identb], writes=[pbT])
            self.V_(lambda v, g8=g8, tc=tc, pbT=pbT: v.tensor_copy(
                self.xT.ap[:, g8 * 8:(g8 + 1) * 8, tc * 128:(tc + 1) * 128],
                pbT.bf[:, 0:1024].rearrange("p (a b) -> p a b", a=8)),
                reads=[pbT], writes=[self.xT])

    def b4_ffn(self):
        T_, V_, S_ = self.T_, self.V_, self.S_
        I = self.I

        def emit_gateup(hb, hs):
            hidx = hb * 8 + hs
            par = hs % 2
            fg, fu = self.ffg[par], self.ffu[par]
            gbb = self.gb[hb % 2]
            self.dma("gpsimd", fg.ap[:, :, :],
                     I["wgT"][:, hidx * 128:(hidx + 1) * 128].rearrange(
                         "(a p) c -> p a c", p=128), writes=[fg])
            self.dma("gpsimd", fu.ap[:, :, :],
                     I["wuT"][:, hidx * 128:(hidx + 1) * 128].rearrange(
                         "(a p) c -> p a c", p=128), writes=[fu])
            bg = self.banks[4 + par]
            bu = self.banks[6 + par]
            for half in range(2):
                for (bk, fw) in ((bg, fg), (bu, fu)):
                    for a in range(NSL):
                        self.T_(lambda t, a=a, half=half, bk=bk, fw=fw: t.matmul(
                            bk.ap[:, :], fw.ap[:, a, :],
                            self.xT.ap[:, a, half * 512:(half + 1) * 512],
                            start=(a == 0), stop=(a == NSL - 1)),
                            reads=[fw, self.xT], writes=[bk])
                self.S_(lambda s, bg=bg, half=half: s.activation(
                    self.sgf.ap[:, half * 512:(half + 1) * 512], bg.ap[:, :],
                    ACT.Silu), reads=[bg], writes=[self.sgf])
                self.V_(lambda v, bu=bu, hs=hs, half=half, gbb=gbb:
                        v.tensor_tensor(
                    gbb.ap[:, hs, half * 512:(half + 1) * 512], bu.ap[:, :],
                    self.sgf.ap[:, half * 512:(half + 1) * 512], ALU.mult),
                    reads=[bu, self.sgf], writes=[gbb])

        def emit_down_unit(hb, unit):
            tc, jh = unit // 2, unit % 2
            pair = unit % 2
            gbb = self.gb[hb % 2]
            for jj in range(2):
                j = jh * 2 + jj
                pb = self.banks[pair * 2 + jj]
                for i in range(8):
                    wd = self.wd_all[i]
                    self.T_(lambda t, i=i, j=j, tc=tc, pb=pb, wd=wd, gbb=gbb:
                            t.matmul(
                        pb.ap[:, :], gbb.ap[:, i, tc * 128:(tc + 1) * 128],
                        wd.ap[:, j * 512:(j + 1) * 512],
                        start=(i == 0), stop=(i == 7)),
                        reads=[gbb, wd], writes=[pb])
            for jj in range(2):
                j = jh * 2 + jj
                pb = self.banks[pair * 2 + jj]
                self.V_(lambda v, j=j, tc=tc, pb=pb: v.tensor_tensor(
                    self.r1a.ap[:, tc, j * 512:(j + 1) * 512],
                    self.r1a.ap[:, tc, j * 512:(j + 1) * 512],
                    pb.ap[:, :], ALU.add),
                    reads=[pb], writes=[self.r1a])

        def emit_wd_loads(hb):
            for i in range(8):
                hidx = hb * 8 + i
                wd = self.wd_all[i]
                self.dma("gpsimd", wd.ap[:, :],
                         I["wdT"][hidx * 128:(hidx + 1) * 128, :], writes=[wd])

        self.emit_warmup()
        emit_wd_loads(0)
        for hs in range(8):
            emit_gateup(0, hs)
        for hb in range(1, 8):
            for hs in range(8):
                emit_down_unit(hb - 1, 2 * hs)
                emit_down_unit(hb - 1, 2 * hs + 1)
                emit_gateup(hb, hs)
            emit_wd_loads(hb)
        for unit in range(16):
            emit_down_unit(7, unit)
        for tc in range(NCH):
            self.emit_ln(tc, to_xT=False)
            self.dma_store(self.out[tc * 128:(tc + 1) * 128, :],
                           self.r1a.ap[:, tc, :], [self.r1a])

    def finish(self):
        toks = [(f"st{i}", self.P.cnt[f"st{i}"])
                for i in range(self.P.LANES["st"]) if self.P.cnt[f"st{i}"]]
        if toks:
            self.P.emit("gpsimd", lambda g: g.engine_nop(), toks, "gs", 1)


def build_host_inputs(inputs):
    h = np.ascontiguousarray(np.asarray(inputs["L0_attn_hidden_states"],
                                        np.float32)[0])
    hT_full = np.ascontiguousarray(h.T)
    per_core = []
    for c in range(NCORES):
        lo = c * Tc
        hT_ext = np.zeros((HID, Tc + 3), np.float32)
        if c == 0:
            hT_ext[:, 3:] = hT_full[:, 0:Tc]
        else:
            hT_ext[:, :] = hT_full[:, lo - 3:lo + Tc]
        Mm = np.zeros((9, 8), np.float32)
        for cp in range(8):
            if cp < c:
                for cm in range(cp + 1, c):
                    Mm[cm, cp] = 1.0
            else:
                Mm[8, cp] = 1e5
        per_core.append({
            "hT_ext": hT_ext,
            "h_nat": np.ascontiguousarray(h[lo:lo + Tc]),
            "Mmat": Mm,
        })
    f32 = lambda k: np.ascontiguousarray(np.asarray(inputs[k], np.float32))
    shared = {
        "Wq": f32("L0_attn_Wq"), "Wk": f32("L0_attn_Wk"),
        "Wv": f32("L0_attn_Wv"), "Ww": f32("L0_attn_Ww"),
        "Wo": f32("L0_attn_Wo"),
        "cq": f32("L0_attn_cq"), "ck": f32("L0_attn_ck"),
        "cv": f32("L0_attn_cv"),
        "Wf1": f32("L0_attn_Wf1"), "Wf2": f32("L0_attn_Wf2"),
        "Wb": f32("L0_attn_Wb"),
        "Wg1": f32("L0_attn_Wg1"), "Wg2": f32("L0_attn_Wg2"),
        "bg2": f32("L0_attn_bg2"), "normw": f32("L0_attn_norm_w"),
        "wgT": np.ascontiguousarray(np.asarray(inputs["L0_ffn_w_gate"],
                                               np.float32).T),
        "wuT": np.ascontiguousarray(np.asarray(inputs["L0_ffn_w_up"],
                                               np.float32).T),
        "wdT": np.ascontiguousarray(np.asarray(inputs["L0_ffn_w_down"],
                                               np.float32).T),
    }
    return [{**shared, **pc} for pc in per_core]


_BUILD_CACHE = {}


def kernel(**inputs):
    in_maps = build_host_inputs(inputs)
    key = (STAGES, WANT_DBG)
    if key not in _BUILD_CACHE:
        b = Builder()
        nc = b.build()
        _BUILD_CACHE[key] = (nc, b)
    nc, b = _BUILD_CACHE[key]
    res = run_bass_kernel_spmd(nc, in_maps, core_ids=list(range(NCORES)))
    out = np.concatenate([r["out"] for r in res.results], axis=0)
    return out[None]


# version 09
# speedup vs baseline: 1.0167x; 1.0021x over previous
import numpy as np
import ml_dtypes
import concourse.bass as bass
import concourse.mybir as mybir
from concourse.bass_utils import run_bass_kernel_spmd

F32 = mybir.dt.float32
BF16 = mybir.dt.bfloat16
U8 = mybir.dt.uint8
AX = mybir.AxisListType
ALU = mybir.AluOpType
ACT = mybir.ActivationFunctionType

NCORES = 8
T = 8192
HID = 2048
Tc = T // NCORES
C = 128
NCH = Tc // C
H = 16
NSL = HID // 128
DH = 8192
SCALE = float(128) ** -0.5

ENGINES = ("sync", "tensor", "vector", "scalar", "gpsimd")
STAGES = "all"
WANT_DBG = ()


class Buf:
    __slots__ = ("ap", "w", "r", "rng", "fresh")

    def __init__(self, ap, rng=None):
        self.ap = ap
        self.w = None
        self.r = {}
        self.rng = rng
        self.fresh = True

    def add_r(self, tok):
        s, v = tok
        if self.r.get(s, -1) < v:
            self.r[s] = v

    def r_toks(self):
        return [(s, v) for s, v in self.r.items()]

    def __getitem__(self, key):
        return self.ap[key]


class Bank(Buf):
    __slots__ = ("bf",)

    def __init__(self, ap):
        super().__init__(ap)
        self.bf = ap.bitcast(BF16)


class Plan:
    LANES = {"ld": 4, "lw": 6, "st": 4}

    def __init__(self, nc):
        self.nc = nc
        self.plan = []
        self.sems = {}
        self.cnt = {}
        for s in ("ts", "vs", "ss", "gs", "cs"):
            self.sems[s] = nc.alloc_semaphore(s)
            self.cnt[s] = 0
        self.rr = {}
        for q, n in self.LANES.items():
            self.rr[q] = 0
            for i in range(n):
                name = f"{q}{i}"
                self.sems[name] = nc.alloc_semaphore(name)
                self.cnt[name] = 0

    def lane(self, q):
        i = self.rr[q] % self.LANES[q]
        self.rr[q] += 1
        name = f"{q}{i}"
        pre = (name, self.cnt[name]) if self.cnt[name] else None
        return name, pre

    def emit(self, eng, fn, waits, inc, inc_by):
        tok = None
        if inc is not None:
            self.cnt[inc] += inc_by
            tok = (inc, self.cnt[inc])
        self.plan.append((eng, fn, tuple(w for w in waits if w is not None),
                          (inc, inc_by) if inc else None))
        return tok

    def realize(self, block):
        wm = {e: {} for e in ENGINES}
        waited = {}
        for (_, _, waits, _) in self.plan:
            for (s, v) in waits:
                waited.setdefault(s, set()).add(v)
        cum = {s: 0 for s in self.cnt}
        pend = {s: 0 for s in self.cnt}
        plan2 = []
        for (e, fn, waits, inc) in self.plan:
            inc2 = None
            if inc is not None:
                s, by = inc
                cum[s] += by
                pend[s] += by
                if cum[s] in waited.get(s, ()):
                    inc2 = (s, pend[s])
                    pend[s] = 0
            plan2.append((e, fn, waits, inc2))

        def mk(eng_name):
            def body(eng):
                for (e, fn, waits, inc) in plan2:
                    if e != eng_name:
                        continue
                    for (s, v) in waits:
                        if wm[e].get(s, 0) < v:
                            eng.wait_ge(self.sems[s], v)
                            wm[e][s] = v
                    ins = fn(eng)
                    if inc is not None:
                        s, by = inc
                        ins.then_inc(self.sems[s], by)
            return body

        block.sync(mk("sync"))
        block.tensor(mk("tensor"))
        block.vector(mk("vector"))
        block.scalar(mk("scalar"))
        block.gpsimd(mk("gpsimd"))


def bc_in(ap, n):
    return bass.AP(ap.tensor, ap.offset, [list(p) for p in ap.ap] + [[0, n]])


def bc_mid(ap, n):
    pr = [list(p) for p in ap.ap]
    return bass.AP(ap.tensor, ap.offset, [pr[0], [0, n]] + pr[1:])


class Builder:
    SEM_OF = {"sync": "ld", "gpsimd_dma": "lw", "tensor": "ts", "vector": "vs",
              "scalar": "ss", "gpsimd": "gs"}

    def __init__(self):
        nc = bass.Bass("TRN2", target_bir_lowering=False, debug=False)
        self.nc = nc
        self.P = Plan(nc)
        self.dbg = {}

        I = {}
        def inp(name, shape, dtype=F32):
            I[name] = nc.dram_tensor(name, shape, dtype, kind="ExternalInput").ap()
        self.I = I
        inp("hT_ext", [HID, Tc + 3]); inp("h_nat", [Tc, HID])
        for w in ("Wq", "Wk", "Wv", "Ww", "Wo"):
            inp(w, [HID, HID])
        for w in ("cq", "ck", "cv"):
            inp(w, [HID, 4])
        inp("Wf1", [HID, 128]); inp("Wf2", [128, HID]); inp("Wb", [HID, H])
        inp("Wg1", [HID, 128]); inp("Wg2", [128, HID])
        inp("bg2", [HID]); inp("normw", [128])
        inp("wgT", [HID, DH]); inp("wuT", [HID, DH]); inp("wdT", [DH, HID])
        inp("Mmat", [9, 8])
        self.out = nc.dram_tensor("out", [Tc, HID], F32, kind="ExternalOutput").ap()

        self.pub1 = nc.dram_tensor("pub1", [128, H * 256], BF16).ap()
        self.pub2 = nc.dram_tensor("pub2", [128, H], F32).ap()
        self.agg1 = nc.dram_tensor("agg1", [NCORES * 128, H * 256], BF16).ap()
        self.agg2 = nc.dram_tensor("agg2", [NCORES * 128, H], F32).ap()
        self.o_scr = nc.dram_tensor("o_scr", [HID, Tc], BF16).ap()

        def sb(name, shape, dtype):
            return Buf(nc.alloc_sbuf_tensor(name, shape, dtype).ap())
        self.trilS = sb("trilS", [128, 128], F32)
        self.tril1 = sb("tril1", [128, 128], BF16)
        self.identb = sb("identb", [128, 128], BF16)
        self.identf = sb("identf", [16, 16], F32)
        self.U2n = sb("U2n", [128, 128], F32)
        self.U3n = sb("U3n", [128, 128], F32)
        self.onescol = sb("onescol", [128, 1], F32)
        self.halfcol = sb("halfcol", [128, 1], F32)
        self.colpair = sb("colpair", [128, 2], F32)
        self.ones1p = sb("ones1p", [1, 128], F32)
        self.Wf2b = sb("Wf2b", [128, HID], BF16)
        self.Wg2b = sb("Wg2b", [128, HID], BF16)
        self.bg2T = sb("bg2T", [128, HID], BF16)
        self.nwcol = sb("nwcol", [128, 1], F32)
        self.MmatS = sb("MmatS", [9, 8], F32)
        self.f1c = sb("f1c", [128, Tc], BF16)
        self.G1c = sb("G1c", [128, Tc], BF16)
        self.SkTs = sb("SkTs", [128, H, 128], BF16)
        self.Svs = sb("Svs", [128, H, 128], BF16)
        self.Btn = sb("Btn", [128, H], F32)
        self.ecol = sb("ecol", [128, 2 * H], F32)
        self.beta_tm = sb("beta_tm", [128, NCH, H], F32)
        self.smx = sb("smx", [128, 16], F32)
        self.rst = sb("rst", [128, 16], F32)
        self.lnc = sb("lnc", [128, 8], F32)
        self.cpool = sb("cpool", [128, 4], F32)

        self.alias_reg = []
        self.AR = 184320
        self.arena = nc.alloc_sbuf_tensor("arena", [128, self.AR], U8)
        self.sbuf_slack = nc.sbuf_bytes_remaining
        assert self.sbuf_slack >= 0, f"sbuf over by {-self.sbuf_slack}"

        a = self.ab
        self.kc = a(0, [NSL, Tc], BF16)
        self.vc = a(32768, [NCH, HID], BF16)
        self.wc = a(65536, [NCH, HID], BF16)
        self.qc = a(98304, [NSL, Tc], BF16)
        self.hTb = a(131072, [NSL, Tc + 3], BF16)
        self.wsl = [a(164096, [NSL, 128], BF16), a(168192, [NSL, 128], BF16)]
        self.cvo = [a(172288, [264], F32), a(173376, [264], F32)]
        self.svo = [a(174464, [264], BF16), a(175040, [264], BF16)]
        self.betaR = a(175616, [1024], F32)
        self.bgrow = a(0, [HID], F32)
        self.Uh = a(8192, [128], F32)
        self.cwt = a(179712, [NSL, 16], F32)
        self.Wbb = a(180736, [NSL, H], BF16)
        self.sq2 = a(131072, [HID], BF16)
        self.spA = a(135168, [HID], F32)
        self.escrA = a(143360, [HID], BF16)
        self.bwdA = a(147456, [HID], BF16)
        self.ktmA = a(151552, [H, 128], BF16)
        self.diagA = a(155648, [H, 128], BF16)
        self.aggS = a(131072, [NCORES, 4, 256], BF16)
        self.augB = a(139264, [HID], F32)
        self.WcolS = a(147456, [H, 8], F32)
        self.diagC = a(155648, [NCORES, 128], BF16)
        self.spT = a(131072, [HID], F32)
        self.expBp = a(139264, [HID], BF16)
        self.escr = a(143360, [HID], BF16)
        self.bwR = a(147456, [HID], BF16)
        self.bwd = a(151552, [HID], BF16)
        self.gsig = a(155648, [HID], BF16)
        self.diagB = a(159744, [H, 128], BF16)
        self.ktm = a(163840, [H, 128], BF16)
        self.oTst = a(167936, [4, 128], BF16)
        self.gPM = a(168960, [4, 128], BF16)
        self.gRT = a(169984, [4, 128], BF16)
        self.gSR = a(171008, [4, 128], BF16)
        self.gAB = a(172032, [4, 128], BF16)
        self.gAT = a(173056, [4, 128], BF16)
        self.gP2 = a(174080, [4, 128], BF16)
        self.gOF = a(175104, [4, 128], BF16)
        self.sgrp = a(176128, [512], F32)
        self.ogrp = a(178176, [512], F32)
        self.svr = a(180224, [4, 128], BF16)
        self.diagR = a(181248, [4, 128], BF16)
        self.oT = a(0, [NSL, Tc], BF16)
        self.r1a = a(32768, [NCH, HID], F32)
        self.xT = a(98304, [NSL, Tc], BF16)
        self.wo_b = [a(131072 + i * 4096, [HID], BF16) for i in range(8)]
        self.sqscr = a(163840, [HID], BF16)
        self.xs = [a(167936, [HID], F32)]
        self.gb = [a(131072, [8, Tc], BF16), a(147456, [8, Tc], BF16)]
        self.ffg = [a(0, [NSL, 128], BF16), a(4096, [NSL, 128], BF16)]
        self.ffu = [a(8192, [NSL, 128], BF16), a(12288, [NSL, 128], BF16)]
        self.sgf = a(16384, [Tc], F32)
        self.wd_all = ([a(20480 + i * 4096, [HID], BF16) for i in range(3)] +
                       [a(163840 + i * 4096, [HID], BF16) for i in range(5)])
        self.sqscr2 = a(131072, [HID], BF16)

        self.banks = [Bank(nc.alloc_psum_tensor(f"pb{i}", [128, 512], F32).ap())
                      for i in range(8)]

    def ab(self, off, shape, dtype):
        esz = mybir.dt.size(dtype)
        n = int(np.prod(shape))
        assert off + n * esz <= self.AR, f"arena overflow at {off}+{n*esz}"
        ap = self.arena.ap()[:, off:off + n * esz].bitcast(dtype)
        if len(shape) > 1:
            dims = " ".join(f"d{i}" for i in range(len(shape)))
            kw = {f"d{i}": int(s) for i, s in enumerate(shape)}
            ap = ap.rearrange(f"p ({dims}) -> p {dims}", **kw)
        b = Buf(ap, rng=(off, off + n * esz))
        self.alias_reg.append(b)
        return b

    def _write_waits(self, b, waits):
        if b.w is not None:
            waits.append(b.w)
        waits.extend(b.r_toks())
        if b.fresh and b.rng is not None:
            lo, hi = b.rng
            for o in self.alias_reg:
                if o is b or o.rng is None:
                    continue
                if o.rng[0] < hi and lo < o.rng[1]:
                    if o.w is not None:
                        waits.append(o.w)
                    waits.extend(o.r_toks())
            b.fresh = False

    def op(self, eng, fn, reads=(), writes=(), extra=(), inc=None, inc_by=1):
        if inc is None:
            inc = self.SEM_OF[eng]
        waits = list(extra)
        for b in reads:
            if b.w is not None:
                waits.append(b.w)
        for b in writes:
            self._write_waits(b, waits)
        tok = self.P.emit(eng, fn, waits, inc, inc_by)
        for b in reads:
            b.add_r(tok)
        for b in writes:
            b.w = tok
            b.r = {}
        return tok

    def dma(self, eng, out_ap, in_ap, reads=(), writes=(), extra=(),
            slow_ok=False):
        q = "lw" if eng == "gpsimd" else "ld"
        sem, pre = self.P.lane(q)
        waits = list(extra)
        if pre is not None:
            waits.append(pre)
        for b in reads:
            if b.w is not None:
                waits.append(b.w)
        for b in writes:
            self._write_waits(b, waits)
        tok = self.P.emit(eng, lambda e: e.dma_start(
            out=out_ap, in_=in_ap, allow_slow_non_contiguous=slow_ok),
            waits, sem, 16)
        for b in reads:
            b.add_r(tok)
        for b in writes:
            b.w = tok
            b.r = {}
        return tok

    def dma_store(self, out_ap, in_ap, bufs, extra=(), eng="sync"):
        sem, pre = self.P.lane("st")
        waits = list(extra)
        if pre is not None:
            waits.append(pre)
        for b in bufs:
            if b.w is not None:
                waits.append(b.w)
        tok = self.P.emit(eng, lambda e: e.dma_start(out=out_ap, in_=in_ap),
                          waits, sem, 16)
        for b in bufs:
            b.add_r(tok)
        return tok

    def T_(self, fn, **kw):
        return self.op("tensor", fn, **kw)

    def V_(self, fn, **kw):
        return self.op("vector", fn, **kw)

    def S_(self, fn, **kw):
        return self.op("scalar", fn, **kw)

    def G_(self, fn, **kw):
        return self.op("gpsimd", fn, **kw)

    def emit_warmup(self, n=30):
        bk = self.banks[4]
        for i in range(n):
            self.T_(lambda t, bk=bk: t.matmul(
                bk.ap[:, :], self.identb.ap[:, :],
                self.bg2T.ap[:, 0:512], start=True, stop=True),
                reads=[self.identb, self.bg2T],
                writes=[bk] if i == 0 else (), extra=() if i == 0 else [bk.w])

    def tap(self, name, buf, shape, dtype):
        if name not in WANT_DBG:
            return
        d = self.nc.dram_tensor("dbg_" + name, shape, dtype,
                                kind="ExternalOutput").ap()
        self.dbg[name] = d
        self.dma_store(d[:, :] if len(shape) == 2 else d, buf.ap, [buf])

    def build(self):
        self.p0_constants()
        self.a1_projections()
        self.a2_aux()
        if STAGES != "A1":
            self.a3_l2norm()
            self.a4_states()
        if STAGES in ("A5", "B", "B3", "all"):
            self.a5_combine()
        if STAGES in ("B", "B3", "all"):
            self.b_scan()
        if STAGES in ("B3", "all"):
            self.b3_wo_ln1()
        if STAGES == "all":
            self.b4_ffn()
        self.finish()
        with self.nc.Block() as block:
            self.P.realize(block)
        return self.nc

    def p0_constants(self):
        G_, V_, S_, T_ = self.G_, self.V_, self.S_, self.T_
        I = self.I

        def mask(buf, fill_first, pat, cmp, fill, cmul, base=0):
            G_(lambda g: g.memset(buf.ap[:, :], fill_first), writes=[buf])
            G_(lambda g: g.affine_select(buf.ap[:, :], buf.ap[:, :], pattern=pat,
                                         compare_op=cmp, fill=fill,
                                         channel_multiplier=cmul, base=base),
               writes=[buf])

        mask(self.trilS, SCALE, [[1, 128]], ALU.is_ge, 0.0, -1)
        mask(self.tril1, 1.0, [[1, 128]], ALU.is_ge, 0.0, -1)
        for t_ in (self.identb,):
            mask(t_, 0.0, [[-1, 128]], ALU.not_equal, 1.0, 1)
        mask(self.identf, 0.0, [[-1, 16]], ALU.not_equal, 1.0, 1)
        mask(self.U2n, 1.0, [[1, 128]], ALU.is_ge, 0.0, -1)
        mask(self.Uh, 1.0, [[0, 128]], ALU.is_ge, 0.0, -1, base=63)
        self.V_(lambda v: v.tensor_sub(self.U2n.ap[:, :], self.U2n.ap[:, :],
                                       self.Uh.ap[:, :]),
                reads=[self.Uh], writes=[self.U2n])
        mask(self.U3n, 1.0, [[-1, 128]], ALU.is_ge, 0.0, 1, base=-1)
        G_(lambda g: g.memset(self.onescol.ap[:, :], 1.0), writes=[self.onescol])
        mask(self.halfcol, 1.0, [[0, 1]], ALU.is_ge, 0.0, -1, base=63)
        G_(lambda g: g.memset(self.ones1p.ap[:, :], 1.0), writes=[self.ones1p])
        self.V_(lambda v: v.tensor_copy(self.colpair.ap[:, 0:1],
                                        self.onescol.ap[:, :]),
                reads=[self.onescol], writes=[self.colpair])
        self.V_(lambda v: v.tensor_copy(self.colpair.ap[:, 1:2],
                                        self.halfcol.ap[:, :]),
                reads=[self.halfcol], writes=[self.colpair])
        for i, val in enumerate((1e-6, 1.0, 1e-5)):
            G_(lambda g, i=i, val=val: g.memset(self.cpool.ap[:, i:i + 1], val),
               writes=[self.cpool])

        for i, cwn in enumerate(("cq", "ck", "cv", "cv")):
            self.dma("sync", self.cwt.ap[:, :, 4 * i:4 * i + 4],
                     I[cwn].rearrange("(s p) t -> p s t", p=128),
                     writes=[self.cwt])
        self.dma("gpsimd", self.Wf2b.ap[:, :], I["Wf2"][:, :], writes=[self.Wf2b])
        self.dma("gpsimd", self.Wg2b.ap[:, :], I["Wg2"][:, :], writes=[self.Wg2b])
        self.dma("gpsimd", self.Wbb.ap[:, :, :],
                 I["Wb"].rearrange("(s p) h -> p s h", p=128), writes=[self.Wbb])
        self.dma("sync", self.nwcol.ap[:, :],
                 I["normw"].rearrange("(p o) -> p o", o=1), writes=[self.nwcol])
        self.dma("sync", self.MmatS.ap[:, :], I["Mmat"][:, :], writes=[self.MmatS])
        self.dma("sync", self.bgrow.ap[0:1, :],
                 I["bg2"].rearrange("(o f) -> o f", o=1), writes=[self.bgrow])
        for j in range(4):
            bk = self.banks[4 + j]
            self.T_(lambda t, j=j, bk=bk: t.matmul(
                bk.ap[:, :], self.ones1p.ap[:, :],
                self.bgrow.ap[0:1, j * 512:(j + 1) * 512], start=True, stop=True),
                reads=[self.ones1p, self.bgrow], writes=[bk])
            self.V_(lambda v, j=j, bk=bk: v.tensor_copy(
                self.bg2T.ap[:, j * 512:(j + 1) * 512], bk.ap[:, :]),
                reads=[bk], writes=[self.bg2T])
        self.dma("gpsimd", self.hTb.ap[:, :, :],
                 I["hT_ext"].rearrange("(s p) t -> p s t", p=128),
                 writes=[self.hTb])

    def a1_projections(self):
        T_, V_, S_ = self.T_, self.V_, self.S_
        I = self.I
        streams = [("q", "Wq", 0), ("k", "Wk", 1), ("v", "Wv", 2), ("w", "Ww", 3)]
        for sname, wn, sidx in streams:
            for fs in range(NSL):
                par = (sidx * NSL + fs) % 2
                wb = self.wsl[par]
                self.dma("gpsimd", wb.ap[:, :, :],
                         I[wn][:, fs * 128:(fs + 1) * 128].rearrange(
                             "(a p) c -> p a c", p=128), writes=[wb])
                for g in range(4):
                    t0 = g * 256
                    pb = self.banks[g]
                    for a in range(NSL):
                        self.T_(lambda t, a=a, pb=pb, t0=t0, wb=wb: t.matmul(
                            pb.ap[:, 0:259], wb.ap[:, a, :],
                            self.hTb.ap[:, a, t0:t0 + 259],
                            start=(a == 0), stop=(a == NSL - 1)),
                            reads=[wb, self.hTb], writes=[pb])
                    cp = g % 2
                    co = self.cvo[cp]
                    cb = sidx * 4
                    self.V_(lambda v, pb=pb, co=co, fs=fs, cb=cb: v.tensor_scalar(
                        co.ap[:, 0:256], pb.ap[:, 0:256],
                        self.cwt.ap[:, fs, cb:cb + 1], None, ALU.mult),
                        reads=[pb, self.cwt], writes=[co])
                    for i in range(1, 4):
                        self.V_(lambda v, pb=pb, co=co, fs=fs, cb=cb, i=i:
                                v.scalar_tensor_tensor(
                                    co.ap[:, 0:256], pb.ap[:, i:256 + i],
                                    self.cwt.ap[:, fs, cb + i:cb + i + 1],
                                    co.ap[:, 0:256], ALU.mult, ALU.add),
                                reads=[pb, self.cwt], writes=[co])
                    if sidx == 0:
                        self.S_(lambda s, co=co: s.activation(
                            co.ap[:, 0:256], co.ap[:, 0:256], ACT.Silu),
                            writes=[co])
                        self.V_(lambda v, co=co, fs=fs, t0=t0: v.tensor_scalar(
                            self.qc.ap[:, fs, t0:t0 + 256], co.ap[:, 0:256],
                            SCALE, None, ALU.mult), reads=[co],
                            writes=[self.qc])
                    elif sidx == 1:
                        self.S_(lambda s, co=co, fs=fs, t0=t0: s.activation(
                            self.kc.ap[:, fs, t0:t0 + 256], co.ap[:, 0:256],
                            ACT.Silu), reads=[co], writes=[self.kc])
                    else:
                        so = self.svo[cp]
                        self.S_(lambda s, co=co, so=so: s.activation(
                            so.ap[:, 0:256], co.ap[:, 0:256], ACT.Silu),
                            reads=[co], writes=[so])
                        dst = self.vc if sidx == 2 else self.wc
                        tb = self.banks[4 + cp]
                        for hf in range(2):
                            self.T_(lambda t, tb=tb, so=so, hf=hf: t.transpose(
                                tb.bf[:, hf * 128:(hf + 1) * 128],
                                so.ap[:, hf * 128:hf * 128 + 128],
                                self.identb.ap[:, :]),
                                reads=[so, self.identb],
                                writes=[tb])
                        self.V_(lambda v, tb=tb, dst=dst, g=g, fs=fs: v.tensor_copy(
                            dst.ap[:, 2 * g:2 * g + 2, fs * 128:(fs + 1) * 128],
                            tb.bf[:, 0:256].rearrange("p (a b) -> p a b", a=2)),
                            reads=[tb], writes=[dst])
        self.tap("qc", self.qc, [128, NSL, Tc], BF16)
        self.tap("kc", self.kc, [128, NSL, Tc], BF16)
        self.tap("vc", self.vc, [128, NCH, HID], BF16)
        self.tap("wcr", self.wc, [128, NCH, HID], BF16)

    def a2_aux(self):
        T_, V_, S_ = self.T_, self.V_, self.S_
        I = self.I
        for (dst, wn) in ((self.f1c, "Wf1"), (self.G1c, "Wg1")):
            wb = self.wsl[0]
            self.dma("gpsimd", wb.ap[:, :, :],
                     I[wn].rearrange("(a p) c -> p a c", p=128), writes=[wb])
            for half in range(2):
                pb = self.banks[4 + half]
                for a in range(NSL):
                    self.T_(lambda t, a=a, pb=pb, half=half, wb=wb: t.matmul(
                        pb.ap[:, :], wb.ap[:, a, :],
                        self.hTb.ap[:, a, 3 + half * 512:3 + (half + 1) * 512],
                        start=(a == 0), stop=(a == NSL - 1)),
                        reads=[wb, self.hTb], writes=[pb])
                self.V_(lambda v, pb=pb, dst=dst, half=half: v.tensor_copy(
                    dst.ap[:, half * 512:(half + 1) * 512], pb.ap[:, :]),
                    reads=[pb], writes=[dst])
        pb = self.banks[6]
        for half in range(2):
            for a in range(NSL):
                self.T_(lambda t, a=a, half=half, pb=pb: t.matmul(
                    pb.ap[0:16, 0:512], self.Wbb.ap[:, a, :],
                    self.hTb.ap[:, a, 3 + half * 512:3 + (half + 1) * 512],
                    start=(a == 0), stop=(a == NSL - 1)),
                    reads=[self.Wbb, self.hTb], writes=[pb])
            self.V_(lambda v, pb=pb, half=half: v.tensor_copy(
                self.betaR.ap[0:16, half * 512:(half + 1) * 512],
                pb.ap[0:16, 0:512]),
                reads=[pb], writes=[self.betaR])
        for ch in range(NCH):
            pb7 = self.banks[7]
            self.T_(lambda t, ch=ch, pb7=pb7: t.transpose(
                pb7.ap[:, 0:16], self.betaR.ap[0:16, ch * 128:(ch + 1) * 128],
                self.identf.ap[:, :]),
                reads=[self.betaR, self.identf], writes=[pb7])
            self.S_(lambda s, ch=ch, pb7=pb7: s.activation(
                self.beta_tm.ap[:, ch, :], pb7.ap[:, 0:16], ACT.Exp, scale=-1.0),
                reads=[pb7], writes=[self.beta_tm])
            self.V_(lambda v, ch=ch: v.tensor_scalar(
                self.beta_tm.ap[:, ch, :], self.beta_tm.ap[:, ch, :], 1.0, None,
                ALU.add), writes=[self.beta_tm])
            self.V_(lambda v, ch=ch: v.reciprocal(
                self.beta_tm.ap[:, ch, :], self.beta_tm.ap[:, ch, :]),
                writes=[self.beta_tm])
        self.tap("f1c", self.f1c, [128, Tc], BF16)
        self.tap("G1c", self.G1c, [128, Tc], BF16)
        self.tap("beta", self.beta_tm, [128, NCH, H], F32)

    def a3_l2norm(self):
        for ch in range(NCH):
            self.V_(lambda v, ch=ch: v.tensor_mul(
                self.sq2.ap[:, :], self.wc.ap[:, ch, :], self.wc.ap[:, ch, :]),
                reads=[self.wc], writes=[self.sq2])
            self.V_(lambda v: v.tensor_reduce(
                self.rst.ap[:, :], self.sq2.ap[:, :].rearrange(
                    "p (h m) -> p h m", h=H), AX.X, ALU.add),
                reads=[self.sq2], writes=[self.rst])
            self.S_(lambda s: s.activation(self.rst.ap[:, :], self.rst.ap[:, :],
                                           ACT.Ln, bias=self.cpool.ap[:, 0:1]),
                    reads=[self.cpool], writes=[self.rst])
            self.S_(lambda s: s.activation(self.rst.ap[:, :], self.rst.ap[:, :],
                                           ACT.Exp, scale=-0.5), writes=[self.rst])
            self.V_(lambda v, ch=ch: v.tensor_mul(
                self.rst.ap[:, :], self.rst.ap[:, :], self.beta_tm.ap[:, ch, :]),
                reads=[self.beta_tm], writes=[self.rst])
            self.V_(lambda v, ch=ch: v.tensor_tensor(
                self.wc.ap[:, ch, :].rearrange("p (h m) -> p h m", h=H),
                self.wc.ap[:, ch, :].rearrange("p (h m) -> p h m", h=H),
                bc_in(self.rst.ap[:, :], 128), ALU.mult),
                reads=[self.rst], writes=[self.wc])
        self.tap("wcn", self.wc, [128, NCH, HID], BF16)

    def emit_sp(self, sp, ch):
        for j in range(4):
            pb = self.banks[j]
            self.T_(lambda t, j=j, ch=ch, pb=pb: t.matmul(
                pb.ap[:, :], self.f1c.ap[:, ch * 128:(ch + 1) * 128],
                self.Wf2b.ap[:, j * 512:(j + 1) * 512], start=True, stop=True),
                reads=[self.f1c, self.Wf2b], writes=[pb])
            self.S_(lambda s, j=j, pb=pb, sp=sp: s.activation(
                sp.ap[:, j * 512:(j + 1) * 512], pb.ap[:, :], ACT.Exp, scale=-1.0),
                reads=[pb], writes=[sp])
            self.S_(lambda s, j=j, sp=sp: s.activation(
                sp.ap[:, j * 512:(j + 1) * 512], sp.ap[:, j * 512:(j + 1) * 512],
                ACT.Ln, bias=self.cpool.ap[:, 1:2]),
                reads=[self.cpool], writes=[sp])

    def emit_cumexp(self, U, sp, dst_scales):
        for j in range(4):
            pb = self.banks[j]
            self.T_(lambda t, j=j, pb=pb, U=U, sp=sp: t.matmul(
                pb.ap[:, :], U.ap[:, :], sp.ap[:, j * 512:(j + 1) * 512],
                start=True, stop=True), reads=[U, sp], writes=[pb])
            for (dst, sc) in dst_scales:
                self.S_(lambda s, j=j, pb=pb, dst=dst, sc=sc: s.activation(
                    dst.ap[:, j * 512:(j + 1) * 512], pb.ap[:, :], ACT.Exp,
                    scale=sc), reads=[pb], writes=[dst])

    def emit_cols(self, sp, col, dst_ap, bank_i=5):
        pb = self.banks[bank_i]
        for h in range(H):
            self.T_(lambda t, h=h, pb=pb, sp=sp, col=col: t.matmul(
                pb.ap[:, h:h + 1], sp.ap[:, h * 128:(h + 1) * 128], col.ap[:, :],
                start=True, stop=True),
                reads=[sp, col], writes=[pb])
        return self.S_(lambda s, pb=pb, dst_ap=dst_ap: s.activation(
            dst_ap, pb.ap[:, 0:H], ACT.Exp, scale=-1.0),
            reads=[pb, self.ecol], writes=[self.ecol])

    def emit_cols2(self, sp, bank_i=5):
        pb = self.banks[bank_i]
        for h in range(H):
            self.T_(lambda t, h=h, pb=pb, sp=sp: t.matmul(
                pb.ap[:, 2 * h:2 * h + 2], sp.ap[:, h * 128:(h + 1) * 128],
                self.colpair.ap[:, :], start=True, stop=True),
                reads=[sp, self.colpair], writes=[pb])
        ec = self.ecol.ap[:, 0:2 * H].rearrange("p (a b) -> p b a", a=2) \
            if False else None
        pv = pb.ap[:, 0:2 * H].rearrange("p (h w) -> p w h", h=H)
        dv = self.ecol.ap[:, 0:2 * H].rearrange("p (w h) -> p w h", w=2)
        self.S_(lambda s: s.activation(dv, pv, ACT.Exp, scale=-1.0),
                reads=[pb], writes=[self.ecol])

    def emit_diag(self, dst, cols_ap, n=H, reads=()):
        for h in range(n):
            self.V_(lambda v, h=h, dst=dst, cols_ap=cols_ap: v.tensor_scalar(
                dst.ap[:, h, :], self.identb.ap[:, :],
                cols_ap[:, h:h + 1], None, ALU.mult),
                reads=[self.identb] + list(reads),
                writes=[dst])

    def emit_ktm(self, dst, ch, bank0=6):
        for g8 in range(2):
            pbT = self.banks[bank0 + g8]
            for hi in range(8):
                h = g8 * 8 + hi
                self.T_(lambda t, h=h, ch=ch, pbT=pbT, hi=hi: t.transpose(
                    pbT.bf[:, hi * 128:(hi + 1) * 128],
                    self.kc.ap[:, h, ch * 128:(ch + 1) * 128],
                    self.identb.ap[:, :]),
                    reads=[self.kc, self.identb], writes=[pbT])
            self.V_(lambda v, g8=g8, pbT=pbT, dst=dst: v.tensor_copy(
                dst.ap[:, g8 * 8:(g8 + 1) * 8, :],
                pbT.bf[:, 0:1024].rearrange("p (a b) -> p a b", a=8)),
                reads=[pbT], writes=[dst])

    def emit_states(self, ktm, bwdb, diag, ch, bank0=0):
        for g8 in range(2):
            evac_w = []
            for hi in range(8):
                h = g8 * 8 + hi
                bsk = self.banks[bank0 + hi // 4]
                bsv = self.banks[bank0 + 2 + hi // 4]
                sl = (hi % 4) * 128
                self.T_(lambda t, h=h, bsk=bsk, sl=sl, bwdb=bwdb, ktm=ktm:
                        t.matmul(bsk.ap[:, sl:sl + 128],
                                 bwdb.ap[:, h * 128:(h + 1) * 128],
                                 ktm.ap[:, h, :], start=True,
                                 stop=False),
                        reads=[bwdb, ktm], writes=[bsk])
                self.T_(lambda t, h=h, bsk=bsk, sl=sl, diag=diag: t.matmul(
                    bsk.ap[:, sl:sl + 128], diag.ap[:, h, :],
                    self.SkTs.ap[:, h, :], start=False, stop=True),
                    reads=[diag, self.SkTs], writes=[bsk])
                self.T_(lambda t, h=h, bsv=bsv, sl=sl, ch=ch, bwdb=bwdb: t.matmul(
                    bsv.ap[:, sl:sl + 128], bwdb.ap[:, h * 128:(h + 1) * 128],
                    self.vc.ap[:, ch, h * 128:(h + 1) * 128],
                    start=True, stop=False),
                    reads=[bwdb, self.vc], writes=[bsv])
                tok = self.T_(lambda t, h=h, bsv=bsv, sl=sl, diag=diag: t.matmul(
                    bsv.ap[:, sl:sl + 128], diag.ap[:, h, :],
                    self.Svs.ap[:, h, :], start=False, stop=True),
                    reads=[diag, self.Svs], writes=[bsv])
                evac_w.append(tok)
            for bi in range(2):
                bsk = self.banks[bank0 + bi]
                bsv = self.banks[bank0 + 2 + bi]
                hlo = g8 * 8 + bi * 4
                self.V_(lambda v, bsk=bsk, hlo=hlo: v.tensor_copy(
                    self.SkTs.ap[:, hlo:hlo + 4, :],
                    bsk.ap[:, :].rearrange("p (a b) -> p a b", a=4)),
                    reads=[bsk], writes=[self.SkTs], extra=evac_w)
                self.V_(lambda v, bsv=bsv, hlo=hlo: v.tensor_copy(
                    self.Svs.ap[:, hlo:hlo + 4, :],
                    bsv.ap[:, :].rearrange("p (a b) -> p a b", a=4)),
                    reads=[bsv], writes=[self.Svs], extra=evac_w)

    def a4_states(self):
        self.V_(lambda v: v.memset(self.SkTs.ap[:, :, :], 0.0), writes=[self.SkTs])
        self.V_(lambda v: v.memset(self.Svs.ap[:, :, :], 0.0), writes=[self.Svs])
        self.V_(lambda v: v.memset(self.Btn.ap[:, :], 0.0), writes=[self.Btn])
        for ch in range(NCH):
            self.emit_sp(self.spA, ch)
            pb5 = self.banks[5]
            for h in range(H):
                self.T_(lambda t, h=h, pb5=pb5: t.matmul(
                    pb5.ap[:, h:h + 1], self.spA.ap[:, h * 128:(h + 1) * 128],
                    self.onescol.ap[:, :], start=True, stop=True),
                    reads=[self.spA, self.onescol],
                    writes=[pb5])
            self.S_(lambda s, pb5=pb5: s.activation(
                self.ecol.ap[:, 0:H], pb5.ap[:, 0:H], ACT.Exp, scale=-1.0),
                reads=[pb5], writes=[self.ecol])
            self.V_(lambda v, pb5=pb5: v.tensor_add(
                self.Btn.ap[:, :], self.Btn.ap[:, :], pb5.ap[:, 0:H]),
                reads=[pb5], writes=[self.Btn])
            self.emit_cumexp(self.U3n, self.spA, [(self.escrA, -1.0)])
            self.V_(lambda v, ch=ch: v.tensor_mul(
                self.bwdA.ap[:, :], self.wc.ap[:, ch, :], self.escrA.ap[:, :]),
                reads=[self.wc, self.escrA], writes=[self.bwdA])
            self.emit_diag(self.diagA, self.ecol.ap[:, 0:H], reads=[self.ecol])
            self.emit_ktm(self.ktmA, ch)
            self.emit_states(self.ktmA, self.bwdA, self.diagA, ch)
        self.tap("SkT_loc", self.SkTs, [128, H, 128], BF16)
        self.tap("Sv_loc", self.Svs, [128, H, 128], BF16)
        self.tap("Btn", self.Btn, [128, H], F32)

    def a5_combine(self):
        V_, S_, T_ = self.V_, self.S_, self.T_
        pubv = self.pub1.rearrange("p (h two d) -> p h two d", h=H, two=2)
        t_p1a = self.dma_store(pubv[:, :, 0, :], self.SkTs.ap[:, :, :],
                               [self.SkTs])
        t_p1b = self.dma_store(pubv[:, :, 1, :], self.Svs.ap[:, :, :],
                               [self.Svs])
        t_p2 = self.dma_store(self.pub2[:, :], self.Btn.ap[:, :], [self.Btn])
        t_c1 = self.P.emit("gpsimd", lambda g: g.collective_compute(
            "AllGather", ALU.bypass, replica_groups=[list(range(NCORES))],
            ins=[self.pub1[:, :]], outs=[self.agg1[:, :]]),
            [t_p1a, t_p1b], "cs", 1)
        t_c2 = self.P.emit("gpsimd", lambda g: g.collective_compute(
            "AllGather", ALU.bypass, replica_groups=[list(range(NCORES))],
            ins=[self.pub2[:, :]], outs=[self.agg2[:, :]]),
            [t_p2], "cs", 1)
        augv = self.augB.ap.rearrange("p (m h) -> p m h", m=128)
        self.V_(lambda v: v.memset(self.augB.ap[0:9, :], 1.0), writes=[self.augB])
        self.dma("sync", augv[0:8, :, :],
                 self.agg2.rearrange("(c m) h -> c m h", m=128),
                 writes=[self.augB], extra=[t_c2])
        pb = self.banks[4]
        for h in range(H):
            self.T_(lambda t, h=h, pb=pb, augv=augv: t.matmul(
                pb.ap[:, (h % 8) * 8:(h % 8) * 8 + 8],
                augv[0:9, :, h:h + 1],
                self.MmatS.ap[:, :], start=True, stop=True),
                reads=[self.augB, self.MmatS],
                writes=[pb])
            if h % 8 == 7:
                hlo = h - 7
                self.S_(lambda s, pb=pb, hlo=hlo: s.activation(
                    self.WcolS.ap[:, hlo:hlo + 8, :].rearrange("p a b -> p (a b)"),
                    pb.ap[:, 0:64], ACT.Exp, scale=-1.0),
                    reads=[pb], writes=[self.WcolS])
        agg1v = self.agg1.rearrange("(c p) x -> p c x", p=128).rearrange(
            "p c (h d) -> p c h d", h=H)
        for hg in range(4):
            self.dma("sync", self.aggS.ap[:, :, :, :],
                     agg1v[:, :, hg * 4:(hg + 1) * 4, :],
                     writes=[self.aggS], extra=[t_c1])
            for hi in range(4):
                h = hg * 4 + hi
                self.emit_diag(self.diagC, self.WcolS.ap[:, h, :], n=NCORES,
                               reads=[self.WcolS])
                bsk = self.banks[0]
                bsv = self.banks[1]
                for c in range(NCORES):
                    self.T_(lambda t, hi=hi, c=c, bsk=bsk: t.matmul(
                        bsk.ap[:, 0:128], self.diagC.ap[:, c, :],
                        self.aggS.ap[:, c, hi, 0:128], start=(c == 0),
                        stop=(c == NCORES - 1)),
                        reads=[self.diagC, self.aggS],
                        writes=[bsk])
                    self.T_(lambda t, hi=hi, c=c, bsv=bsv: t.matmul(
                        bsv.ap[:, 0:128], self.diagC.ap[:, c, :],
                        self.aggS.ap[:, c, hi, 128:256], start=(c == 0),
                        stop=(c == NCORES - 1)),
                        reads=[self.diagC, self.aggS],
                        writes=[bsv])
                self.V_(lambda v, h=h, bsk=bsk: v.tensor_copy(
                    self.SkTs.ap[:, h, :], bsk.ap[:, 0:128]),
                    reads=[bsk], writes=[self.SkTs])
                self.V_(lambda v, h=h, bsv=bsv: v.tensor_copy(
                    self.Svs.ap[:, h, :], bsv.ap[:, 0:128]),
                    reads=[bsv], writes=[self.Svs])
        self.tap("SkT_in", self.SkTs, [128, H, 128], BF16)
        self.tap("Sv_in", self.Svs, [128, H, 128], BF16)

    def b_scan(self):
        T_, V_, S_ = self.T_, self.V_, self.S_
        for ch in range(NCH):
            self.emit_sp(self.spT, ch)
            self.emit_cumexp(self.U2n, self.spT,
                             [(self.expBp, -1.0), (self.escr, 1.0)])
            self.V_(lambda v, ch=ch: v.tensor_mul(
                self.bwR.ap[:, :], self.wc.ap[:, ch, :], self.escr.ap[:, :]),
                reads=[self.wc, self.escr], writes=[self.bwR])
            self.emit_cumexp(self.U3n, self.spT, [(self.escr, -1.0)])
            self.V_(lambda v, ch=ch: v.tensor_mul(
                self.bwd.ap[:, :], self.wc.ap[:, ch, :], self.escr.ap[:, :]),
                reads=[self.wc, self.escr], writes=[self.bwd])
            self.emit_cols2(self.spT)
            for j in range(4):
                pb = self.banks[j]
                self.T_(lambda t, j=j, ch=ch, pb=pb: t.matmul(
                    pb.ap[:, :], self.G1c.ap[:, ch * 128:(ch + 1) * 128],
                    self.Wg2b.ap[:, j * 512:(j + 1) * 512], start=True,
                    stop=True), reads=[self.G1c, self.Wg2b], writes=[pb])
                self.V_(lambda v, j=j, pb=pb: v.tensor_tensor(
                    self.sgrp.ap[:, :], pb.ap[:, :],
                    self.bg2T.ap[:, j * 512:(j + 1) * 512], ALU.add),
                    reads=[pb, self.bg2T], writes=[self.sgrp])
                self.S_(lambda s: s.activation(self.sgrp.ap[:, :],
                                               self.sgrp.ap[:, :], ACT.Exp,
                                               scale=-1.0), writes=[self.sgrp])
                self.V_(lambda v: v.tensor_scalar(self.sgrp.ap[:, :],
                                                  self.sgrp.ap[:, :], 1.0, None,
                                                  ALU.add), writes=[self.sgrp])
                self.V_(lambda v: v.reciprocal(self.sgrp.ap[:, :],
                                               self.sgrp.ap[:, :]),
                        writes=[self.sgrp])
                self.V_(lambda v, j=j: v.tensor_copy(
                    self.gsig.ap[:, j * 512:(j + 1) * 512], self.sgrp.ap[:, :]),
                    reads=[self.sgrp], writes=[self.gsig])
            self.emit_diag(self.diagB, self.ecol.ap[:, 0:H], reads=[self.ecol])
            for g4 in range(4):
                hlo = g4 * 4
                bPT = self.banks[4]
                bS = self.banks[5]
                bTR = self.banks[6]
                bO = self.banks[7]
                self.emit_diag(self.diagR, self.ecol.ap[:, H + hlo:H + hlo + 4],
                               n=4, reads=[self.ecol])
                for hi in range(4):
                    h = hlo + hi
                    self.T_(lambda t, h=h, hi=hi, bPT=bPT: t.matmul(
                        bPT.ap[:, hi * 128:(hi + 1) * 128],
                        self.diagR.ap[:, hi, :], self.SkTs.ap[:, h, :],
                        start=True, stop=True),
                        reads=[self.diagR, self.SkTs],
                        writes=[bPT])
                self.V_(lambda v, bPT=bPT: v.tensor_copy(
                    self.gSR.ap[:, :, :],
                    bPT.ap[:, :].rearrange("p (a b) -> p a b", a=4)),
                    reads=[bPT], writes=[self.gSR])
                for hi in range(4):
                    h = hlo + hi
                    self.T_(lambda t, h=h, hi=hi, ch=ch, bPT=bPT: t.matmul(
                        bPT.ap[:, hi * 128:(hi + 1) * 128],
                        self.kc.ap[:, h, ch * 128:(ch + 1) * 128],
                        self.qc.ap[:, h, ch * 128:(ch + 1) * 128],
                        start=True, stop=True),
                        reads=[self.kc, self.qc],
                        writes=[bPT])
                self.V_(lambda v, bPT=bPT: v.tensor_tensor(
                    self.gPM.ap[:, :, :],
                    bPT.ap[:, :].rearrange("p (a b) -> p a b", a=4),
                    bc_mid(self.trilS.ap[:, :], 4), ALU.mult),
                    reads=[bPT, self.trilS], writes=[self.gPM])
                for hi in range(4):
                    h = hlo + hi
                    self.T_(lambda t, h=h, hi=hi, bTR=bTR: t.transpose(
                        bTR.bf[:, 512 + hi * 128:512 + (hi + 1) * 128],
                        self.bwR.ap[:, h * 128:(h + 1) * 128],
                        self.identb.ap[:, :]),
                        reads=[self.bwR, self.identb],
                        writes=[bTR])
                self.V_(lambda v, bTR=bTR: v.tensor_copy(
                    self.gRT.ap[:, :, :],
                    bTR.bf[:, 512:1024].rearrange("p (a b) -> p a b", a=4)),
                    reads=[bTR], writes=[self.gRT])
                for hi in range(4):
                    h = hlo + hi
                    self.T_(lambda t, h=h, hi=hi, bS=bS: t.matmul(
                        bS.ap[:, hi * 128:(hi + 1) * 128], self.gPM.ap[:, hi, :],
                        self.bwR.ap[:, h * 128:(h + 1) * 128],
                        start=True, stop=False),
                        reads=[self.gPM, self.bwR],
                        writes=[bS])
                    self.T_(lambda t, h=h, hi=hi, ch=ch, bS=bS: t.matmul(
                        bS.ap[:, hi * 128:(hi + 1) * 128],
                        self.qc.ap[:, h, ch * 128:(ch + 1) * 128],
                        self.gSR.ap[:, hi, :], start=False, stop=True),
                        reads=[self.qc, self.gSR], writes=[bS])
                self.V_(lambda v, bS=bS, hlo=hlo: v.tensor_tensor(
                    self.sgrp.ap[:, :], bS.ap[:, :],
                    self.expBp.ap[:, hlo * 128:(hlo + 4) * 128], ALU.mult),
                    reads=[bS, self.expBp], writes=[self.sgrp])
                self.V_(lambda v: v.tensor_reduce(
                    self.smx.ap[:, 0:4], self.sgrp.ap[:, :].rearrange(
                        "p (a b) -> p a b", a=4), AX.X, ALU.max, negate=True),
                    reads=[self.sgrp], writes=[self.smx])
                for hi in range(4):
                    self.S_(lambda s, hi=hi: s.activation(
                        self.sgrp.ap[:, hi * 128:(hi + 1) * 128],
                        self.sgrp.ap[:, hi * 128:(hi + 1) * 128], ACT.Exp,
                        bias=self.smx.ap[:, hi:hi + 1]),
                        reads=[self.smx], writes=[self.sgrp])
                self.V_(lambda v: v.tensor_reduce(
                    self.smx.ap[:, 4:8], self.sgrp.ap[:, :].rearrange(
                        "p (a b) -> p a b", a=4), AX.X, ALU.add),
                    reads=[self.sgrp], writes=[self.smx])
                self.V_(lambda v: v.reciprocal(self.smx.ap[:, 4:8],
                                               self.smx.ap[:, 4:8]),
                        writes=[self.smx])
                self.V_(lambda v: v.tensor_tensor(
                    self.sgrp.ap[:, :].rearrange("p (a b) -> p a b", a=4),
                    self.sgrp.ap[:, :].rearrange("p (a b) -> p a b", a=4),
                    bc_in(self.smx.ap[:, 4:8], 128), ALU.mult),
                    reads=[self.smx], writes=[self.sgrp])
                self.V_(lambda v, hlo=hlo: v.tensor_tensor(
                    self.gAB.ap[:, :, :].rearrange("p a b -> p (a b)"),
                    self.sgrp.ap[:, :],
                    self.expBp.ap[:, hlo * 128:(hlo + 4) * 128], ALU.mult),
                    reads=[self.sgrp, self.expBp], writes=[self.gAB])
                for hi in range(4):
                    self.T_(lambda t, hi=hi, bTR=bTR: t.transpose(
                        bTR.bf[:, hi * 128:(hi + 1) * 128], self.gAB.ap[:, hi, :],
                        self.identb.ap[:, :]),
                        reads=[self.gAB, self.identb],
                        writes=[bTR])
                self.V_(lambda v, bTR=bTR: v.tensor_copy(
                    self.gAT.ap[:, :, :],
                    bTR.bf[:, 0:512].rearrange("p (a b) -> p a b", a=4)),
                    reads=[bTR], writes=[self.gAT])
                for hi in range(4):
                    self.T_(lambda t, hi=hi, bPT=bPT: t.matmul(
                        bPT.ap[:, hi * 128:(hi + 1) * 128], self.gRT.ap[:, hi, :],
                        self.gAT.ap[:, hi, :], start=True, stop=True),
                        reads=[self.gRT, self.gAT],
                        writes=[bPT])
                self.V_(lambda v, bPT=bPT: v.tensor_tensor(
                    self.gP2.ap[:, :, :],
                    bPT.ap[:, :].rearrange("p (a b) -> p a b", a=4),
                    bc_mid(self.tril1.ap[:, :], 4), ALU.mult),
                    reads=[bPT, self.tril1], writes=[self.gP2])
                self.V_(lambda v, hlo=hlo: v.tensor_tensor(
                    self.svr.ap[:, :, :], self.Svs.ap[:, hlo:hlo + 4, :],
                    bc_in(self.ecol.ap[:, H + hlo:H + hlo + 4], 128), ALU.mult),
                    reads=[self.Svs, self.ecol], writes=[self.svr])
                for hi in range(4):
                    h = hlo + hi
                    self.T_(lambda t, h=h, hi=hi, ch=ch, bO=bO: t.matmul(
                        bO.ap[:, hi * 128:(hi + 1) * 128], self.gP2.ap[:, hi, :],
                        self.vc.ap[:, ch, h * 128:(h + 1) * 128],
                        start=True, stop=False),
                        reads=[self.gP2, self.vc],
                        writes=[bO])
                    self.T_(lambda t, hi=hi, bO=bO: t.matmul(
                        bO.ap[:, hi * 128:(hi + 1) * 128], self.gAT.ap[:, hi, :],
                        self.svr.ap[:, hi, :], start=False, stop=True),
                        reads=[self.gAT, self.svr], writes=[bO])
                self.S_(lambda s, bO=bO: s.activation(
                    self.ogrp.ap[:, :], bO.ap[:, :], ACT.Square),
                    reads=[bO], writes=[self.ogrp])
                self.V_(lambda v: v.tensor_reduce(
                    self.rst.ap[:, 0:4], self.ogrp.ap[:, :].rearrange(
                        "p (a b) -> p a b", a=4), AX.X, ALU.add),
                    reads=[self.ogrp], writes=[self.rst])
                self.S_(lambda s: s.activation(self.rst.ap[:, 0:4],
                                               self.rst.ap[:, 0:4], ACT.Ln,
                                               scale=1.0 / 128.0,
                                               bias=self.cpool.ap[:, 2:3]),
                        reads=[self.cpool], writes=[self.rst])
                self.S_(lambda s: s.activation(self.rst.ap[:, 0:4],
                                               self.rst.ap[:, 0:4], ACT.Exp,
                                               scale=-0.5), writes=[self.rst])
                self.V_(lambda v, bO=bO: v.tensor_tensor(
                    self.ogrp.ap[:, :].rearrange("p (a b) -> p a b", a=4),
                    bO.ap[:, :].rearrange("p (a b) -> p a b", a=4),
                    bc_in(self.rst.ap[:, 0:4], 128), ALU.mult),
                    reads=[bO, self.rst], writes=[self.ogrp])
                self.V_(lambda v, hlo=hlo: v.tensor_tensor(
                    self.gOF.ap[:, :, :].rearrange("p a b -> p (a b)"),
                    self.ogrp.ap[:, :],
                    self.gsig.ap[:, hlo * 128:(hlo + 4) * 128], ALU.mult),
                    reads=[self.ogrp, self.gsig], writes=[self.gOF])
                for hi in range(4):
                    self.T_(lambda t, hi=hi, bTR=bTR: t.transpose(
                        bTR.bf[:, 512 + hi * 128:512 + (hi + 1) * 128],
                        self.gOF.ap[:, hi, :], self.identb.ap[:, :]),
                        reads=[self.gOF, self.identb],
                        writes=[bTR])
                self.V_(lambda v, bTR=bTR: v.tensor_copy(
                    self.oTst.ap[:, :, :],
                    bTR.bf[:, 512:1024].rearrange("p (a b) -> p a b", a=4)),
                    reads=[bTR], writes=[self.oTst])
                self.dma_store(
                    self.o_scr.rearrange("(s p) t -> p s t", p=128)[
                        :, hlo:hlo + 4, ch * 128:(ch + 1) * 128],
                    self.oTst.ap[:, :, :], [self.oTst])
            self.emit_ktm(self.ktm, ch, bank0=4)
            self.emit_states(self.ktm, self.bwd, self.diagB, ch, bank0=4)

    def b3_wo_ln1(self):
        T_, V_, S_ = self.T_, self.V_, self.S_
        I = self.I
        self.emit_warmup()
        st_toks = [(f"st{i}", self.P.cnt[f"st{i}"])
                   for i in range(self.P.LANES["st"]) if self.P.cnt[f"st{i}"]]
        self.dma("sync", self.oT.ap[:, :, :],
                 self.o_scr.rearrange("(s p) t -> p s t", p=128),
                 writes=[self.oT], extra=st_toks)
        for band in range(2):
            for i in range(8):
                sl = band * 8 + i
                wb = self.wo_b[i]
                self.dma("gpsimd", wb.ap[:, :],
                         I["Wo"][sl * 128:(sl + 1) * 128, :], writes=[wb])
                self.V_(lambda v, wb=wb: v.tensor_scalar(
                    wb.ap[:, :], wb.ap[:, :], self.nwcol.ap[:, 0:1], None,
                    ALU.mult), reads=[self.nwcol], writes=[wb])
            for tc in range(NCH):
                par = tc % 2
                if band == 0:
                    xb = self.xs[0]
                    self.dma("sync", xb.ap[:, :],
                             I["h_nat"][tc * 128:(tc + 1) * 128, :], writes=[xb])
                for j in range(4):
                    pb = self.banks[par * 4 + j]
                    for i in range(8):
                        sl = band * 8 + i
                        wb = self.wo_b[i]
                        self.T_(lambda t, i=i, j=j, tc=tc, sl=sl, pb=pb, wb=wb:
                                t.matmul(pb.ap[:, :],
                                         self.oT.ap[:, sl, tc * 128:(tc + 1) * 128],
                                         wb.ap[:, j * 512:(j + 1) * 512],
                                         start=(i == 0), stop=(i == 7)),
                                reads=[self.oT, wb],
                                writes=[pb])
                for j in range(4):
                    pb = self.banks[par * 4 + j]
                    if band == 0:
                        xb = self.xs[0]
                        self.V_(lambda v, j=j, tc=tc, pb=pb, xb=xb: v.tensor_tensor(
                            self.r1a.ap[:, tc, j * 512:(j + 1) * 512], pb.ap[:, :],
                            xb.ap[:, j * 512:(j + 1) * 512], ALU.add),
                            reads=[pb, xb], writes=[self.r1a])
                    else:
                        self.V_(lambda v, j=j, tc=tc, pb=pb: v.tensor_tensor(
                            self.r1a.ap[:, tc, j * 512:(j + 1) * 512],
                            self.r1a.ap[:, tc, j * 512:(j + 1) * 512],
                            pb.ap[:, :], ALU.add),
                            reads=[pb], writes=[self.r1a])
        for tc in range(NCH):
            self.emit_ln(tc, to_xT=True)
        self.tap("x1", self.r1a, [128, NCH, HID], F32)

    def emit_ln(self, tc, to_xT):
        V_, S_, T_ = self.V_, self.S_, self.T_
        acc = self.r1a
        sq = self.sqscr if to_xT else self.sqscr2
        self.V_(lambda v, tc=tc: v.tensor_reduce(
            self.lnc.ap[:, 0:1], acc.ap[:, tc, :], AX.X, ALU.add),
            reads=[acc], writes=[self.lnc])
        self.V_(lambda v: v.tensor_scalar(self.lnc.ap[:, 0:1], self.lnc.ap[:, 0:1],
                                          1.0 / HID, None, ALU.mult),
                writes=[self.lnc])
        self.S_(lambda s, tc=tc, sq=sq: s.activation(
            sq.ap[:, :], acc.ap[:, tc, :], ACT.Square,
            accum_out=self.lnc.ap[:, 1:2]),
            reads=[acc], writes=[sq, self.lnc])
        self.V_(lambda v: v.tensor_scalar(self.lnc.ap[:, 1:2], self.lnc.ap[:, 1:2],
                                          1.0 / HID, None, ALU.mult),
                writes=[self.lnc])
        self.V_(lambda v: v.tensor_mul(self.lnc.ap[:, 2:3], self.lnc.ap[:, 0:1],
                                       self.lnc.ap[:, 0:1]), writes=[self.lnc])
        self.V_(lambda v: v.tensor_sub(self.lnc.ap[:, 1:2], self.lnc.ap[:, 1:2],
                                       self.lnc.ap[:, 2:3]), writes=[self.lnc])
        self.S_(lambda s: s.activation(self.lnc.ap[:, 3:4], self.lnc.ap[:, 1:2],
                                       ACT.Ln, bias=self.cpool.ap[:, 2:3]),
                reads=[self.cpool], writes=[self.lnc])
        self.S_(lambda s: s.activation(self.lnc.ap[:, 3:4], self.lnc.ap[:, 3:4],
                                       ACT.Exp, scale=-0.5), writes=[self.lnc])
        self.V_(lambda v, tc=tc: v.tensor_scalar(
            acc.ap[:, tc, :], acc.ap[:, tc, :], self.lnc.ap[:, 0:1], None,
            ALU.subtract), reads=[self.lnc], writes=[acc])
        self.V_(lambda v, tc=tc: v.tensor_scalar(
            acc.ap[:, tc, :], acc.ap[:, tc, :], self.lnc.ap[:, 3:4], None,
            ALU.mult), reads=[self.lnc], writes=[acc])
        if not to_xT:
            return
        self.V_(lambda v, tc=tc, sq=sq: v.tensor_copy(sq.ap[:, :],
                                                      acc.ap[:, tc, :]),
                reads=[acc], writes=[sq])
        for g8 in range(2):
            pbT = self.banks[4 + g8]
            for hi in range(8):
                sl = g8 * 8 + hi
                self.T_(lambda t, sl=sl, hi=hi, pbT=pbT, sq=sq: t.transpose(
                    pbT.bf[:, hi * 128:(hi + 1) * 128],
                    sq.ap[:, sl * 128:(sl + 1) * 128], self.identb.ap[:, :]),
                    reads=[sq, self.identb], writes=[pbT])
            self.V_(lambda v, g8=g8, tc=tc, pbT=pbT: v.tensor_copy(
                self.xT.ap[:, g8 * 8:(g8 + 1) * 8, tc * 128:(tc + 1) * 128],
                pbT.bf[:, 0:1024].rearrange("p (a b) -> p a b", a=8)),
                reads=[pbT], writes=[self.xT])

    def b4_ffn(self):
        T_, V_, S_ = self.T_, self.V_, self.S_
        I = self.I

        def emit_gateup(hb, hs):
            hidx = hb * 8 + hs
            par = hs % 2
            fg, fu = self.ffg[par], self.ffu[par]
            gbb = self.gb[hb % 2]
            self.dma("gpsimd", fg.ap[:, :, :],
                     I["wgT"][:, hidx * 128:(hidx + 1) * 128].rearrange(
                         "(a p) c -> p a c", p=128), writes=[fg])
            self.dma("gpsimd", fu.ap[:, :, :],
                     I["wuT"][:, hidx * 128:(hidx + 1) * 128].rearrange(
                         "(a p) c -> p a c", p=128), writes=[fu])
            bg = self.banks[4 + par]
            bu = self.banks[6 + par]
            for half in range(2):
                for (bk, fw) in ((bg, fg), (bu, fu)):
                    for a in range(NSL):
                        self.T_(lambda t, a=a, half=half, bk=bk, fw=fw: t.matmul(
                            bk.ap[:, :], fw.ap[:, a, :],
                            self.xT.ap[:, a, half * 512:(half + 1) * 512],
                            start=(a == 0), stop=(a == NSL - 1)),
                            reads=[fw, self.xT], writes=[bk])
                self.S_(lambda s, bg=bg, half=half: s.activation(
                    self.sgf.ap[:, half * 512:(half + 1) * 512], bg.ap[:, :],
                    ACT.Silu), reads=[bg], writes=[self.sgf])
                self.V_(lambda v, bu=bu, hs=hs, half=half, gbb=gbb:
                        v.tensor_tensor(
                    gbb.ap[:, hs, half * 512:(half + 1) * 512], bu.ap[:, :],
                    self.sgf.ap[:, half * 512:(half + 1) * 512], ALU.mult),
                    reads=[bu, self.sgf], writes=[gbb])

        def emit_down_unit(hb, unit):
            tc, jh = unit // 2, unit % 2
            pair = unit % 2
            gbb = self.gb[hb % 2]
            for jj in range(2):
                j = jh * 2 + jj
                pb = self.banks[pair * 2 + jj]
                for i in range(8):
                    wd = self.wd_all[i]
                    self.T_(lambda t, i=i, j=j, tc=tc, pb=pb, wd=wd, gbb=gbb:
                            t.matmul(
                        pb.ap[:, :], gbb.ap[:, i, tc * 128:(tc + 1) * 128],
                        wd.ap[:, j * 512:(j + 1) * 512],
                        start=(i == 0), stop=(i == 7)),
                        reads=[gbb, wd], writes=[pb])
            for jj in range(2):
                j = jh * 2 + jj
                pb = self.banks[pair * 2 + jj]
                self.V_(lambda v, j=j, tc=tc, pb=pb: v.tensor_tensor(
                    self.r1a.ap[:, tc, j * 512:(j + 1) * 512],
                    self.r1a.ap[:, tc, j * 512:(j + 1) * 512],
                    pb.ap[:, :], ALU.add),
                    reads=[pb], writes=[self.r1a])

        def emit_wd_loads(hb):
            for i in range(8):
                hidx = hb * 8 + i
                wd = self.wd_all[i]
                self.dma("gpsimd", wd.ap[:, :],
                         I["wdT"][hidx * 128:(hidx + 1) * 128, :], writes=[wd])

        self.emit_warmup()
        emit_wd_loads(0)
        for hs in range(8):
            emit_gateup(0, hs)
        for hb in range(1, 8):
            for hs in range(8):
                emit_down_unit(hb - 1, 2 * hs)
                emit_down_unit(hb - 1, 2 * hs + 1)
                emit_gateup(hb, hs)
            emit_wd_loads(hb)
        for unit in range(16):
            emit_down_unit(7, unit)
        for tc in range(NCH):
            self.emit_ln(tc, to_xT=False)
            self.dma_store(self.out[tc * 128:(tc + 1) * 128, :],
                           self.r1a.ap[:, tc, :], [self.r1a])

    def finish(self):
        toks = [(f"st{i}", self.P.cnt[f"st{i}"])
                for i in range(self.P.LANES["st"]) if self.P.cnt[f"st{i}"]]
        if toks:
            self.P.emit("gpsimd", lambda g: g.engine_nop(), toks, "gs", 1)


def build_host_inputs(inputs):
    h = np.ascontiguousarray(np.asarray(inputs["L0_attn_hidden_states"],
                                        np.float32)[0])
    hT_full = np.ascontiguousarray(h.T)
    per_core = []
    for c in range(NCORES):
        lo = c * Tc
        hT_ext = np.zeros((HID, Tc + 3), np.float32)
        if c == 0:
            hT_ext[:, 3:] = hT_full[:, 0:Tc]
        else:
            hT_ext[:, :] = hT_full[:, lo - 3:lo + Tc]
        Mm = np.zeros((9, 8), np.float32)
        for cp in range(8):
            if cp < c:
                for cm in range(cp + 1, c):
                    Mm[cm, cp] = 1.0
            else:
                Mm[8, cp] = 1e5
        per_core.append({
            "hT_ext": hT_ext,
            "h_nat": np.ascontiguousarray(h[lo:lo + Tc]),
            "Mmat": Mm,
        })
    f32 = lambda k: np.ascontiguousarray(np.asarray(inputs[k], np.float32))
    shared = {
        "Wq": f32("L0_attn_Wq"), "Wk": f32("L0_attn_Wk"),
        "Wv": f32("L0_attn_Wv"), "Ww": f32("L0_attn_Ww"),
        "Wo": f32("L0_attn_Wo"),
        "cq": f32("L0_attn_cq"), "ck": f32("L0_attn_ck"),
        "cv": f32("L0_attn_cv"),
        "Wf1": f32("L0_attn_Wf1"), "Wf2": f32("L0_attn_Wf2"),
        "Wb": f32("L0_attn_Wb"),
        "Wg1": f32("L0_attn_Wg1"), "Wg2": f32("L0_attn_Wg2"),
        "bg2": f32("L0_attn_bg2"), "normw": f32("L0_attn_norm_w"),
        "wgT": np.ascontiguousarray(np.asarray(inputs["L0_ffn_w_gate"],
                                               np.float32).T),
        "wuT": np.ascontiguousarray(np.asarray(inputs["L0_ffn_w_up"],
                                               np.float32).T),
        "wdT": np.ascontiguousarray(np.asarray(inputs["L0_ffn_w_down"],
                                               np.float32).T),
    }
    return [{**shared, **pc} for pc in per_core]


_BUILD_CACHE = {}


def kernel(**inputs):
    in_maps = build_host_inputs(inputs)
    key = (STAGES, WANT_DBG)
    if key not in _BUILD_CACHE:
        b = Builder()
        nc = b.build()
        _BUILD_CACHE[key] = (nc, b)
    nc, b = _BUILD_CACHE[key]
    res = run_bass_kernel_spmd(nc, in_maps, core_ids=list(range(NCORES)))
    out = np.concatenate([r["out"] for r in res.results], axis=0)
    return out[None]


# version 11
# speedup vs baseline: 1.9527x; 1.9206x over previous
import numpy as np
import ml_dtypes
import concourse.bass as bass
import concourse.mybir as mybir
from concourse.bass_utils import run_bass_kernel_spmd

F32 = mybir.dt.float32
BF16 = mybir.dt.bfloat16
U8 = mybir.dt.uint8
AX = mybir.AxisListType
ALU = mybir.AluOpType
ACT = mybir.ActivationFunctionType

NCORES = 8
T = 8192
HID = 2048
Tc = T // NCORES
C = 128
NCH = Tc // C
H = 16
NSL = HID // 128
DH = 8192
SCALE = float(128) ** -0.5

ENGINES = ("sync", "tensor", "vector", "scalar", "gpsimd")
STAGES = "all"
WANT_DBG = ()


class Buf:
    __slots__ = ("ap", "w", "r", "rng", "fresh")

    def __init__(self, ap, rng=None):
        self.ap = ap
        self.w = None
        self.r = {}
        self.rng = rng
        self.fresh = True

    def add_r(self, tok):
        s, v = tok
        if self.r.get(s, -1) < v:
            self.r[s] = v

    def r_toks(self):
        return [(s, v) for s, v in self.r.items()]

    def __getitem__(self, key):
        return self.ap[key]


class Bank(Buf):
    __slots__ = ("bf",)

    def __init__(self, ap):
        super().__init__(ap)
        self.bf = ap.bitcast(BF16)


class Plan:
    LANES = {"ld": 4, "lw": 6, "st": 4}

    def __init__(self, nc):
        self.nc = nc
        self.plan = []
        self.sems = {}
        self.cnt = {}
        for s in ("ts", "vs", "ss", "gs", "cs"):
            self.sems[s] = nc.alloc_semaphore(s)
            self.cnt[s] = 0
        self.rr = {}
        for q, n in self.LANES.items():
            self.rr[q] = 0
            for i in range(n):
                name = f"{q}{i}"
                self.sems[name] = nc.alloc_semaphore(name)
                self.cnt[name] = 0

    def lane(self, q):
        i = self.rr[q] % self.LANES[q]
        self.rr[q] += 1
        name = f"{q}{i}"
        pre = (name, self.cnt[name]) if self.cnt[name] else None
        return name, pre

    def emit(self, eng, fn, waits, inc, inc_by):
        tok = None
        if inc is not None:
            self.cnt[inc] += inc_by
            tok = (inc, self.cnt[inc])
        self.plan.append((eng, fn, tuple(w for w in waits if w is not None),
                          (inc, inc_by) if inc else None))
        return tok

    SEM_ENG = {"ts": "tensor"}

    def realize(self, block):
        wm = {e: {} for e in ENGINES}
        eff_waits = []
        needed = {}
        for (e, fn, waits, inc) in self.plan:
            ew = []
            for (s, v) in waits:
                if self.SEM_ENG.get(s) == e:
                    continue
                if wm[e].get(s, 0) < v:
                    ew.append((s, v))
                    wm[e][s] = v
                    needed.setdefault(s, set()).add(v)
            eff_waits.append(ew)
        cum = {s: 0 for s in self.cnt}
        pend = {s: 0 for s in self.cnt}
        plan2 = []
        for (e, fn, waits, inc), ew in zip(self.plan, eff_waits):
            inc2 = None
            if inc is not None:
                s, by = inc
                cum[s] += by
                pend[s] += by
                if by >= 16 or s == "cs" or cum[s] in needed.get(s, ()):
                    inc2 = (s, pend[s])
                    pend[s] = 0
            plan2.append((e, fn, ew, inc2))

        def mk(eng_name):
            def body(eng):
                for (e, fn, waits, inc) in plan2:
                    if e != eng_name:
                        continue
                    for (s, v) in waits:
                        eng.wait_ge(self.sems[s], v)
                    ins = fn(eng)
                    if inc is not None:
                        s, by = inc
                        ins.then_inc(self.sems[s], by)
            return body

        block.sync(mk("sync"))
        block.tensor(mk("tensor"))
        block.vector(mk("vector"))
        block.scalar(mk("scalar"))
        block.gpsimd(mk("gpsimd"))


def bc_in(ap, n):
    return bass.AP(ap.tensor, ap.offset, [list(p) for p in ap.ap] + [[0, n]])


def bc_mid(ap, n):
    pr = [list(p) for p in ap.ap]
    return bass.AP(ap.tensor, ap.offset, [pr[0], [0, n]] + pr[1:])


class Builder:
    SEM_OF = {"sync": "ld", "gpsimd_dma": "lw", "tensor": "ts", "vector": "vs",
              "scalar": "ss", "gpsimd": "gs"}

    def __init__(self):
        nc = bass.Bass("TRN2", target_bir_lowering=False, debug=False)
        self.nc = nc
        self.P = Plan(nc)
        self.dbg = {}

        I = {}
        def inp(name, shape, dtype=F32):
            I[name] = nc.dram_tensor(name, shape, dtype, kind="ExternalInput").ap()
        self.I = I
        inp("hT_ext", [HID, Tc + 3]); inp("h_nat", [Tc, HID])
        for w in ("Wq", "Wk", "Wv", "Ww", "Wo"):
            inp(w, [HID, HID])
        for w in ("cq", "ck", "cv"):
            inp(w, [HID, 4])
        inp("Wf1", [HID, 128]); inp("Wf2", [128, HID]); inp("Wb", [HID, H])
        inp("Wg1", [HID, 128]); inp("Wg2", [128, HID])
        inp("bg2", [HID]); inp("normw", [128])
        inp("wgT", [HID, DH]); inp("wuT", [HID, DH]); inp("wdT", [DH, HID])
        inp("Mmat", [9, 8])
        self.out = nc.dram_tensor("out", [Tc, HID], F32, kind="ExternalOutput").ap()

        self.pub1 = nc.dram_tensor("pub1", [128, H * 256], BF16).ap()
        self.pub2 = nc.dram_tensor("pub2", [128, H], F32).ap()
        self.agg1 = nc.dram_tensor("agg1", [NCORES * 128, H * 256], BF16).ap()
        self.agg2 = nc.dram_tensor("agg2", [NCORES * 128, H], F32).ap()
        self.o_scr = nc.dram_tensor("o_scr", [HID, Tc], BF16).ap()

        def sb(name, shape, dtype):
            return Buf(nc.alloc_sbuf_tensor(name, shape, dtype).ap())
        self.trilS = sb("trilS", [128, 128], F32)
        self.tril1 = sb("tril1", [128, 128], BF16)
        self.identb = sb("identb", [128, 128], BF16)
        self.identf = sb("identf", [16, 16], F32)
        self.U2n = sb("U2n", [128, 128], F32)
        self.U3n = sb("U3n", [128, 128], F32)
        self.onescol = sb("onescol", [128, 1], F32)
        self.halfcol = sb("halfcol", [128, 1], F32)
        self.colpair = sb("colpair", [128, 2], F32)
        self.ones1p = sb("ones1p", [1, 128], F32)
        self.Wf2b = sb("Wf2b", [128, HID], BF16)
        self.Wg2b = sb("Wg2b", [128, HID], BF16)
        self.bg2T = sb("bg2T", [128, HID], BF16)
        self.nwcol = sb("nwcol", [128, 1], F32)
        self.MmatS = sb("MmatS", [9, 8], F32)
        self.f1c = sb("f1c", [128, Tc], BF16)
        self.G1c = sb("G1c", [128, Tc], BF16)
        self.SkTs = sb("SkTs", [128, H, 128], BF16)
        self.Svs = sb("Svs", [128, H, 128], BF16)
        self.Btn = sb("Btn", [128, H], F32)
        self.ecol = sb("ecol", [128, 2 * H], F32)
        self.beta_tm = sb("beta_tm", [128, NCH, H], F32)
        self.smx = sb("smx", [128, 16], F32)
        self.rst = sb("rst", [128, 16], F32)
        self.lnc = sb("lnc", [128, 8], F32)
        self.cpool = sb("cpool", [128, 4], F32)

        self.alias_reg = []
        self.AR = 184320
        self.arena = nc.alloc_sbuf_tensor("arena", [128, self.AR], U8)
        self.sbuf_slack = nc.sbuf_bytes_remaining
        assert self.sbuf_slack >= 0, f"sbuf over by {-self.sbuf_slack}"

        a = self.ab
        self.kc = a(0, [NSL, Tc], BF16)
        self.vc = a(32768, [NCH, HID], BF16)
        self.wc = a(65536, [NCH, HID], BF16)
        self.qc = a(98304, [NSL, Tc], BF16)
        self.hTb = a(131072, [NSL, Tc + 3], BF16)
        self.wsl = [a(164096, [NSL, 128], BF16), a(168192, [NSL, 128], BF16)]
        self.cvo = [a(172288, [264], F32), a(173376, [264], F32)]
        self.svo = [a(174464, [264], BF16), a(175040, [264], BF16)]
        self.betaR = a(175616, [1024], F32)
        self.bgrow = a(0, [HID], F32)
        self.Uh = a(8192, [128], F32)
        self.cwt = a(179712, [NSL, 16], F32)
        self.Wbb = a(180736, [NSL, H], BF16)
        self.sq2 = a(131072, [HID], BF16)
        self.spA = a(135168, [HID], F32)
        self.escrA = a(143360, [HID], BF16)
        self.bwdA = a(147456, [HID], BF16)
        self.ktmA = a(151552, [H, 128], BF16)
        self.diagA = a(155648, [H, 128], BF16)
        self.aggS = a(131072, [NCORES, 4, 256], BF16)
        self.augB = a(139264, [HID], F32)
        self.WcolS = a(147456, [H, 8], F32)
        self.diagC = a(155648, [NCORES, 128], BF16)
        self.spT = a(131072, [HID], F32)
        self.expBp = a(139264, [HID], BF16)
        self.escr = a(143360, [HID], BF16)
        self.bwR = a(147456, [HID], BF16)
        self.bwd = a(151552, [HID], BF16)
        self.gsig = a(155648, [HID], BF16)
        self.diagB = a(159744, [H, 128], BF16)
        self.ktm = a(163840, [H, 128], BF16)
        self.oTst = a(167936, [4, 128], BF16)
        self.gPM = a(168960, [4, 128], BF16)
        self.gRT = a(169984, [4, 128], BF16)
        self.gSR = a(171008, [4, 128], BF16)
        self.gAB = a(172032, [4, 128], BF16)
        self.gAT = a(173056, [4, 128], BF16)
        self.gP2 = a(174080, [4, 128], BF16)
        self.gOF = a(175104, [4, 128], BF16)
        self.sgrp = a(176128, [512], F32)
        self.ogrp = a(178176, [512], F32)
        self.svr = a(180224, [4, 128], BF16)
        self.diagR = a(181248, [4, 128], BF16)
        self.oT = a(0, [NSL, Tc], BF16)
        self.r1a = a(32768, [NCH, HID], F32)
        self.xT = a(98304, [NSL, Tc], BF16)
        self.wo_b = [a(131072 + i * 4096, [HID], BF16) for i in range(8)]
        self.sqscr = a(163840, [HID], BF16)
        self.xs = [a(167936, [HID], F32)]
        self.gb = [a(131072, [8, Tc], BF16), a(147456, [8, Tc], BF16)]
        self.ffg = [a(0, [NSL, 128], BF16), a(4096, [NSL, 128], BF16)]
        self.ffu = [a(8192, [NSL, 128], BF16), a(12288, [NSL, 128], BF16)]
        self.sgf = a(16384, [Tc], F32)
        self.wd_all = ([a(20480 + i * 4096, [HID], BF16) for i in range(3)] +
                       [a(163840 + i * 4096, [HID], BF16) for i in range(5)])
        self.sqscr2 = a(131072, [HID], BF16)

        self.banks = [Bank(nc.alloc_psum_tensor(f"pb{i}", [128, 512], F32).ap())
                      for i in range(8)]

    def ab(self, off, shape, dtype):
        esz = mybir.dt.size(dtype)
        n = int(np.prod(shape))
        assert off + n * esz <= self.AR, f"arena overflow at {off}+{n*esz}"
        ap = self.arena.ap()[:, off:off + n * esz].bitcast(dtype)
        if len(shape) > 1:
            dims = " ".join(f"d{i}" for i in range(len(shape)))
            kw = {f"d{i}": int(s) for i, s in enumerate(shape)}
            ap = ap.rearrange(f"p ({dims}) -> p {dims}", **kw)
        b = Buf(ap, rng=(off, off + n * esz))
        self.alias_reg.append(b)
        return b

    def _write_waits(self, b, waits):
        if b.w is not None:
            waits.append(b.w)
        waits.extend(b.r_toks())
        if b.fresh and b.rng is not None:
            lo, hi = b.rng
            for o in self.alias_reg:
                if o is b or o.rng is None:
                    continue
                if o.rng[0] < hi and lo < o.rng[1]:
                    if o.w is not None:
                        waits.append(o.w)
                    waits.extend(o.r_toks())
            b.fresh = False

    def op(self, eng, fn, reads=(), writes=(), extra=(), inc=None, inc_by=1):
        if inc is None:
            inc = self.SEM_OF[eng]
        waits = list(extra)
        for b in reads:
            if b.w is not None:
                waits.append(b.w)
        for b in writes:
            self._write_waits(b, waits)
        tok = self.P.emit(eng, fn, waits, inc, inc_by)
        for b in reads:
            b.add_r(tok)
        for b in writes:
            b.w = tok
            b.r = {}
        return tok

    def dma(self, eng, out_ap, in_ap, reads=(), writes=(), extra=(),
            slow_ok=False):
        q = "lw" if eng == "gpsimd" else "ld"
        sem, pre = self.P.lane(q)
        waits = list(extra)
        if pre is not None:
            waits.append(pre)
        for b in reads:
            if b.w is not None:
                waits.append(b.w)
        for b in writes:
            self._write_waits(b, waits)
        tok = self.P.emit(eng, lambda e: e.dma_start(
            out=out_ap, in_=in_ap, allow_slow_non_contiguous=slow_ok),
            waits, sem, 16)
        for b in reads:
            b.add_r(tok)
        for b in writes:
            b.w = tok
            b.r = {}
        return tok

    def dma_store(self, out_ap, in_ap, bufs, extra=(), eng="sync"):
        sem, pre = self.P.lane("st")
        waits = list(extra)
        if pre is not None:
            waits.append(pre)
        for b in bufs:
            if b.w is not None:
                waits.append(b.w)
        tok = self.P.emit(eng, lambda e: e.dma_start(out=out_ap, in_=in_ap),
                          waits, sem, 16)
        for b in bufs:
            b.add_r(tok)
        return tok

    def T_(self, fn, **kw):
        return self.op("tensor", fn, **kw)

    def V_(self, fn, **kw):
        return self.op("vector", fn, **kw)

    def S_(self, fn, **kw):
        return self.op("scalar", fn, **kw)

    def G_(self, fn, **kw):
        return self.op("gpsimd", fn, **kw)

    def emit_warmup(self, n=30):
        bk = self.banks[4]
        for i in range(n):
            self.T_(lambda t, bk=bk: t.matmul(
                bk.ap[:, :], self.identb.ap[:, :],
                self.bg2T.ap[:, 0:512], start=True, stop=True),
                reads=[self.identb, self.bg2T],
                writes=[bk] if i == 0 else (), extra=() if i == 0 else [bk.w])

    def tap(self, name, buf, shape, dtype):
        if name not in WANT_DBG:
            return
        d = self.nc.dram_tensor("dbg_" + name, shape, dtype,
                                kind="ExternalOutput").ap()
        self.dbg[name] = d
        self.dma_store(d[:, :] if len(shape) == 2 else d, buf.ap, [buf])

    def build(self):
        if STAGES == "B4o":
            self.p0_constants()
            self.b4_ffn()
            self.finish()
            with self.nc.Block() as block:
                self.P.realize(block)
            return self.nc
        self.p0_constants()
        self.a1_projections()
        self.a2_aux()
        if STAGES != "A1":
            self.a3_l2norm()
            self.a4_states()
        if STAGES in ("A5", "B", "B3", "all"):
            self.a5_combine()
        if STAGES in ("B", "B3", "all"):
            self.b_scan()
        if STAGES in ("B3", "all"):
            self.b3_wo_ln1()
        if STAGES == "all":
            self.b4_ffn()
        self.finish()
        with self.nc.Block() as block:
            self.P.realize(block)
        return self.nc

    def p0_constants(self):
        G_, V_, S_, T_ = self.G_, self.V_, self.S_, self.T_
        I = self.I

        def mask(buf, fill_first, pat, cmp, fill, cmul, base=0):
            G_(lambda g: g.memset(buf.ap[:, :], fill_first), writes=[buf])
            G_(lambda g: g.affine_select(buf.ap[:, :], buf.ap[:, :], pattern=pat,
                                         compare_op=cmp, fill=fill,
                                         channel_multiplier=cmul, base=base),
               writes=[buf])

        mask(self.trilS, SCALE, [[1, 128]], ALU.is_ge, 0.0, -1)
        mask(self.tril1, 1.0, [[1, 128]], ALU.is_ge, 0.0, -1)
        for t_ in (self.identb,):
            mask(t_, 0.0, [[-1, 128]], ALU.not_equal, 1.0, 1)
        mask(self.identf, 0.0, [[-1, 16]], ALU.not_equal, 1.0, 1)
        mask(self.U2n, 1.0, [[1, 128]], ALU.is_ge, 0.0, -1)
        mask(self.Uh, 1.0, [[0, 128]], ALU.is_ge, 0.0, -1, base=63)
        self.V_(lambda v: v.tensor_sub(self.U2n.ap[:, :], self.U2n.ap[:, :],
                                       self.Uh.ap[:, :]),
                reads=[self.Uh], writes=[self.U2n])
        mask(self.U3n, 1.0, [[-1, 128]], ALU.is_ge, 0.0, 1, base=-1)
        G_(lambda g: g.memset(self.onescol.ap[:, :], 1.0), writes=[self.onescol])
        mask(self.halfcol, 1.0, [[0, 1]], ALU.is_ge, 0.0, -1, base=63)
        G_(lambda g: g.memset(self.ones1p.ap[:, :], 1.0), writes=[self.ones1p])
        self.V_(lambda v: v.tensor_copy(self.colpair.ap[:, 0:1],
                                        self.onescol.ap[:, :]),
                reads=[self.onescol], writes=[self.colpair])
        self.V_(lambda v: v.tensor_copy(self.colpair.ap[:, 1:2],
                                        self.halfcol.ap[:, :]),
                reads=[self.halfcol], writes=[self.colpair])
        for i, val in enumerate((1e-6, 1.0, 1e-5)):
            G_(lambda g, i=i, val=val: g.memset(self.cpool.ap[:, i:i + 1], val),
               writes=[self.cpool])

        for i, cwn in enumerate(("cq", "ck", "cv", "cv")):
            self.dma("sync", self.cwt.ap[:, :, 4 * i:4 * i + 4],
                     I[cwn].rearrange("(s p) t -> p s t", p=128),
                     writes=[self.cwt])
        self.dma("gpsimd", self.Wf2b.ap[:, :], I["Wf2"][:, :], writes=[self.Wf2b])
        self.dma("gpsimd", self.Wg2b.ap[:, :], I["Wg2"][:, :], writes=[self.Wg2b])
        self.dma("gpsimd", self.Wbb.ap[:, :, :],
                 I["Wb"].rearrange("(s p) h -> p s h", p=128), writes=[self.Wbb])
        self.dma("sync", self.nwcol.ap[:, :],
                 I["normw"].rearrange("(p o) -> p o", o=1), writes=[self.nwcol])
        self.dma("sync", self.MmatS.ap[:, :], I["Mmat"][:, :], writes=[self.MmatS])
        self.dma("sync", self.bgrow.ap[0:1, :],
                 I["bg2"].rearrange("(o f) -> o f", o=1), writes=[self.bgrow])
        for j in range(4):
            bk = self.banks[4 + j]
            self.T_(lambda t, j=j, bk=bk: t.matmul(
                bk.ap[:, :], self.ones1p.ap[:, :],
                self.bgrow.ap[0:1, j * 512:(j + 1) * 512], start=True, stop=True),
                reads=[self.ones1p, self.bgrow], writes=[bk])
            self.V_(lambda v, j=j, bk=bk: v.tensor_copy(
                self.bg2T.ap[:, j * 512:(j + 1) * 512], bk.ap[:, :]),
                reads=[bk], writes=[self.bg2T])
        self.dma("gpsimd", self.hTb.ap[:, :, :],
                 I["hT_ext"].rearrange("(s p) t -> p s t", p=128),
                 writes=[self.hTb])

    def a1_projections(self):
        T_, V_, S_ = self.T_, self.V_, self.S_
        I = self.I
        streams = [("q", "Wq", 0), ("k", "Wk", 1), ("v", "Wv", 2), ("w", "Ww", 3)]
        for sname, wn, sidx in streams:
            for fs in range(NSL):
                par = (sidx * NSL + fs) % 2
                wb = self.wsl[par]
                self.dma("gpsimd", wb.ap[:, :, :],
                         I[wn][:, fs * 128:(fs + 1) * 128].rearrange(
                             "(a p) c -> p a c", p=128), writes=[wb])
                for g in range(4):
                    t0 = g * 256
                    pb = self.banks[g]
                    for a in range(NSL):
                        self.T_(lambda t, a=a, pb=pb, t0=t0, wb=wb: t.matmul(
                            pb.ap[:, 0:259], wb.ap[:, a, :],
                            self.hTb.ap[:, a, t0:t0 + 259],
                            start=(a == 0), stop=(a == NSL - 1)),
                            reads=[wb, self.hTb], writes=[pb])
                    cp = g % 2
                    co = self.cvo[cp]
                    cb = sidx * 4
                    self.V_(lambda v, pb=pb, co=co, fs=fs, cb=cb: v.tensor_scalar(
                        co.ap[:, 0:256], pb.ap[:, 0:256],
                        self.cwt.ap[:, fs, cb:cb + 1], None, ALU.mult),
                        reads=[pb, self.cwt], writes=[co])
                    for i in range(1, 4):
                        self.V_(lambda v, pb=pb, co=co, fs=fs, cb=cb, i=i:
                                v.scalar_tensor_tensor(
                                    co.ap[:, 0:256], pb.ap[:, i:256 + i],
                                    self.cwt.ap[:, fs, cb + i:cb + i + 1],
                                    co.ap[:, 0:256], ALU.mult, ALU.add),
                                reads=[pb, self.cwt], writes=[co])
                    if sidx == 0:
                        self.S_(lambda s, co=co: s.activation(
                            co.ap[:, 0:256], co.ap[:, 0:256], ACT.Silu),
                            writes=[co])
                        self.V_(lambda v, co=co, fs=fs, t0=t0: v.tensor_scalar(
                            self.qc.ap[:, fs, t0:t0 + 256], co.ap[:, 0:256],
                            SCALE, None, ALU.mult), reads=[co],
                            writes=[self.qc])
                    elif sidx == 1:
                        self.S_(lambda s, co=co, fs=fs, t0=t0: s.activation(
                            self.kc.ap[:, fs, t0:t0 + 256], co.ap[:, 0:256],
                            ACT.Silu), reads=[co], writes=[self.kc])
                    else:
                        so = self.svo[cp]
                        self.S_(lambda s, co=co, so=so: s.activation(
                            so.ap[:, 0:256], co.ap[:, 0:256], ACT.Silu),
                            reads=[co], writes=[so])
                        dst = self.vc if sidx == 2 else self.wc
                        tb = self.banks[4 + cp]
                        for hf in range(2):
                            self.T_(lambda t, tb=tb, so=so, hf=hf: t.transpose(
                                tb.bf[:, hf * 128:(hf + 1) * 128],
                                so.ap[:, hf * 128:hf * 128 + 128],
                                self.identb.ap[:, :]),
                                reads=[so, self.identb],
                                writes=[tb])
                        self.V_(lambda v, tb=tb, dst=dst, g=g, fs=fs: v.tensor_copy(
                            dst.ap[:, 2 * g:2 * g + 2, fs * 128:(fs + 1) * 128],
                            tb.bf[:, 0:256].rearrange("p (a b) -> p a b", a=2)),
                            reads=[tb], writes=[dst])
        self.tap("qc", self.qc, [128, NSL, Tc], BF16)
        self.tap("kc", self.kc, [128, NSL, Tc], BF16)
        self.tap("vc", self.vc, [128, NCH, HID], BF16)
        self.tap("wcr", self.wc, [128, NCH, HID], BF16)

    def a2_aux(self):
        T_, V_, S_ = self.T_, self.V_, self.S_
        I = self.I
        for (dst, wn) in ((self.f1c, "Wf1"), (self.G1c, "Wg1")):
            wb = self.wsl[0]
            self.dma("gpsimd", wb.ap[:, :, :],
                     I[wn].rearrange("(a p) c -> p a c", p=128), writes=[wb])
            for half in range(2):
                pb = self.banks[4 + half]
                for a in range(NSL):
                    self.T_(lambda t, a=a, pb=pb, half=half, wb=wb: t.matmul(
                        pb.ap[:, :], wb.ap[:, a, :],
                        self.hTb.ap[:, a, 3 + half * 512:3 + (half + 1) * 512],
                        start=(a == 0), stop=(a == NSL - 1)),
                        reads=[wb, self.hTb], writes=[pb])
                self.V_(lambda v, pb=pb, dst=dst, half=half: v.tensor_copy(
                    dst.ap[:, half * 512:(half + 1) * 512], pb.ap[:, :]),
                    reads=[pb], writes=[dst])
        pb = self.banks[6]
        for half in range(2):
            for a in range(NSL):
                self.T_(lambda t, a=a, half=half, pb=pb: t.matmul(
                    pb.ap[0:16, 0:512], self.Wbb.ap[:, a, :],
                    self.hTb.ap[:, a, 3 + half * 512:3 + (half + 1) * 512],
                    start=(a == 0), stop=(a == NSL - 1)),
                    reads=[self.Wbb, self.hTb], writes=[pb])
            self.V_(lambda v, pb=pb, half=half: v.tensor_copy(
                self.betaR.ap[0:16, half * 512:(half + 1) * 512],
                pb.ap[0:16, 0:512]),
                reads=[pb], writes=[self.betaR])
        for ch in range(NCH):
            pb7 = self.banks[7]
            self.T_(lambda t, ch=ch, pb7=pb7: t.transpose(
                pb7.ap[:, 0:16], self.betaR.ap[0:16, ch * 128:(ch + 1) * 128],
                self.identf.ap[:, :]),
                reads=[self.betaR, self.identf], writes=[pb7])
            self.S_(lambda s, ch=ch, pb7=pb7: s.activation(
                self.beta_tm.ap[:, ch, :], pb7.ap[:, 0:16], ACT.Exp, scale=-1.0),
                reads=[pb7], writes=[self.beta_tm])
            self.V_(lambda v, ch=ch: v.tensor_scalar(
                self.beta_tm.ap[:, ch, :], self.beta_tm.ap[:, ch, :], 1.0, None,
                ALU.add), writes=[self.beta_tm])
            self.V_(lambda v, ch=ch: v.reciprocal(
                self.beta_tm.ap[:, ch, :], self.beta_tm.ap[:, ch, :]),
                writes=[self.beta_tm])
        self.tap("f1c", self.f1c, [128, Tc], BF16)
        self.tap("G1c", self.G1c, [128, Tc], BF16)
        self.tap("beta", self.beta_tm, [128, NCH, H], F32)

    def a3_l2norm(self):
        for ch in range(NCH):
            self.V_(lambda v, ch=ch: v.tensor_mul(
                self.sq2.ap[:, :], self.wc.ap[:, ch, :], self.wc.ap[:, ch, :]),
                reads=[self.wc], writes=[self.sq2])
            self.V_(lambda v: v.tensor_reduce(
                self.rst.ap[:, :], self.sq2.ap[:, :].rearrange(
                    "p (h m) -> p h m", h=H), AX.X, ALU.add),
                reads=[self.sq2], writes=[self.rst])
            self.S_(lambda s: s.activation(self.rst.ap[:, :], self.rst.ap[:, :],
                                           ACT.Ln, bias=self.cpool.ap[:, 0:1]),
                    reads=[self.cpool], writes=[self.rst])
            self.S_(lambda s: s.activation(self.rst.ap[:, :], self.rst.ap[:, :],
                                           ACT.Exp, scale=-0.5), writes=[self.rst])
            self.V_(lambda v, ch=ch: v.tensor_mul(
                self.rst.ap[:, :], self.rst.ap[:, :], self.beta_tm.ap[:, ch, :]),
                reads=[self.beta_tm], writes=[self.rst])
            self.V_(lambda v, ch=ch: v.tensor_tensor(
                self.wc.ap[:, ch, :].rearrange("p (h m) -> p h m", h=H),
                self.wc.ap[:, ch, :].rearrange("p (h m) -> p h m", h=H),
                bc_in(self.rst.ap[:, :], 128), ALU.mult),
                reads=[self.rst], writes=[self.wc])
        self.tap("wcn", self.wc, [128, NCH, HID], BF16)

    def emit_sp(self, sp, ch):
        for j in range(4):
            pb = self.banks[j]
            self.T_(lambda t, j=j, ch=ch, pb=pb: t.matmul(
                pb.ap[:, :], self.f1c.ap[:, ch * 128:(ch + 1) * 128],
                self.Wf2b.ap[:, j * 512:(j + 1) * 512], start=True, stop=True),
                reads=[self.f1c, self.Wf2b], writes=[pb])
            self.S_(lambda s, j=j, pb=pb, sp=sp: s.activation(
                sp.ap[:, j * 512:(j + 1) * 512], pb.ap[:, :], ACT.Exp, scale=-1.0),
                reads=[pb], writes=[sp])
            self.S_(lambda s, j=j, sp=sp: s.activation(
                sp.ap[:, j * 512:(j + 1) * 512], sp.ap[:, j * 512:(j + 1) * 512],
                ACT.Ln, bias=self.cpool.ap[:, 1:2]),
                reads=[self.cpool], writes=[sp])

    def emit_cumexp(self, U, sp, dst_scales):
        for j in range(4):
            pb = self.banks[j]
            self.T_(lambda t, j=j, pb=pb, U=U, sp=sp: t.matmul(
                pb.ap[:, :], U.ap[:, :], sp.ap[:, j * 512:(j + 1) * 512],
                start=True, stop=True), reads=[U, sp], writes=[pb])
            for (dst, sc) in dst_scales:
                self.S_(lambda s, j=j, pb=pb, dst=dst, sc=sc: s.activation(
                    dst.ap[:, j * 512:(j + 1) * 512], pb.ap[:, :], ACT.Exp,
                    scale=sc), reads=[pb], writes=[dst])

    def emit_cols(self, sp, col, dst_ap, bank_i=5):
        pb = self.banks[bank_i]
        for h in range(H):
            self.T_(lambda t, h=h, pb=pb, sp=sp, col=col: t.matmul(
                pb.ap[:, h:h + 1], sp.ap[:, h * 128:(h + 1) * 128], col.ap[:, :],
                start=True, stop=True),
                reads=[sp, col], writes=[pb])
        return self.S_(lambda s, pb=pb, dst_ap=dst_ap: s.activation(
            dst_ap, pb.ap[:, 0:H], ACT.Exp, scale=-1.0),
            reads=[pb, self.ecol], writes=[self.ecol])

    def emit_cols2(self, sp, bank_i=5):
        pb = self.banks[bank_i]
        for h in range(H):
            self.T_(lambda t, h=h, pb=pb, sp=sp: t.matmul(
                pb.ap[:, 2 * h:2 * h + 2], sp.ap[:, h * 128:(h + 1) * 128],
                self.colpair.ap[:, :], start=True, stop=True),
                reads=[sp, self.colpair], writes=[pb])
        ec = self.ecol.ap[:, 0:2 * H].rearrange("p (a b) -> p b a", a=2) \
            if False else None
        pv = pb.ap[:, 0:2 * H].rearrange("p (h w) -> p w h", h=H)
        dv = self.ecol.ap[:, 0:2 * H].rearrange("p (w h) -> p w h", w=2)
        self.S_(lambda s: s.activation(dv, pv, ACT.Exp, scale=-1.0),
                reads=[pb], writes=[self.ecol])

    def emit_diag(self, dst, cols_ap, n=H, reads=()):
        for h in range(n):
            self.V_(lambda v, h=h, dst=dst, cols_ap=cols_ap: v.tensor_scalar(
                dst.ap[:, h, :], self.identb.ap[:, :],
                cols_ap[:, h:h + 1], None, ALU.mult),
                reads=[self.identb] + list(reads),
                writes=[dst])

    def emit_ktm(self, dst, ch, bank0=6):
        for g8 in range(2):
            pbT = self.banks[bank0 + g8]
            for hi in range(8):
                h = g8 * 8 + hi
                self.T_(lambda t, h=h, ch=ch, pbT=pbT, hi=hi: t.transpose(
                    pbT.bf[:, hi * 128:(hi + 1) * 128],
                    self.kc.ap[:, h, ch * 128:(ch + 1) * 128],
                    self.identb.ap[:, :]),
                    reads=[self.kc, self.identb], writes=[pbT])
            self.V_(lambda v, g8=g8, pbT=pbT, dst=dst: v.tensor_copy(
                dst.ap[:, g8 * 8:(g8 + 1) * 8, :],
                pbT.bf[:, 0:1024].rearrange("p (a b) -> p a b", a=8)),
                reads=[pbT], writes=[dst])

    def emit_states(self, ktm, bwdb, diag, ch, bank0=0):
        for g8 in range(2):
            evac_w = []
            for hi in range(8):
                h = g8 * 8 + hi
                bsk = self.banks[bank0 + hi // 4]
                bsv = self.banks[bank0 + 2 + hi // 4]
                sl = (hi % 4) * 128
                self.T_(lambda t, h=h, bsk=bsk, sl=sl, bwdb=bwdb, ktm=ktm:
                        t.matmul(bsk.ap[:, sl:sl + 128],
                                 bwdb.ap[:, h * 128:(h + 1) * 128],
                                 ktm.ap[:, h, :], start=True,
                                 stop=False),
                        reads=[bwdb, ktm], writes=[bsk])
                self.T_(lambda t, h=h, bsk=bsk, sl=sl, diag=diag: t.matmul(
                    bsk.ap[:, sl:sl + 128], diag.ap[:, h, :],
                    self.SkTs.ap[:, h, :], start=False, stop=True),
                    reads=[diag, self.SkTs], writes=[bsk])
                self.T_(lambda t, h=h, bsv=bsv, sl=sl, ch=ch, bwdb=bwdb: t.matmul(
                    bsv.ap[:, sl:sl + 128], bwdb.ap[:, h * 128:(h + 1) * 128],
                    self.vc.ap[:, ch, h * 128:(h + 1) * 128],
                    start=True, stop=False),
                    reads=[bwdb, self.vc], writes=[bsv])
                tok = self.T_(lambda t, h=h, bsv=bsv, sl=sl, diag=diag: t.matmul(
                    bsv.ap[:, sl:sl + 128], diag.ap[:, h, :],
                    self.Svs.ap[:, h, :], start=False, stop=True),
                    reads=[diag, self.Svs], writes=[bsv])
                evac_w.append(tok)
            for bi in range(2):
                bsk = self.banks[bank0 + bi]
                bsv = self.banks[bank0 + 2 + bi]
                hlo = g8 * 8 + bi * 4
                self.V_(lambda v, bsk=bsk, hlo=hlo: v.tensor_copy(
                    self.SkTs.ap[:, hlo:hlo + 4, :],
                    bsk.ap[:, :].rearrange("p (a b) -> p a b", a=4)),
                    reads=[bsk], writes=[self.SkTs], extra=evac_w)
                self.V_(lambda v, bsv=bsv, hlo=hlo: v.tensor_copy(
                    self.Svs.ap[:, hlo:hlo + 4, :],
                    bsv.ap[:, :].rearrange("p (a b) -> p a b", a=4)),
                    reads=[bsv], writes=[self.Svs], extra=evac_w)

    def a4_states(self):
        self.V_(lambda v: v.memset(self.SkTs.ap[:, :, :], 0.0), writes=[self.SkTs])
        self.V_(lambda v: v.memset(self.Svs.ap[:, :, :], 0.0), writes=[self.Svs])
        self.V_(lambda v: v.memset(self.Btn.ap[:, :], 0.0), writes=[self.Btn])
        for ch in range(NCH):
            self.emit_sp(self.spA, ch)
            pb5 = self.banks[5]
            for h in range(H):
                self.T_(lambda t, h=h, pb5=pb5: t.matmul(
                    pb5.ap[:, h:h + 1], self.spA.ap[:, h * 128:(h + 1) * 128],
                    self.onescol.ap[:, :], start=True, stop=True),
                    reads=[self.spA, self.onescol],
                    writes=[pb5])
            self.S_(lambda s, pb5=pb5: s.activation(
                self.ecol.ap[:, 0:H], pb5.ap[:, 0:H], ACT.Exp, scale=-1.0),
                reads=[pb5], writes=[self.ecol])
            self.V_(lambda v, pb5=pb5: v.tensor_add(
                self.Btn.ap[:, :], self.Btn.ap[:, :], pb5.ap[:, 0:H]),
                reads=[pb5], writes=[self.Btn])
            self.emit_cumexp(self.U3n, self.spA, [(self.escrA, -1.0)])
            self.V_(lambda v, ch=ch: v.tensor_mul(
                self.bwdA.ap[:, :], self.wc.ap[:, ch, :], self.escrA.ap[:, :]),
                reads=[self.wc, self.escrA], writes=[self.bwdA])
            self.emit_diag(self.diagA, self.ecol.ap[:, 0:H], reads=[self.ecol])
            self.emit_ktm(self.ktmA, ch)
            self.emit_states(self.ktmA, self.bwdA, self.diagA, ch)
        self.tap("SkT_loc", self.SkTs, [128, H, 128], BF16)
        self.tap("Sv_loc", self.Svs, [128, H, 128], BF16)
        self.tap("Btn", self.Btn, [128, H], F32)

    def a5_combine(self):
        V_, S_, T_ = self.V_, self.S_, self.T_
        pubv = self.pub1.rearrange("p (h two d) -> p h two d", h=H, two=2)
        t_p1a = self.dma_store(pubv[:, :, 0, :], self.SkTs.ap[:, :, :],
                               [self.SkTs])
        t_p1b = self.dma_store(pubv[:, :, 1, :], self.Svs.ap[:, :, :],
                               [self.Svs])
        t_p2 = self.dma_store(self.pub2[:, :], self.Btn.ap[:, :], [self.Btn])
        t_c1 = self.P.emit("gpsimd", lambda g: g.collective_compute(
            "AllGather", ALU.bypass, replica_groups=[list(range(NCORES))],
            ins=[self.pub1[:, :]], outs=[self.agg1[:, :]]),
            [t_p1a, t_p1b], "cs", 1)
        t_c2 = self.P.emit("gpsimd", lambda g: g.collective_compute(
            "AllGather", ALU.bypass, replica_groups=[list(range(NCORES))],
            ins=[self.pub2[:, :]], outs=[self.agg2[:, :]]),
            [t_p2], "cs", 1)
        augv = self.augB.ap.rearrange("p (m h) -> p m h", m=128)
        self.V_(lambda v: v.memset(self.augB.ap[0:9, :], 1.0), writes=[self.augB])
        self.dma("sync", augv[0:8, :, :],
                 self.agg2.rearrange("(c m) h -> c m h", m=128),
                 writes=[self.augB], extra=[t_c2])
        pb = self.banks[4]
        for h in range(H):
            self.T_(lambda t, h=h, pb=pb, augv=augv: t.matmul(
                pb.ap[:, (h % 8) * 8:(h % 8) * 8 + 8],
                augv[0:9, :, h:h + 1],
                self.MmatS.ap[:, :], start=True, stop=True),
                reads=[self.augB, self.MmatS],
                writes=[pb])
            if h % 8 == 7:
                hlo = h - 7
                self.S_(lambda s, pb=pb, hlo=hlo: s.activation(
                    self.WcolS.ap[:, hlo:hlo + 8, :].rearrange("p a b -> p (a b)"),
                    pb.ap[:, 0:64], ACT.Exp, scale=-1.0),
                    reads=[pb], writes=[self.WcolS])
        agg1v = self.agg1.rearrange("(c p) x -> p c x", p=128).rearrange(
            "p c (h d) -> p c h d", h=H)
        for hg in range(4):
            self.dma("sync", self.aggS.ap[:, :, :, :],
                     agg1v[:, :, hg * 4:(hg + 1) * 4, :],
                     writes=[self.aggS], extra=[t_c1])
            for hi in range(4):
                h = hg * 4 + hi
                self.emit_diag(self.diagC, self.WcolS.ap[:, h, :], n=NCORES,
                               reads=[self.WcolS])
                bsk = self.banks[0]
                bsv = self.banks[1]
                for c in range(NCORES):
                    self.T_(lambda t, hi=hi, c=c, bsk=bsk: t.matmul(
                        bsk.ap[:, 0:128], self.diagC.ap[:, c, :],
                        self.aggS.ap[:, c, hi, 0:128], start=(c == 0),
                        stop=(c == NCORES - 1)),
                        reads=[self.diagC, self.aggS],
                        writes=[bsk])
                    self.T_(lambda t, hi=hi, c=c, bsv=bsv: t.matmul(
                        bsv.ap[:, 0:128], self.diagC.ap[:, c, :],
                        self.aggS.ap[:, c, hi, 128:256], start=(c == 0),
                        stop=(c == NCORES - 1)),
                        reads=[self.diagC, self.aggS],
                        writes=[bsv])
                self.V_(lambda v, h=h, bsk=bsk: v.tensor_copy(
                    self.SkTs.ap[:, h, :], bsk.ap[:, 0:128]),
                    reads=[bsk], writes=[self.SkTs])
                self.V_(lambda v, h=h, bsv=bsv: v.tensor_copy(
                    self.Svs.ap[:, h, :], bsv.ap[:, 0:128]),
                    reads=[bsv], writes=[self.Svs])
        self.tap("SkT_in", self.SkTs, [128, H, 128], BF16)
        self.tap("Sv_in", self.Svs, [128, H, 128], BF16)

    def b_scan(self):
        T_, V_, S_ = self.T_, self.V_, self.S_
        for ch in range(NCH):
            self.emit_sp(self.spT, ch)
            self.emit_cumexp(self.U2n, self.spT,
                             [(self.expBp, -1.0), (self.escr, 1.0)])
            self.V_(lambda v, ch=ch: v.tensor_mul(
                self.bwR.ap[:, :], self.wc.ap[:, ch, :], self.escr.ap[:, :]),
                reads=[self.wc, self.escr], writes=[self.bwR])
            self.emit_cumexp(self.U3n, self.spT, [(self.escr, -1.0)])
            self.V_(lambda v, ch=ch: v.tensor_mul(
                self.bwd.ap[:, :], self.wc.ap[:, ch, :], self.escr.ap[:, :]),
                reads=[self.wc, self.escr], writes=[self.bwd])
            self.emit_cols2(self.spT)
            for j in range(4):
                pb = self.banks[j]
                self.T_(lambda t, j=j, ch=ch, pb=pb: t.matmul(
                    pb.ap[:, :], self.G1c.ap[:, ch * 128:(ch + 1) * 128],
                    self.Wg2b.ap[:, j * 512:(j + 1) * 512], start=True,
                    stop=True), reads=[self.G1c, self.Wg2b], writes=[pb])
                self.V_(lambda v, j=j, pb=pb: v.tensor_tensor(
                    self.sgrp.ap[:, :], pb.ap[:, :],
                    self.bg2T.ap[:, j * 512:(j + 1) * 512], ALU.add),
                    reads=[pb, self.bg2T], writes=[self.sgrp])
                self.S_(lambda s: s.activation(self.sgrp.ap[:, :],
                                               self.sgrp.ap[:, :], ACT.Exp,
                                               scale=-1.0), writes=[self.sgrp])
                self.V_(lambda v: v.tensor_scalar(self.sgrp.ap[:, :],
                                                  self.sgrp.ap[:, :], 1.0, None,
                                                  ALU.add), writes=[self.sgrp])
                self.V_(lambda v: v.reciprocal(self.sgrp.ap[:, :],
                                               self.sgrp.ap[:, :]),
                        writes=[self.sgrp])
                self.V_(lambda v, j=j: v.tensor_copy(
                    self.gsig.ap[:, j * 512:(j + 1) * 512], self.sgrp.ap[:, :]),
                    reads=[self.sgrp], writes=[self.gsig])
            self.emit_diag(self.diagB, self.ecol.ap[:, 0:H], reads=[self.ecol])
            for g4 in range(4):
                hlo = g4 * 4
                qb = 4 if g4 % 2 == 0 else 0
                bPT = self.banks[qb]
                bS = self.banks[qb + 1]
                bTR = self.banks[qb + 2]
                bO = self.banks[qb + 3]
                self.emit_diag(self.diagR, self.ecol.ap[:, H + hlo:H + hlo + 4],
                               n=4, reads=[self.ecol])
                for hi in range(4):
                    h = hlo + hi
                    self.T_(lambda t, h=h, hi=hi, bPT=bPT: t.matmul(
                        bPT.ap[:, hi * 128:(hi + 1) * 128],
                        self.diagR.ap[:, hi, :], self.SkTs.ap[:, h, :],
                        start=True, stop=True),
                        reads=[self.diagR, self.SkTs],
                        writes=[bPT])
                self.V_(lambda v, bPT=bPT: v.tensor_copy(
                    self.gSR.ap[:, :, :],
                    bPT.ap[:, :].rearrange("p (a b) -> p a b", a=4)),
                    reads=[bPT], writes=[self.gSR])
                for hi in range(4):
                    h = hlo + hi
                    self.T_(lambda t, h=h, hi=hi, ch=ch, bPT=bPT: t.matmul(
                        bPT.ap[:, hi * 128:(hi + 1) * 128],
                        self.kc.ap[:, h, ch * 128:(ch + 1) * 128],
                        self.qc.ap[:, h, ch * 128:(ch + 1) * 128],
                        start=True, stop=True),
                        reads=[self.kc, self.qc],
                        writes=[bPT])
                self.V_(lambda v, bPT=bPT: v.tensor_tensor(
                    self.gPM.ap[:, :, :],
                    bPT.ap[:, :].rearrange("p (a b) -> p a b", a=4),
                    bc_mid(self.trilS.ap[:, :], 4), ALU.mult),
                    reads=[bPT, self.trilS], writes=[self.gPM])
                for hi in range(4):
                    h = hlo + hi
                    self.T_(lambda t, h=h, hi=hi, bTR=bTR: t.transpose(
                        bTR.bf[:, 512 + hi * 128:512 + (hi + 1) * 128],
                        self.bwR.ap[:, h * 128:(h + 1) * 128],
                        self.identb.ap[:, :]),
                        reads=[self.bwR, self.identb],
                        writes=[bTR])
                self.V_(lambda v, bTR=bTR: v.tensor_copy(
                    self.gRT.ap[:, :, :],
                    bTR.bf[:, 512:1024].rearrange("p (a b) -> p a b", a=4)),
                    reads=[bTR], writes=[self.gRT])
                for hi in range(4):
                    h = hlo + hi
                    self.T_(lambda t, h=h, hi=hi, bS=bS: t.matmul(
                        bS.ap[:, hi * 128:(hi + 1) * 128], self.gPM.ap[:, hi, :],
                        self.bwR.ap[:, h * 128:(h + 1) * 128],
                        start=True, stop=False),
                        reads=[self.gPM, self.bwR],
                        writes=[bS])
                    self.T_(lambda t, h=h, hi=hi, ch=ch, bS=bS: t.matmul(
                        bS.ap[:, hi * 128:(hi + 1) * 128],
                        self.qc.ap[:, h, ch * 128:(ch + 1) * 128],
                        self.gSR.ap[:, hi, :], start=False, stop=True),
                        reads=[self.qc, self.gSR], writes=[bS])
                self.V_(lambda v, bS=bS, hlo=hlo: v.tensor_tensor(
                    self.sgrp.ap[:, :], bS.ap[:, :],
                    self.expBp.ap[:, hlo * 128:(hlo + 4) * 128], ALU.mult),
                    reads=[bS, self.expBp], writes=[self.sgrp])
                self.V_(lambda v: v.tensor_reduce(
                    self.smx.ap[:, 0:4], self.sgrp.ap[:, :].rearrange(
                        "p (a b) -> p a b", a=4), AX.X, ALU.max, negate=True),
                    reads=[self.sgrp], writes=[self.smx])
                for hi in range(4):
                    self.S_(lambda s, hi=hi: s.activation(
                        self.sgrp.ap[:, hi * 128:(hi + 1) * 128],
                        self.sgrp.ap[:, hi * 128:(hi + 1) * 128], ACT.Exp,
                        bias=self.smx.ap[:, hi:hi + 1]),
                        reads=[self.smx], writes=[self.sgrp])
                self.V_(lambda v: v.tensor_reduce(
                    self.smx.ap[:, 4:8], self.sgrp.ap[:, :].rearrange(
                        "p (a b) -> p a b", a=4), AX.X, ALU.add),
                    reads=[self.sgrp], writes=[self.smx])
                self.V_(lambda v: v.reciprocal(self.smx.ap[:, 4:8],
                                               self.smx.ap[:, 4:8]),
                        writes=[self.smx])
                self.V_(lambda v: v.tensor_tensor(
                    self.sgrp.ap[:, :].rearrange("p (a b) -> p a b", a=4),
                    self.sgrp.ap[:, :].rearrange("p (a b) -> p a b", a=4),
                    bc_in(self.smx.ap[:, 4:8], 128), ALU.mult),
                    reads=[self.smx], writes=[self.sgrp])
                self.V_(lambda v, hlo=hlo: v.tensor_tensor(
                    self.gAB.ap[:, :, :].rearrange("p a b -> p (a b)"),
                    self.sgrp.ap[:, :],
                    self.expBp.ap[:, hlo * 128:(hlo + 4) * 128], ALU.mult),
                    reads=[self.sgrp, self.expBp], writes=[self.gAB])
                for hi in range(4):
                    self.T_(lambda t, hi=hi, bTR=bTR: t.transpose(
                        bTR.bf[:, hi * 128:(hi + 1) * 128], self.gAB.ap[:, hi, :],
                        self.identb.ap[:, :]),
                        reads=[self.gAB, self.identb],
                        writes=[bTR])
                self.V_(lambda v, bTR=bTR: v.tensor_copy(
                    self.gAT.ap[:, :, :],
                    bTR.bf[:, 0:512].rearrange("p (a b) -> p a b", a=4)),
                    reads=[bTR], writes=[self.gAT])
                for hi in range(4):
                    self.T_(lambda t, hi=hi, bPT=bPT: t.matmul(
                        bPT.ap[:, hi * 128:(hi + 1) * 128], self.gRT.ap[:, hi, :],
                        self.gAT.ap[:, hi, :], start=True, stop=True),
                        reads=[self.gRT, self.gAT],
                        writes=[bPT])
                self.V_(lambda v, bPT=bPT: v.tensor_tensor(
                    self.gP2.ap[:, :, :],
                    bPT.ap[:, :].rearrange("p (a b) -> p a b", a=4),
                    bc_mid(self.tril1.ap[:, :], 4), ALU.mult),
                    reads=[bPT, self.tril1], writes=[self.gP2])
                self.V_(lambda v, hlo=hlo: v.tensor_tensor(
                    self.svr.ap[:, :, :], self.Svs.ap[:, hlo:hlo + 4, :],
                    bc_in(self.ecol.ap[:, H + hlo:H + hlo + 4], 128), ALU.mult),
                    reads=[self.Svs, self.ecol], writes=[self.svr])
                for hi in range(4):
                    h = hlo + hi
                    self.T_(lambda t, h=h, hi=hi, ch=ch, bO=bO: t.matmul(
                        bO.ap[:, hi * 128:(hi + 1) * 128], self.gP2.ap[:, hi, :],
                        self.vc.ap[:, ch, h * 128:(h + 1) * 128],
                        start=True, stop=False),
                        reads=[self.gP2, self.vc],
                        writes=[bO])
                    self.T_(lambda t, hi=hi, bO=bO: t.matmul(
                        bO.ap[:, hi * 128:(hi + 1) * 128], self.gAT.ap[:, hi, :],
                        self.svr.ap[:, hi, :], start=False, stop=True),
                        reads=[self.gAT, self.svr], writes=[bO])
                self.S_(lambda s, bO=bO: s.activation(
                    self.ogrp.ap[:, :], bO.ap[:, :], ACT.Square),
                    reads=[bO], writes=[self.ogrp])
                self.V_(lambda v: v.tensor_reduce(
                    self.rst.ap[:, 0:4], self.ogrp.ap[:, :].rearrange(
                        "p (a b) -> p a b", a=4), AX.X, ALU.add),
                    reads=[self.ogrp], writes=[self.rst])
                self.S_(lambda s: s.activation(self.rst.ap[:, 0:4],
                                               self.rst.ap[:, 0:4], ACT.Ln,
                                               scale=1.0 / 128.0,
                                               bias=self.cpool.ap[:, 2:3]),
                        reads=[self.cpool], writes=[self.rst])
                self.S_(lambda s: s.activation(self.rst.ap[:, 0:4],
                                               self.rst.ap[:, 0:4], ACT.Exp,
                                               scale=-0.5), writes=[self.rst])
                self.V_(lambda v, bO=bO: v.tensor_tensor(
                    self.ogrp.ap[:, :].rearrange("p (a b) -> p a b", a=4),
                    bO.ap[:, :].rearrange("p (a b) -> p a b", a=4),
                    bc_in(self.rst.ap[:, 0:4], 128), ALU.mult),
                    reads=[bO, self.rst], writes=[self.ogrp])
                self.V_(lambda v, hlo=hlo: v.tensor_tensor(
                    self.gOF.ap[:, :, :].rearrange("p a b -> p (a b)"),
                    self.ogrp.ap[:, :],
                    self.gsig.ap[:, hlo * 128:(hlo + 4) * 128], ALU.mult),
                    reads=[self.ogrp, self.gsig], writes=[self.gOF])
                for hi in range(4):
                    self.T_(lambda t, hi=hi, bTR=bTR: t.transpose(
                        bTR.bf[:, 512 + hi * 128:512 + (hi + 1) * 128],
                        self.gOF.ap[:, hi, :], self.identb.ap[:, :]),
                        reads=[self.gOF, self.identb],
                        writes=[bTR])
                self.V_(lambda v, bTR=bTR: v.tensor_copy(
                    self.oTst.ap[:, :, :],
                    bTR.bf[:, 512:1024].rearrange("p (a b) -> p a b", a=4)),
                    reads=[bTR], writes=[self.oTst])
                self.dma_store(
                    self.o_scr.rearrange("(s p) t -> p s t", p=128)[
                        :, hlo:hlo + 4, ch * 128:(ch + 1) * 128],
                    self.oTst.ap[:, :, :], [self.oTst])
            self.emit_ktm(self.ktm, ch, bank0=4)
            self.emit_states(self.ktm, self.bwd, self.diagB, ch, bank0=4)

    def b3_wo_ln1(self):
        T_, V_, S_ = self.T_, self.V_, self.S_
        I = self.I
        self.emit_warmup()
        st_toks = [(f"st{i}", self.P.cnt[f"st{i}"])
                   for i in range(self.P.LANES["st"]) if self.P.cnt[f"st{i}"]]
        self.dma("sync", self.oT.ap[:, :, :],
                 self.o_scr.rearrange("(s p) t -> p s t", p=128),
                 writes=[self.oT], extra=st_toks)
        for band in range(2):
            for i in range(8):
                sl = band * 8 + i
                wb = self.wo_b[i]
                self.dma("gpsimd", wb.ap[:, :],
                         I["Wo"][sl * 128:(sl + 1) * 128, :], writes=[wb])
                self.V_(lambda v, wb=wb: v.tensor_scalar(
                    wb.ap[:, :], wb.ap[:, :], self.nwcol.ap[:, 0:1], None,
                    ALU.mult), reads=[self.nwcol], writes=[wb])
            for tc in range(NCH):
                par = tc % 2
                if band == 0:
                    xb = self.xs[0]
                    self.dma("sync", xb.ap[:, :],
                             I["h_nat"][tc * 128:(tc + 1) * 128, :], writes=[xb])
                for j in range(4):
                    pb = self.banks[par * 4 + j]
                    for i in range(8):
                        sl = band * 8 + i
                        wb = self.wo_b[i]
                        self.T_(lambda t, i=i, j=j, tc=tc, sl=sl, pb=pb, wb=wb:
                                t.matmul(pb.ap[:, :],
                                         self.oT.ap[:, sl, tc * 128:(tc + 1) * 128],
                                         wb.ap[:, j * 512:(j + 1) * 512],
                                         start=(i == 0), stop=(i == 7)),
                                reads=[self.oT, wb],
                                writes=[pb])
                for j in range(4):
                    pb = self.banks[par * 4 + j]
                    if band == 0:
                        xb = self.xs[0]
                        self.V_(lambda v, j=j, tc=tc, pb=pb, xb=xb: v.tensor_tensor(
                            self.r1a.ap[:, tc, j * 512:(j + 1) * 512], pb.ap[:, :],
                            xb.ap[:, j * 512:(j + 1) * 512], ALU.add),
                            reads=[pb, xb], writes=[self.r1a])
                    else:
                        self.V_(lambda v, j=j, tc=tc, pb=pb: v.tensor_tensor(
                            self.r1a.ap[:, tc, j * 512:(j + 1) * 512],
                            self.r1a.ap[:, tc, j * 512:(j + 1) * 512],
                            pb.ap[:, :], ALU.add),
                            reads=[pb], writes=[self.r1a])
        for tc in range(NCH):
            self.emit_ln(tc, to_xT=True)
        self.tap("x1", self.r1a, [128, NCH, HID], F32)

    def emit_ln(self, tc, to_xT):
        V_, S_, T_ = self.V_, self.S_, self.T_
        acc = self.r1a
        sq = self.sqscr if to_xT else self.sqscr2
        self.V_(lambda v, tc=tc: v.tensor_reduce(
            self.lnc.ap[:, 0:1], acc.ap[:, tc, :], AX.X, ALU.add),
            reads=[acc], writes=[self.lnc])
        self.V_(lambda v: v.tensor_scalar(self.lnc.ap[:, 0:1], self.lnc.ap[:, 0:1],
                                          1.0 / HID, None, ALU.mult),
                writes=[self.lnc])
        self.S_(lambda s, tc=tc, sq=sq: s.activation(
            sq.ap[:, :], acc.ap[:, tc, :], ACT.Square,
            accum_out=self.lnc.ap[:, 1:2]),
            reads=[acc], writes=[sq, self.lnc])
        self.V_(lambda v: v.tensor_scalar(self.lnc.ap[:, 1:2], self.lnc.ap[:, 1:2],
                                          1.0 / HID, None, ALU.mult),
                writes=[self.lnc])
        self.V_(lambda v: v.tensor_mul(self.lnc.ap[:, 2:3], self.lnc.ap[:, 0:1],
                                       self.lnc.ap[:, 0:1]), writes=[self.lnc])
        self.V_(lambda v: v.tensor_sub(self.lnc.ap[:, 1:2], self.lnc.ap[:, 1:2],
                                       self.lnc.ap[:, 2:3]), writes=[self.lnc])
        self.S_(lambda s: s.activation(self.lnc.ap[:, 3:4], self.lnc.ap[:, 1:2],
                                       ACT.Ln, bias=self.cpool.ap[:, 2:3]),
                reads=[self.cpool], writes=[self.lnc])
        self.S_(lambda s: s.activation(self.lnc.ap[:, 3:4], self.lnc.ap[:, 3:4],
                                       ACT.Exp, scale=-0.5), writes=[self.lnc])
        self.V_(lambda v, tc=tc: v.tensor_scalar(
            acc.ap[:, tc, :], acc.ap[:, tc, :], self.lnc.ap[:, 0:1], None,
            ALU.subtract), reads=[self.lnc], writes=[acc])
        self.V_(lambda v, tc=tc: v.tensor_scalar(
            acc.ap[:, tc, :], acc.ap[:, tc, :], self.lnc.ap[:, 3:4], None,
            ALU.mult), reads=[self.lnc], writes=[acc])
        if not to_xT:
            return
        self.V_(lambda v, tc=tc, sq=sq: v.tensor_copy(sq.ap[:, :],
                                                      acc.ap[:, tc, :]),
                reads=[acc], writes=[sq])
        for g8 in range(2):
            pbT = self.banks[4 + g8]
            for hi in range(8):
                sl = g8 * 8 + hi
                self.T_(lambda t, sl=sl, hi=hi, pbT=pbT, sq=sq: t.transpose(
                    pbT.bf[:, hi * 128:(hi + 1) * 128],
                    sq.ap[:, sl * 128:(sl + 1) * 128], self.identb.ap[:, :]),
                    reads=[sq, self.identb], writes=[pbT])
            self.V_(lambda v, g8=g8, tc=tc, pbT=pbT: v.tensor_copy(
                self.xT.ap[:, g8 * 8:(g8 + 1) * 8, tc * 128:(tc + 1) * 128],
                pbT.bf[:, 0:1024].rearrange("p (a b) -> p a b", a=8)),
                reads=[pbT], writes=[self.xT])

    def b4_ffn(self):
        T_, V_, S_ = self.T_, self.V_, self.S_
        I = self.I

        def emit_gateup(hb, hs):
            hidx = hb * 8 + hs
            par = hs % 2
            fg, fu = self.ffg[par], self.ffu[par]
            gbb = self.gb[hb % 2]
            self.dma("gpsimd", fg.ap[:, :, :],
                     I["wgT"][:, hidx * 128:(hidx + 1) * 128].rearrange(
                         "(a p) c -> p a c", p=128), writes=[fg])
            self.dma("gpsimd", fu.ap[:, :, :],
                     I["wuT"][:, hidx * 128:(hidx + 1) * 128].rearrange(
                         "(a p) c -> p a c", p=128), writes=[fu])
            bg = self.banks[4 + par]
            bu = self.banks[6 + par]
            for half in range(2):
                for (bk, fw) in ((bg, fg), (bu, fu)):
                    for a in range(NSL):
                        self.T_(lambda t, a=a, half=half, bk=bk, fw=fw: t.matmul(
                            bk.ap[:, :], fw.ap[:, a, :],
                            self.xT.ap[:, a, half * 512:(half + 1) * 512],
                            start=(a == 0), stop=(a == NSL - 1)),
                            reads=[fw, self.xT], writes=[bk])
                self.S_(lambda s, bg=bg, half=half: s.activation(
                    self.sgf.ap[:, half * 512:(half + 1) * 512], bg.ap[:, :],
                    ACT.Silu), reads=[bg], writes=[self.sgf])
                self.V_(lambda v, bu=bu, hs=hs, half=half, gbb=gbb:
                        v.tensor_tensor(
                    gbb.ap[:, hs, half * 512:(half + 1) * 512], bu.ap[:, :],
                    self.sgf.ap[:, half * 512:(half + 1) * 512], ALU.mult),
                    reads=[bu, self.sgf], writes=[gbb])

        def emit_down_unit(hb, unit):
            tc, jh = unit // 2, unit % 2
            pair = unit % 2
            gbb = self.gb[hb % 2]
            for jj in range(2):
                j = jh * 2 + jj
                pb = self.banks[pair * 2 + jj]
                for i in range(8):
                    wd = self.wd_all[i]
                    self.T_(lambda t, i=i, j=j, tc=tc, pb=pb, wd=wd, gbb=gbb:
                            t.matmul(
                        pb.ap[:, :], gbb.ap[:, i, tc * 128:(tc + 1) * 128],
                        wd.ap[:, j * 512:(j + 1) * 512],
                        start=(i == 0), stop=(i == 7)),
                        reads=[gbb, wd], writes=[pb])
            for jj in range(2):
                j = jh * 2 + jj
                pb = self.banks[pair * 2 + jj]
                self.V_(lambda v, j=j, tc=tc, pb=pb: v.tensor_tensor(
                    self.r1a.ap[:, tc, j * 512:(j + 1) * 512],
                    self.r1a.ap[:, tc, j * 512:(j + 1) * 512],
                    pb.ap[:, :], ALU.add),
                    reads=[pb], writes=[self.r1a])

        def emit_wd_loads(hb):
            for i in range(8):
                hidx = hb * 8 + i
                wd = self.wd_all[i]
                self.dma("gpsimd", wd.ap[:, :],
                         I["wdT"][hidx * 128:(hidx + 1) * 128, :], writes=[wd])

        self.emit_warmup()
        emit_wd_loads(0)
        for hs in range(8):
            emit_gateup(0, hs)
        for hb in range(1, 8):
            for hs in range(8):
                emit_down_unit(hb - 1, 2 * hs)
                emit_down_unit(hb - 1, 2 * hs + 1)
                emit_gateup(hb, hs)
            emit_wd_loads(hb)
        for unit in range(16):
            emit_down_unit(7, unit)
        for tc in range(NCH):
            self.emit_ln(tc, to_xT=False)
            self.dma_store(self.out[tc * 128:(tc + 1) * 128, :],
                           self.r1a.ap[:, tc, :], [self.r1a])

    def finish(self):
        toks = [(f"st{i}", self.P.cnt[f"st{i}"])
                for i in range(self.P.LANES["st"]) if self.P.cnt[f"st{i}"]]
        if toks:
            self.P.emit("gpsimd", lambda g: g.engine_nop(), toks, "gs", 1)


def build_host_inputs(inputs):
    h = np.ascontiguousarray(np.asarray(inputs["L0_attn_hidden_states"],
                                        np.float32)[0])
    hT_full = np.ascontiguousarray(h.T)
    per_core = []
    for c in range(NCORES):
        lo = c * Tc
        hT_ext = np.zeros((HID, Tc + 3), np.float32)
        if c == 0:
            hT_ext[:, 3:] = hT_full[:, 0:Tc]
        else:
            hT_ext[:, :] = hT_full[:, lo - 3:lo + Tc]
        Mm = np.zeros((9, 8), np.float32)
        for cp in range(8):
            if cp < c:
                for cm in range(cp + 1, c):
                    Mm[cm, cp] = 1.0
            else:
                Mm[8, cp] = 1e5
        per_core.append({
            "hT_ext": hT_ext,
            "h_nat": np.ascontiguousarray(h[lo:lo + Tc]),
            "Mmat": Mm,
        })
    f32 = lambda k: np.ascontiguousarray(np.asarray(inputs[k], np.float32))
    shared = {
        "Wq": f32("L0_attn_Wq"), "Wk": f32("L0_attn_Wk"),
        "Wv": f32("L0_attn_Wv"), "Ww": f32("L0_attn_Ww"),
        "Wo": f32("L0_attn_Wo"),
        "cq": f32("L0_attn_cq"), "ck": f32("L0_attn_ck"),
        "cv": f32("L0_attn_cv"),
        "Wf1": f32("L0_attn_Wf1"), "Wf2": f32("L0_attn_Wf2"),
        "Wb": f32("L0_attn_Wb"),
        "Wg1": f32("L0_attn_Wg1"), "Wg2": f32("L0_attn_Wg2"),
        "bg2": f32("L0_attn_bg2"), "normw": f32("L0_attn_norm_w"),
        "wgT": np.ascontiguousarray(np.asarray(inputs["L0_ffn_w_gate"],
                                               np.float32).T),
        "wuT": np.ascontiguousarray(np.asarray(inputs["L0_ffn_w_up"],
                                               np.float32).T),
        "wdT": np.ascontiguousarray(np.asarray(inputs["L0_ffn_w_down"],
                                               np.float32).T),
    }
    return [{**shared, **pc} for pc in per_core]


_BUILD_CACHE = {}


def kernel(**inputs):
    in_maps = build_host_inputs(inputs)
    key = (STAGES, WANT_DBG)
    if key not in _BUILD_CACHE:
        b = Builder()
        nc = b.build()
        _BUILD_CACHE[key] = (nc, b)
    nc, b = _BUILD_CACHE[key]
    res = run_bass_kernel_spmd(nc, in_maps, core_ids=list(range(NCORES)))
    out = np.concatenate([r["out"] for r in res.results], axis=0)
    return out[None]


# version 12
# speedup vs baseline: 1.9541x; 1.0007x over previous
import numpy as np
import ml_dtypes
import concourse.bass as bass
import concourse.mybir as mybir
from concourse.bass_utils import run_bass_kernel_spmd

F32 = mybir.dt.float32
BF16 = mybir.dt.bfloat16
U8 = mybir.dt.uint8
AX = mybir.AxisListType
ALU = mybir.AluOpType
ACT = mybir.ActivationFunctionType

NCORES = 8
T = 8192
HID = 2048
Tc = T // NCORES
C = 128
NCH = Tc // C
H = 16
NSL = HID // 128
DH = 8192
SCALE = float(128) ** -0.5

ENGINES = ("sync", "tensor", "vector", "scalar", "gpsimd")
STAGES = "all"
WANT_DBG = ()


class Buf:
    __slots__ = ("ap", "w", "r", "rng", "fresh")

    def __init__(self, ap, rng=None):
        self.ap = ap
        self.w = None
        self.r = {}
        self.rng = rng
        self.fresh = True

    def add_r(self, tok):
        s, v = tok
        if self.r.get(s, -1) < v:
            self.r[s] = v

    def r_toks(self):
        return [(s, v) for s, v in self.r.items()]

    def __getitem__(self, key):
        return self.ap[key]


class Bank(Buf):
    __slots__ = ("bf",)

    def __init__(self, ap):
        super().__init__(ap)
        self.bf = ap.bitcast(BF16)


class Plan:
    LANES = {"ld": 4, "lw": 6, "st": 4}

    def __init__(self, nc):
        self.nc = nc
        self.plan = []
        self.sems = {}
        self.cnt = {}
        for s in ("ts", "vs", "ss", "gs", "cs"):
            self.sems[s] = nc.alloc_semaphore(s)
            self.cnt[s] = 0
        self.rr = {}
        for q, n in self.LANES.items():
            self.rr[q] = 0
            for i in range(n):
                name = f"{q}{i}"
                self.sems[name] = nc.alloc_semaphore(name)
                self.cnt[name] = 0

    def lane(self, q):
        i = self.rr[q] % self.LANES[q]
        self.rr[q] += 1
        name = f"{q}{i}"
        pre = (name, self.cnt[name]) if self.cnt[name] else None
        return name, pre

    def emit(self, eng, fn, waits, inc, inc_by):
        tok = None
        if inc is not None:
            self.cnt[inc] += inc_by
            tok = (inc, self.cnt[inc])
        self.plan.append((eng, fn, tuple(w for w in waits if w is not None),
                          (inc, inc_by) if inc else None))
        return tok

    SEM_ENG = {"ts": "tensor"}

    def realize(self, block):
        wm = {e: {} for e in ENGINES}
        eff_waits = []
        needed = {}
        for (e, fn, waits, inc) in self.plan:
            ew = []
            for (s, v) in waits:
                if self.SEM_ENG.get(s) == e:
                    continue
                if wm[e].get(s, 0) < v:
                    ew.append((s, v))
                    wm[e][s] = v
                    needed.setdefault(s, set()).add(v)
            eff_waits.append(ew)
        cum = {s: 0 for s in self.cnt}
        pend = {s: 0 for s in self.cnt}
        plan2 = []
        for (e, fn, waits, inc), ew in zip(self.plan, eff_waits):
            inc2 = None
            if inc is not None:
                s, by = inc
                cum[s] += by
                pend[s] += by
                if by >= 16 or s == "cs" or cum[s] in needed.get(s, ()):
                    inc2 = (s, pend[s])
                    pend[s] = 0
            plan2.append((e, fn, ew, inc2))

        def mk(eng_name):
            def body(eng):
                for (e, fn, waits, inc) in plan2:
                    if e != eng_name:
                        continue
                    for (s, v) in waits:
                        eng.wait_ge(self.sems[s], v)
                    ins = fn(eng)
                    if inc is not None:
                        s, by = inc
                        ins.then_inc(self.sems[s], by)
            return body

        block.sync(mk("sync"))
        block.tensor(mk("tensor"))
        block.vector(mk("vector"))
        block.scalar(mk("scalar"))
        block.gpsimd(mk("gpsimd"))


def bc_in(ap, n):
    return bass.AP(ap.tensor, ap.offset, [list(p) for p in ap.ap] + [[0, n]])


def bc_mid(ap, n):
    pr = [list(p) for p in ap.ap]
    return bass.AP(ap.tensor, ap.offset, [pr[0], [0, n]] + pr[1:])


class Builder:
    SEM_OF = {"sync": "ld", "gpsimd_dma": "lw", "tensor": "ts", "vector": "vs",
              "scalar": "ss", "gpsimd": "gs"}

    def __init__(self):
        nc = bass.Bass("TRN2", target_bir_lowering=False, debug=False)
        self.nc = nc
        self.P = Plan(nc)
        self.dbg = {}

        I = {}
        def inp(name, shape, dtype=F32):
            I[name] = nc.dram_tensor(name, shape, dtype, kind="ExternalInput").ap()
        self.I = I
        inp("hT_ext", [HID, Tc + 3]); inp("h_nat", [Tc, HID])
        for w in ("Wq", "Wk", "Wv", "Ww", "Wo"):
            inp(w, [HID, HID])
        for w in ("cq", "ck", "cv"):
            inp(w, [HID, 4])
        inp("Wf1", [HID, 128]); inp("Wf2", [128, HID]); inp("Wb", [HID, H])
        inp("Wg1", [HID, 128]); inp("Wg2", [128, HID])
        inp("bg2", [HID]); inp("normw", [128])
        inp("wgT", [HID, DH]); inp("wuT", [HID, DH]); inp("wdT", [DH, HID])
        inp("Mmat", [9, 8])
        self.out = nc.dram_tensor("out", [Tc, HID], F32, kind="ExternalOutput").ap()

        self.pub1 = nc.dram_tensor("pub1", [128, H * 256], BF16).ap()
        self.pub2 = nc.dram_tensor("pub2", [128, H], F32).ap()
        self.agg1 = nc.dram_tensor("agg1", [NCORES * 128, H * 256], BF16).ap()
        self.agg2 = nc.dram_tensor("agg2", [NCORES * 128, H], F32).ap()
        self.o_scr = nc.dram_tensor("o_scr", [HID, Tc], BF16).ap()

        def sb(name, shape, dtype):
            return Buf(nc.alloc_sbuf_tensor(name, shape, dtype).ap())
        self.trilS = sb("trilS", [128, 128], F32)
        self.tril1 = sb("tril1", [128, 128], BF16)
        self.identb = sb("identb", [128, 128], BF16)
        self.identf = sb("identf", [16, 16], F32)
        self.U2n = sb("U2n", [128, 128], F32)
        self.U3n = sb("U3n", [128, 128], F32)
        self.onescol = sb("onescol", [128, 1], F32)
        self.halfcol = sb("halfcol", [128, 1], F32)
        self.colpair = sb("colpair", [128, 2], F32)
        self.ones1p = sb("ones1p", [1, 128], F32)
        self.Wf2b = sb("Wf2b", [128, HID], BF16)
        self.Wg2b = sb("Wg2b", [128, HID], BF16)
        self.bg2T = sb("bg2T", [128, HID], BF16)
        self.nwcol = sb("nwcol", [128, 1], F32)
        self.MmatS = sb("MmatS", [9, 8], F32)
        self.f1c = sb("f1c", [128, Tc], BF16)
        self.G1c = sb("G1c", [128, Tc], BF16)
        self.SkTs = sb("SkTs", [128, H, 128], BF16)
        self.Svs = sb("Svs", [128, H, 128], BF16)
        self.Btn = sb("Btn", [128, H], F32)
        self.ecol = sb("ecol", [128, 2 * H], F32)
        self.beta_tm = sb("beta_tm", [128, NCH, H], F32)
        self.smx = sb("smx", [128, 16], F32)
        self.rst = sb("rst", [128, 16], F32)
        self.lnc = sb("lnc", [128, 8], F32)
        self.cpool = sb("cpool", [128, 4], F32)

        self.alias_reg = []
        self.AR = 184320
        self.arena = nc.alloc_sbuf_tensor("arena", [128, self.AR], U8)
        self.sbuf_slack = nc.sbuf_bytes_remaining
        assert self.sbuf_slack >= 0, f"sbuf over by {-self.sbuf_slack}"

        a = self.ab
        self.kc = a(0, [NSL, Tc], BF16)
        self.vc = a(32768, [NCH, HID], BF16)
        self.wc = a(65536, [NCH, HID], BF16)
        self.qc = a(98304, [NSL, Tc], BF16)
        self.hTb = a(131072, [NSL, Tc + 3], BF16)
        self.wsl = [a(164096, [NSL, 128], BF16), a(168192, [NSL, 128], BF16)]
        self.cvo = [a(172288, [264], F32), a(173376, [264], F32)]
        self.svo = [a(174464, [264], BF16), a(175040, [264], BF16)]
        self.betaR = a(175616, [1024], F32)
        self.bgrow = a(0, [HID], F32)
        self.Uh = a(8192, [128], F32)
        self.cwt = a(179712, [NSL, 16], F32)
        self.Wbb = a(180736, [NSL, H], BF16)
        self.sq2 = a(131072, [HID], BF16)
        self.spA = a(135168, [HID], F32)
        self.escrA = a(143360, [HID], BF16)
        self.bwdA = a(147456, [HID], BF16)
        self.ktmA = a(151552, [H, 128], BF16)
        self.diagA = a(155648, [H, 128], BF16)
        self.aggS = a(131072, [NCORES, 4, 256], BF16)
        self.augB = a(139264, [HID], F32)
        self.WcolS = a(147456, [H, 8], F32)
        self.diagC = a(155648, [NCORES, 128], BF16)
        self.spT = a(131072, [HID], F32)
        self.expBp = a(139264, [HID], BF16)
        self.escr = a(143360, [HID], BF16)
        self.bwR = a(147456, [HID], BF16)
        self.bwd = a(151552, [HID], BF16)
        self.gsig = a(155648, [HID], BF16)
        self.diagB = a(159744, [H, 128], BF16)
        self.ktm = a(163840, [H, 128], BF16)
        self.oTst = a(167936, [4, 128], BF16)
        self.gPM = a(168960, [4, 128], BF16)
        self.gRT = a(169984, [4, 128], BF16)
        self.gSR = a(171008, [4, 128], BF16)
        self.gAB = a(172032, [4, 128], BF16)
        self.gAT = a(173056, [4, 128], BF16)
        self.gP2 = a(174080, [4, 128], BF16)
        self.gOF = a(175104, [4, 128], BF16)
        self.sgrp = a(176128, [512], F32)
        self.ogrp = a(178176, [512], F32)
        self.svr = a(180224, [4, 128], BF16)
        self.diagR = a(181248, [4, 128], BF16)
        self.oT = a(0, [NSL, Tc], BF16)
        self.r1a = a(32768, [NCH, HID], F32)
        self.xT = a(98304, [NSL, Tc], BF16)
        self.wo_b = [a(131072 + i * 4096, [HID], BF16) for i in range(8)]
        self.sqscr = a(163840, [HID], BF16)
        self.xs = [a(167936, [HID], F32)]
        self.gb = [a(131072, [8, Tc], BF16), a(147456, [8, Tc], BF16)]
        self.ffg = [a(0, [NSL, 128], BF16), a(4096, [NSL, 128], BF16)]
        self.ffu = [a(8192, [NSL, 128], BF16), a(12288, [NSL, 128], BF16)]
        self.sgf = a(16384, [Tc], F32)
        self.wd_all = ([a(20480 + i * 4096, [HID], BF16) for i in range(3)] +
                       [a(163840 + i * 4096, [HID], BF16) for i in range(5)])
        self.sqscr2 = a(131072, [HID], BF16)

        self.banks = [Bank(nc.alloc_psum_tensor(f"pb{i}", [128, 512], F32).ap())
                      for i in range(8)]

    def ab(self, off, shape, dtype):
        esz = mybir.dt.size(dtype)
        n = int(np.prod(shape))
        assert off + n * esz <= self.AR, f"arena overflow at {off}+{n*esz}"
        ap = self.arena.ap()[:, off:off + n * esz].bitcast(dtype)
        if len(shape) > 1:
            dims = " ".join(f"d{i}" for i in range(len(shape)))
            kw = {f"d{i}": int(s) for i, s in enumerate(shape)}
            ap = ap.rearrange(f"p ({dims}) -> p {dims}", **kw)
        b = Buf(ap, rng=(off, off + n * esz))
        self.alias_reg.append(b)
        return b

    def _write_waits(self, b, waits):
        if b.w is not None:
            waits.append(b.w)
        waits.extend(b.r_toks())
        if b.fresh and b.rng is not None:
            lo, hi = b.rng
            for o in self.alias_reg:
                if o is b or o.rng is None:
                    continue
                if o.rng[0] < hi and lo < o.rng[1]:
                    if o.w is not None:
                        waits.append(o.w)
                    waits.extend(o.r_toks())
            b.fresh = False

    def op(self, eng, fn, reads=(), writes=(), extra=(), inc=None, inc_by=1):
        if inc is None:
            inc = self.SEM_OF[eng]
        waits = list(extra)
        for b in reads:
            if b.w is not None:
                waits.append(b.w)
        for b in writes:
            self._write_waits(b, waits)
        tok = self.P.emit(eng, fn, waits, inc, inc_by)
        for b in reads:
            b.add_r(tok)
        for b in writes:
            b.w = tok
            b.r = {}
        return tok

    def dma(self, eng, out_ap, in_ap, reads=(), writes=(), extra=(),
            slow_ok=False):
        q = "lw" if eng == "gpsimd" else "ld"
        sem, pre = self.P.lane(q)
        waits = list(extra)
        if pre is not None:
            waits.append(pre)
        for b in reads:
            if b.w is not None:
                waits.append(b.w)
        for b in writes:
            self._write_waits(b, waits)
        tok = self.P.emit(eng, lambda e: e.dma_start(
            out=out_ap, in_=in_ap, allow_slow_non_contiguous=slow_ok),
            waits, sem, 16)
        for b in reads:
            b.add_r(tok)
        for b in writes:
            b.w = tok
            b.r = {}
        return tok

    def dma_store(self, out_ap, in_ap, bufs, extra=(), eng="sync"):
        sem, pre = self.P.lane("st")
        waits = list(extra)
        if pre is not None:
            waits.append(pre)
        for b in bufs:
            if b.w is not None:
                waits.append(b.w)
        tok = self.P.emit(eng, lambda e: e.dma_start(out=out_ap, in_=in_ap),
                          waits, sem, 16)
        for b in bufs:
            b.add_r(tok)
        return tok

    def T_(self, fn, **kw):
        return self.op("tensor", fn, **kw)

    def V_(self, fn, **kw):
        return self.op("vector", fn, **kw)

    def S_(self, fn, **kw):
        return self.op("scalar", fn, **kw)

    def G_(self, fn, **kw):
        return self.op("gpsimd", fn, **kw)

    def emit_warmup(self, n=30):
        bk = self.banks[4]
        for i in range(n):
            self.T_(lambda t, bk=bk: t.matmul(
                bk.ap[:, :], self.identb.ap[:, :],
                self.bg2T.ap[:, 0:512], start=True, stop=True),
                reads=[self.identb, self.bg2T],
                writes=[bk] if i == 0 else (), extra=() if i == 0 else [bk.w])

    def tap(self, name, buf, shape, dtype):
        if name not in WANT_DBG:
            return
        d = self.nc.dram_tensor("dbg_" + name, shape, dtype,
                                kind="ExternalOutput").ap()
        self.dbg[name] = d
        self.dma_store(d[:, :] if len(shape) == 2 else d, buf.ap, [buf])

    def build(self):
        if STAGES == "B4o":
            self.p0_constants()
            self.b4_ffn()
            self.finish()
            with self.nc.Block() as block:
                self.P.realize(block)
            return self.nc
        self.p0_constants()
        self.a1_projections()
        self.a2_aux()
        if STAGES != "A1":
            self.a3_l2norm()
            self.a4_states()
        if STAGES in ("A5", "B", "B3", "all"):
            self.a5_combine()
        if STAGES in ("B", "B3", "all"):
            self.b_scan()
        if STAGES in ("B3", "all"):
            self.b3_wo_ln1()
        if STAGES == "all":
            self.b4_ffn()
        self.finish()
        with self.nc.Block() as block:
            self.P.realize(block)
        return self.nc

    def p0_constants(self):
        G_, V_, S_, T_ = self.G_, self.V_, self.S_, self.T_
        I = self.I

        def mask(buf, fill_first, pat, cmp, fill, cmul, base=0):
            G_(lambda g: g.memset(buf.ap[:, :], fill_first), writes=[buf])
            G_(lambda g: g.affine_select(buf.ap[:, :], buf.ap[:, :], pattern=pat,
                                         compare_op=cmp, fill=fill,
                                         channel_multiplier=cmul, base=base),
               writes=[buf])

        mask(self.trilS, SCALE, [[1, 128]], ALU.is_ge, 0.0, -1)
        mask(self.tril1, 1.0, [[1, 128]], ALU.is_ge, 0.0, -1)
        for t_ in (self.identb,):
            mask(t_, 0.0, [[-1, 128]], ALU.not_equal, 1.0, 1)
        mask(self.identf, 0.0, [[-1, 16]], ALU.not_equal, 1.0, 1)
        mask(self.U2n, 1.0, [[1, 128]], ALU.is_ge, 0.0, -1)
        mask(self.Uh, 1.0, [[0, 128]], ALU.is_ge, 0.0, -1, base=63)
        self.V_(lambda v: v.tensor_sub(self.U2n.ap[:, :], self.U2n.ap[:, :],
                                       self.Uh.ap[:, :]),
                reads=[self.Uh], writes=[self.U2n])
        mask(self.U3n, 1.0, [[-1, 128]], ALU.is_ge, 0.0, 1, base=-1)
        G_(lambda g: g.memset(self.onescol.ap[:, :], 1.0), writes=[self.onescol])
        mask(self.halfcol, 1.0, [[0, 1]], ALU.is_ge, 0.0, -1, base=63)
        G_(lambda g: g.memset(self.ones1p.ap[:, :], 1.0), writes=[self.ones1p])
        self.V_(lambda v: v.tensor_copy(self.colpair.ap[:, 0:1],
                                        self.onescol.ap[:, :]),
                reads=[self.onescol], writes=[self.colpair])
        self.V_(lambda v: v.tensor_copy(self.colpair.ap[:, 1:2],
                                        self.halfcol.ap[:, :]),
                reads=[self.halfcol], writes=[self.colpair])
        for i, val in enumerate((1e-6, 1.0, 1e-5)):
            G_(lambda g, i=i, val=val: g.memset(self.cpool.ap[:, i:i + 1], val),
               writes=[self.cpool])

        for i, cwn in enumerate(("cq", "ck", "cv", "cv")):
            self.dma("sync", self.cwt.ap[:, :, 4 * i:4 * i + 4],
                     I[cwn].rearrange("(s p) t -> p s t", p=128),
                     writes=[self.cwt])
        self.dma("gpsimd", self.Wf2b.ap[:, :], I["Wf2"][:, :], writes=[self.Wf2b])
        self.dma("gpsimd", self.Wg2b.ap[:, :], I["Wg2"][:, :], writes=[self.Wg2b])
        self.dma("gpsimd", self.Wbb.ap[:, :, :],
                 I["Wb"].rearrange("(s p) h -> p s h", p=128), writes=[self.Wbb])
        self.dma("sync", self.nwcol.ap[:, :],
                 I["normw"].rearrange("(p o) -> p o", o=1), writes=[self.nwcol])
        self.dma("sync", self.MmatS.ap[:, :], I["Mmat"][:, :], writes=[self.MmatS])
        self.dma("sync", self.bgrow.ap[0:1, :],
                 I["bg2"].rearrange("(o f) -> o f", o=1), writes=[self.bgrow])
        for j in range(4):
            bk = self.banks[4 + j]
            self.T_(lambda t, j=j, bk=bk: t.matmul(
                bk.ap[:, :], self.ones1p.ap[:, :],
                self.bgrow.ap[0:1, j * 512:(j + 1) * 512], start=True, stop=True),
                reads=[self.ones1p, self.bgrow], writes=[bk])
            self.V_(lambda v, j=j, bk=bk: v.tensor_copy(
                self.bg2T.ap[:, j * 512:(j + 1) * 512], bk.ap[:, :]),
                reads=[bk], writes=[self.bg2T])
        self.dma("gpsimd", self.hTb.ap[:, :, :],
                 I["hT_ext"].rearrange("(s p) t -> p s t", p=128),
                 writes=[self.hTb])

    def a1_projections(self):
        T_, V_, S_ = self.T_, self.V_, self.S_
        I = self.I
        streams = [("q", "Wq", 0), ("k", "Wk", 1), ("v", "Wv", 2), ("w", "Ww", 3)]
        for sname, wn, sidx in streams:
            for fs in range(NSL):
                par = (sidx * NSL + fs) % 2
                wb = self.wsl[par]
                self.dma("gpsimd", wb.ap[:, :, :],
                         I[wn][:, fs * 128:(fs + 1) * 128].rearrange(
                             "(a p) c -> p a c", p=128), writes=[wb])
                for g in range(4):
                    t0 = g * 256
                    pb = self.banks[g]
                    for a in range(NSL):
                        self.T_(lambda t, a=a, pb=pb, t0=t0, wb=wb: t.matmul(
                            pb.ap[:, 0:259], wb.ap[:, a, :],
                            self.hTb.ap[:, a, t0:t0 + 259],
                            start=(a == 0), stop=(a == NSL - 1)),
                            reads=[wb, self.hTb], writes=[pb])
                    cp = g % 2
                    co = self.cvo[cp]
                    cb = sidx * 4
                    self.V_(lambda v, pb=pb, co=co, fs=fs, cb=cb: v.tensor_scalar(
                        co.ap[:, 0:256], pb.ap[:, 0:256],
                        self.cwt.ap[:, fs, cb:cb + 1], None, ALU.mult),
                        reads=[pb, self.cwt], writes=[co])
                    for i in range(1, 4):
                        self.V_(lambda v, pb=pb, co=co, fs=fs, cb=cb, i=i:
                                v.scalar_tensor_tensor(
                                    co.ap[:, 0:256], pb.ap[:, i:256 + i],
                                    self.cwt.ap[:, fs, cb + i:cb + i + 1],
                                    co.ap[:, 0:256], ALU.mult, ALU.add),
                                reads=[pb, self.cwt], writes=[co])
                    if sidx == 0:
                        self.S_(lambda s, co=co: s.activation(
                            co.ap[:, 0:256], co.ap[:, 0:256], ACT.Silu),
                            writes=[co])
                        self.V_(lambda v, co=co, fs=fs, t0=t0: v.tensor_scalar(
                            self.qc.ap[:, fs, t0:t0 + 256], co.ap[:, 0:256],
                            SCALE, None, ALU.mult), reads=[co],
                            writes=[self.qc])
                    elif sidx == 1:
                        self.S_(lambda s, co=co, fs=fs, t0=t0: s.activation(
                            self.kc.ap[:, fs, t0:t0 + 256], co.ap[:, 0:256],
                            ACT.Silu), reads=[co], writes=[self.kc])
                    else:
                        so = self.svo[cp]
                        self.S_(lambda s, co=co, so=so: s.activation(
                            so.ap[:, 0:256], co.ap[:, 0:256], ACT.Silu),
                            reads=[co], writes=[so])
                        dst = self.vc if sidx == 2 else self.wc
                        tb = self.banks[4 + cp]
                        for hf in range(2):
                            self.T_(lambda t, tb=tb, so=so, hf=hf: t.transpose(
                                tb.bf[:, hf * 128:(hf + 1) * 128],
                                so.ap[:, hf * 128:hf * 128 + 128],
                                self.identb.ap[:, :]),
                                reads=[so, self.identb],
                                writes=[tb])
                        self.V_(lambda v, tb=tb, dst=dst, g=g, fs=fs: v.tensor_copy(
                            dst.ap[:, 2 * g:2 * g + 2, fs * 128:(fs + 1) * 128],
                            tb.bf[:, 0:256].rearrange("p (a b) -> p a b", a=2)),
                            reads=[tb], writes=[dst])
        self.tap("qc", self.qc, [128, NSL, Tc], BF16)
        self.tap("kc", self.kc, [128, NSL, Tc], BF16)
        self.tap("vc", self.vc, [128, NCH, HID], BF16)
        self.tap("wcr", self.wc, [128, NCH, HID], BF16)

    def a2_aux(self):
        T_, V_, S_ = self.T_, self.V_, self.S_
        I = self.I
        for (dst, wn) in ((self.f1c, "Wf1"), (self.G1c, "Wg1")):
            wb = self.wsl[0]
            self.dma("gpsimd", wb.ap[:, :, :],
                     I[wn].rearrange("(a p) c -> p a c", p=128), writes=[wb])
            for half in range(2):
                pb = self.banks[4 + half]
                for a in range(NSL):
                    self.T_(lambda t, a=a, pb=pb, half=half, wb=wb: t.matmul(
                        pb.ap[:, :], wb.ap[:, a, :],
                        self.hTb.ap[:, a, 3 + half * 512:3 + (half + 1) * 512],
                        start=(a == 0), stop=(a == NSL - 1)),
                        reads=[wb, self.hTb], writes=[pb])
                self.V_(lambda v, pb=pb, dst=dst, half=half: v.tensor_copy(
                    dst.ap[:, half * 512:(half + 1) * 512], pb.ap[:, :]),
                    reads=[pb], writes=[dst])
        pb = self.banks[6]
        for half in range(2):
            for a in range(NSL):
                self.T_(lambda t, a=a, half=half, pb=pb: t.matmul(
                    pb.ap[0:16, 0:512], self.Wbb.ap[:, a, :],
                    self.hTb.ap[:, a, 3 + half * 512:3 + (half + 1) * 512],
                    start=(a == 0), stop=(a == NSL - 1)),
                    reads=[self.Wbb, self.hTb], writes=[pb])
            self.V_(lambda v, pb=pb, half=half: v.tensor_copy(
                self.betaR.ap[0:16, half * 512:(half + 1) * 512],
                pb.ap[0:16, 0:512]),
                reads=[pb], writes=[self.betaR])
        for ch in range(NCH):
            pb7 = self.banks[7]
            self.T_(lambda t, ch=ch, pb7=pb7: t.transpose(
                pb7.ap[:, 0:16], self.betaR.ap[0:16, ch * 128:(ch + 1) * 128],
                self.identf.ap[:, :]),
                reads=[self.betaR, self.identf], writes=[pb7])
            self.S_(lambda s, ch=ch, pb7=pb7: s.activation(
                self.beta_tm.ap[:, ch, :], pb7.ap[:, 0:16], ACT.Exp, scale=-1.0),
                reads=[pb7], writes=[self.beta_tm])
            self.V_(lambda v, ch=ch: v.tensor_scalar(
                self.beta_tm.ap[:, ch, :], self.beta_tm.ap[:, ch, :], 1.0, None,
                ALU.add), writes=[self.beta_tm])
            self.V_(lambda v, ch=ch: v.reciprocal(
                self.beta_tm.ap[:, ch, :], self.beta_tm.ap[:, ch, :]),
                writes=[self.beta_tm])
        self.tap("f1c", self.f1c, [128, Tc], BF16)
        self.tap("G1c", self.G1c, [128, Tc], BF16)
        self.tap("beta", self.beta_tm, [128, NCH, H], F32)

    def a3_l2norm(self):
        for ch in range(NCH):
            self.V_(lambda v, ch=ch: v.tensor_mul(
                self.sq2.ap[:, :], self.wc.ap[:, ch, :], self.wc.ap[:, ch, :]),
                reads=[self.wc], writes=[self.sq2])
            self.V_(lambda v: v.tensor_reduce(
                self.rst.ap[:, :], self.sq2.ap[:, :].rearrange(
                    "p (h m) -> p h m", h=H), AX.X, ALU.add),
                reads=[self.sq2], writes=[self.rst])
            self.S_(lambda s: s.activation(self.rst.ap[:, :], self.rst.ap[:, :],
                                           ACT.Ln, bias=self.cpool.ap[:, 0:1]),
                    reads=[self.cpool], writes=[self.rst])
            self.S_(lambda s: s.activation(self.rst.ap[:, :], self.rst.ap[:, :],
                                           ACT.Exp, scale=-0.5), writes=[self.rst])
            self.V_(lambda v, ch=ch: v.tensor_mul(
                self.rst.ap[:, :], self.rst.ap[:, :], self.beta_tm.ap[:, ch, :]),
                reads=[self.beta_tm], writes=[self.rst])
            self.V_(lambda v, ch=ch: v.tensor_tensor(
                self.wc.ap[:, ch, :].rearrange("p (h m) -> p h m", h=H),
                self.wc.ap[:, ch, :].rearrange("p (h m) -> p h m", h=H),
                bc_in(self.rst.ap[:, :], 128), ALU.mult),
                reads=[self.rst], writes=[self.wc])
        self.tap("wcn", self.wc, [128, NCH, HID], BF16)

    def emit_sp(self, sp, ch):
        for j in range(4):
            pb = self.banks[j]
            self.T_(lambda t, j=j, ch=ch, pb=pb: t.matmul(
                pb.ap[:, :], self.f1c.ap[:, ch * 128:(ch + 1) * 128],
                self.Wf2b.ap[:, j * 512:(j + 1) * 512], start=True, stop=True),
                reads=[self.f1c, self.Wf2b], writes=[pb])
            self.S_(lambda s, j=j, pb=pb, sp=sp: s.activation(
                sp.ap[:, j * 512:(j + 1) * 512], pb.ap[:, :], ACT.Exp, scale=-1.0),
                reads=[pb], writes=[sp])
            self.S_(lambda s, j=j, sp=sp: s.activation(
                sp.ap[:, j * 512:(j + 1) * 512], sp.ap[:, j * 512:(j + 1) * 512],
                ACT.Ln, bias=self.cpool.ap[:, 1:2]),
                reads=[self.cpool], writes=[sp])

    def emit_cumexp(self, U, sp, dst_scales):
        for j in range(4):
            pb = self.banks[j]
            self.T_(lambda t, j=j, pb=pb, U=U, sp=sp: t.matmul(
                pb.ap[:, :], U.ap[:, :], sp.ap[:, j * 512:(j + 1) * 512],
                start=True, stop=True), reads=[U, sp], writes=[pb])
            for (dst, sc) in dst_scales:
                self.S_(lambda s, j=j, pb=pb, dst=dst, sc=sc: s.activation(
                    dst.ap[:, j * 512:(j + 1) * 512], pb.ap[:, :], ACT.Exp,
                    scale=sc), reads=[pb], writes=[dst])

    def emit_cols(self, sp, col, dst_ap, bank_i=5):
        pb = self.banks[bank_i]
        for h in range(H):
            self.T_(lambda t, h=h, pb=pb, sp=sp, col=col: t.matmul(
                pb.ap[:, h:h + 1], sp.ap[:, h * 128:(h + 1) * 128], col.ap[:, :],
                start=True, stop=True),
                reads=[sp, col], writes=[pb])
        return self.S_(lambda s, pb=pb, dst_ap=dst_ap: s.activation(
            dst_ap, pb.ap[:, 0:H], ACT.Exp, scale=-1.0),
            reads=[pb, self.ecol], writes=[self.ecol])

    def emit_cols2(self, sp, bank_i=5):
        pb = self.banks[bank_i]
        for h in range(H):
            self.T_(lambda t, h=h, pb=pb, sp=sp: t.matmul(
                pb.ap[:, 2 * h:2 * h + 2], sp.ap[:, h * 128:(h + 1) * 128],
                self.colpair.ap[:, :], start=True, stop=True),
                reads=[sp, self.colpair], writes=[pb])
        ec = self.ecol.ap[:, 0:2 * H].rearrange("p (a b) -> p b a", a=2) \
            if False else None
        pv = pb.ap[:, 0:2 * H].rearrange("p (h w) -> p w h", h=H)
        dv = self.ecol.ap[:, 0:2 * H].rearrange("p (w h) -> p w h", w=2)
        self.S_(lambda s: s.activation(dv, pv, ACT.Exp, scale=-1.0),
                reads=[pb], writes=[self.ecol])

    def emit_diag(self, dst, cols_ap, n=H, reads=()):
        for h in range(n):
            self.V_(lambda v, h=h, dst=dst, cols_ap=cols_ap: v.tensor_scalar(
                dst.ap[:, h, :], self.identb.ap[:, :],
                cols_ap[:, h:h + 1], None, ALU.mult),
                reads=[self.identb] + list(reads),
                writes=[dst])

    def emit_ktm(self, dst, ch, bank0=6):
        for g8 in range(2):
            pbT = self.banks[bank0 + g8]
            for hi in range(8):
                h = g8 * 8 + hi
                self.T_(lambda t, h=h, ch=ch, pbT=pbT, hi=hi: t.transpose(
                    pbT.bf[:, hi * 128:(hi + 1) * 128],
                    self.kc.ap[:, h, ch * 128:(ch + 1) * 128],
                    self.identb.ap[:, :]),
                    reads=[self.kc, self.identb], writes=[pbT])
            self.V_(lambda v, g8=g8, pbT=pbT, dst=dst: v.tensor_copy(
                dst.ap[:, g8 * 8:(g8 + 1) * 8, :],
                pbT.bf[:, 0:1024].rearrange("p (a b) -> p a b", a=8)),
                reads=[pbT], writes=[dst])

    def emit_states(self, ktm, bwdb, diag, ch, bank0=0):
        for g8 in range(2):
            evac_w = []
            for hi in range(8):
                h = g8 * 8 + hi
                bsk = self.banks[bank0 + hi // 4]
                bsv = self.banks[bank0 + 2 + hi // 4]
                sl = (hi % 4) * 128
                self.T_(lambda t, h=h, bsk=bsk, sl=sl, bwdb=bwdb, ktm=ktm:
                        t.matmul(bsk.ap[:, sl:sl + 128],
                                 bwdb.ap[:, h * 128:(h + 1) * 128],
                                 ktm.ap[:, h, :], start=True,
                                 stop=False),
                        reads=[bwdb, ktm], writes=[bsk])
                self.T_(lambda t, h=h, bsk=bsk, sl=sl, diag=diag: t.matmul(
                    bsk.ap[:, sl:sl + 128], diag.ap[:, h, :],
                    self.SkTs.ap[:, h, :], start=False, stop=True),
                    reads=[diag, self.SkTs], writes=[bsk])
                self.T_(lambda t, h=h, bsv=bsv, sl=sl, ch=ch, bwdb=bwdb: t.matmul(
                    bsv.ap[:, sl:sl + 128], bwdb.ap[:, h * 128:(h + 1) * 128],
                    self.vc.ap[:, ch, h * 128:(h + 1) * 128],
                    start=True, stop=False),
                    reads=[bwdb, self.vc], writes=[bsv])
                tok = self.T_(lambda t, h=h, bsv=bsv, sl=sl, diag=diag: t.matmul(
                    bsv.ap[:, sl:sl + 128], diag.ap[:, h, :],
                    self.Svs.ap[:, h, :], start=False, stop=True),
                    reads=[diag, self.Svs], writes=[bsv])
                evac_w.append(tok)
            for bi in range(2):
                bsk = self.banks[bank0 + bi]
                bsv = self.banks[bank0 + 2 + bi]
                hlo = g8 * 8 + bi * 4
                self.V_(lambda v, bsk=bsk, hlo=hlo: v.tensor_copy(
                    self.SkTs.ap[:, hlo:hlo + 4, :],
                    bsk.ap[:, :].rearrange("p (a b) -> p a b", a=4)),
                    reads=[bsk], writes=[self.SkTs], extra=evac_w)
                self.V_(lambda v, bsv=bsv, hlo=hlo: v.tensor_copy(
                    self.Svs.ap[:, hlo:hlo + 4, :],
                    bsv.ap[:, :].rearrange("p (a b) -> p a b", a=4)),
                    reads=[bsv], writes=[self.Svs], extra=evac_w)

    def a4_states(self):
        self.V_(lambda v: v.memset(self.SkTs.ap[:, :, :], 0.0), writes=[self.SkTs])
        self.V_(lambda v: v.memset(self.Svs.ap[:, :, :], 0.0), writes=[self.Svs])
        self.V_(lambda v: v.memset(self.Btn.ap[:, :], 0.0), writes=[self.Btn])
        for ch in range(NCH):
            self.emit_sp(self.spA, ch)
            pb5 = self.banks[5]
            for h in range(H):
                self.T_(lambda t, h=h, pb5=pb5: t.matmul(
                    pb5.ap[:, h:h + 1], self.spA.ap[:, h * 128:(h + 1) * 128],
                    self.onescol.ap[:, :], start=True, stop=True),
                    reads=[self.spA, self.onescol],
                    writes=[pb5])
            self.S_(lambda s, pb5=pb5: s.activation(
                self.ecol.ap[:, 0:H], pb5.ap[:, 0:H], ACT.Exp, scale=-1.0),
                reads=[pb5], writes=[self.ecol])
            self.V_(lambda v, pb5=pb5: v.tensor_add(
                self.Btn.ap[:, :], self.Btn.ap[:, :], pb5.ap[:, 0:H]),
                reads=[pb5], writes=[self.Btn])
            self.emit_cumexp(self.U3n, self.spA, [(self.escrA, -1.0)])
            self.V_(lambda v, ch=ch: v.tensor_mul(
                self.bwdA.ap[:, :], self.wc.ap[:, ch, :], self.escrA.ap[:, :]),
                reads=[self.wc, self.escrA], writes=[self.bwdA])
            self.emit_diag(self.diagA, self.ecol.ap[:, 0:H], reads=[self.ecol])
            self.emit_ktm(self.ktmA, ch)
            self.emit_states(self.ktmA, self.bwdA, self.diagA, ch)
        self.tap("SkT_loc", self.SkTs, [128, H, 128], BF16)
        self.tap("Sv_loc", self.Svs, [128, H, 128], BF16)
        self.tap("Btn", self.Btn, [128, H], F32)

    def a5_combine(self):
        V_, S_, T_ = self.V_, self.S_, self.T_
        pubv = self.pub1.rearrange("p (h two d) -> p h two d", h=H, two=2)
        t_p1a = self.dma_store(pubv[:, :, 0, :], self.SkTs.ap[:, :, :],
                               [self.SkTs])
        t_p1b = self.dma_store(pubv[:, :, 1, :], self.Svs.ap[:, :, :],
                               [self.Svs])
        t_p2 = self.dma_store(self.pub2[:, :], self.Btn.ap[:, :], [self.Btn])
        t_c1 = self.P.emit("gpsimd", lambda g: g.collective_compute(
            "AllGather", ALU.bypass, replica_groups=[list(range(NCORES))],
            ins=[self.pub1[:, :]], outs=[self.agg1[:, :]]),
            [t_p1a, t_p1b], "cs", 1)
        t_c2 = self.P.emit("gpsimd", lambda g: g.collective_compute(
            "AllGather", ALU.bypass, replica_groups=[list(range(NCORES))],
            ins=[self.pub2[:, :]], outs=[self.agg2[:, :]]),
            [t_p2], "cs", 1)
        augv = self.augB.ap.rearrange("p (m h) -> p m h", m=128)
        self.V_(lambda v: v.memset(self.augB.ap[0:9, :], 1.0), writes=[self.augB])
        self.dma("sync", augv[0:8, :, :],
                 self.agg2.rearrange("(c m) h -> c m h", m=128),
                 writes=[self.augB], extra=[t_c2])
        pb = self.banks[4]
        for h in range(H):
            self.T_(lambda t, h=h, pb=pb, augv=augv: t.matmul(
                pb.ap[:, (h % 8) * 8:(h % 8) * 8 + 8],
                augv[0:9, :, h:h + 1],
                self.MmatS.ap[:, :], start=True, stop=True),
                reads=[self.augB, self.MmatS],
                writes=[pb])
            if h % 8 == 7:
                hlo = h - 7
                self.S_(lambda s, pb=pb, hlo=hlo: s.activation(
                    self.WcolS.ap[:, hlo:hlo + 8, :].rearrange("p a b -> p (a b)"),
                    pb.ap[:, 0:64], ACT.Exp, scale=-1.0),
                    reads=[pb], writes=[self.WcolS])
        agg1v = self.agg1.rearrange("(c p) x -> p c x", p=128).rearrange(
            "p c (h d) -> p c h d", h=H)
        for hg in range(4):
            self.dma("sync", self.aggS.ap[:, :, :, :],
                     agg1v[:, :, hg * 4:(hg + 1) * 4, :],
                     writes=[self.aggS], extra=[t_c1])
            for hi in range(4):
                h = hg * 4 + hi
                self.emit_diag(self.diagC, self.WcolS.ap[:, h, :], n=NCORES,
                               reads=[self.WcolS])
                bsk = self.banks[0]
                bsv = self.banks[1]
                for c in range(NCORES):
                    self.T_(lambda t, hi=hi, c=c, bsk=bsk: t.matmul(
                        bsk.ap[:, 0:128], self.diagC.ap[:, c, :],
                        self.aggS.ap[:, c, hi, 0:128], start=(c == 0),
                        stop=(c == NCORES - 1)),
                        reads=[self.diagC, self.aggS],
                        writes=[bsk])
                    self.T_(lambda t, hi=hi, c=c, bsv=bsv: t.matmul(
                        bsv.ap[:, 0:128], self.diagC.ap[:, c, :],
                        self.aggS.ap[:, c, hi, 128:256], start=(c == 0),
                        stop=(c == NCORES - 1)),
                        reads=[self.diagC, self.aggS],
                        writes=[bsv])
                self.V_(lambda v, h=h, bsk=bsk: v.tensor_copy(
                    self.SkTs.ap[:, h, :], bsk.ap[:, 0:128]),
                    reads=[bsk], writes=[self.SkTs])
                self.V_(lambda v, h=h, bsv=bsv: v.tensor_copy(
                    self.Svs.ap[:, h, :], bsv.ap[:, 0:128]),
                    reads=[bsv], writes=[self.Svs])
        self.tap("SkT_in", self.SkTs, [128, H, 128], BF16)
        self.tap("Sv_in", self.Svs, [128, H, 128], BF16)

    def b_scan(self):
        T_, V_, S_ = self.T_, self.V_, self.S_
        for ch in range(NCH):
            self.emit_sp(self.spT, ch)
            self.emit_cumexp(self.U2n, self.spT,
                             [(self.expBp, -1.0), (self.escr, 1.0)])
            self.V_(lambda v, ch=ch: v.tensor_mul(
                self.bwR.ap[:, :], self.wc.ap[:, ch, :], self.escr.ap[:, :]),
                reads=[self.wc, self.escr], writes=[self.bwR])
            self.emit_cumexp(self.U3n, self.spT, [(self.escr, -1.0)])
            self.V_(lambda v, ch=ch: v.tensor_mul(
                self.bwd.ap[:, :], self.wc.ap[:, ch, :], self.escr.ap[:, :]),
                reads=[self.wc, self.escr], writes=[self.bwd])
            self.emit_cols2(self.spT)
            for j in range(4):
                pb = self.banks[j]
                self.T_(lambda t, j=j, ch=ch, pb=pb: t.matmul(
                    pb.ap[:, :], self.G1c.ap[:, ch * 128:(ch + 1) * 128],
                    self.Wg2b.ap[:, j * 512:(j + 1) * 512], start=True,
                    stop=True), reads=[self.G1c, self.Wg2b], writes=[pb])
                self.V_(lambda v, j=j, pb=pb: v.tensor_tensor(
                    self.sgrp.ap[:, :], pb.ap[:, :],
                    self.bg2T.ap[:, j * 512:(j + 1) * 512], ALU.add),
                    reads=[pb, self.bg2T], writes=[self.sgrp])
                self.S_(lambda s: s.activation(self.sgrp.ap[:, :],
                                               self.sgrp.ap[:, :], ACT.Exp,
                                               scale=-1.0), writes=[self.sgrp])
                self.V_(lambda v: v.tensor_scalar(self.sgrp.ap[:, :],
                                                  self.sgrp.ap[:, :], 1.0, None,
                                                  ALU.add), writes=[self.sgrp])
                self.V_(lambda v: v.reciprocal(self.sgrp.ap[:, :],
                                               self.sgrp.ap[:, :]),
                        writes=[self.sgrp])
                self.V_(lambda v, j=j: v.tensor_copy(
                    self.gsig.ap[:, j * 512:(j + 1) * 512], self.sgrp.ap[:, :]),
                    reads=[self.sgrp], writes=[self.gsig])
            self.emit_diag(self.diagB, self.ecol.ap[:, 0:H], reads=[self.ecol])
            for g4 in range(4):
                hlo = g4 * 4
                qb = 4 if g4 % 2 == 0 else 0
                bPT = self.banks[qb]
                bS = self.banks[qb + 1]
                bTR = self.banks[qb + 2]
                bO = self.banks[qb + 3]
                self.emit_diag(self.diagR, self.ecol.ap[:, H + hlo:H + hlo + 4],
                               n=4, reads=[self.ecol])
                for hi in range(4):
                    h = hlo + hi
                    self.T_(lambda t, h=h, hi=hi, bPT=bPT: t.matmul(
                        bPT.ap[:, hi * 128:(hi + 1) * 128],
                        self.SkTs.ap[:, h, :], self.diagR.ap[:, hi, :],
                        start=True, stop=True),
                        reads=[self.diagR, self.SkTs],
                        writes=[bPT])
                self.V_(lambda v, bPT=bPT: v.tensor_copy(
                    self.gSR.ap[:, :, :],
                    bPT.ap[:, :].rearrange("p (a b) -> p a b", a=4)),
                    reads=[bPT], writes=[self.gSR])
                for hi in range(4):
                    h = hlo + hi
                    self.T_(lambda t, h=h, hi=hi, ch=ch, bPT=bPT: t.matmul(
                        bPT.ap[:, hi * 128:(hi + 1) * 128],
                        self.kc.ap[:, h, ch * 128:(ch + 1) * 128],
                        self.qc.ap[:, h, ch * 128:(ch + 1) * 128],
                        start=True, stop=True),
                        reads=[self.kc, self.qc],
                        writes=[bPT])
                self.V_(lambda v, bPT=bPT: v.tensor_tensor(
                    self.gPM.ap[:, :, :],
                    bPT.ap[:, :].rearrange("p (a b) -> p a b", a=4),
                    bc_mid(self.trilS.ap[:, :], 4), ALU.mult),
                    reads=[bPT, self.trilS], writes=[self.gPM])
                for hi in range(4):
                    h = hlo + hi
                    self.T_(lambda t, h=h, hi=hi, bTR=bTR: t.transpose(
                        bTR.bf[:, 512 + hi * 128:512 + (hi + 1) * 128],
                        self.bwR.ap[:, h * 128:(h + 1) * 128],
                        self.identb.ap[:, :]),
                        reads=[self.bwR, self.identb],
                        writes=[bTR])
                self.V_(lambda v, bTR=bTR: v.tensor_copy(
                    self.gRT.ap[:, :, :],
                    bTR.bf[:, 512:1024].rearrange("p (a b) -> p a b", a=4)),
                    reads=[bTR], writes=[self.gRT])
                for hi in range(4):
                    h = hlo + hi
                    self.T_(lambda t, h=h, hi=hi, bS=bS: t.matmul(
                        bS.ap[:, hi * 128:(hi + 1) * 128], self.gPM.ap[:, hi, :],
                        self.bwR.ap[:, h * 128:(h + 1) * 128],
                        start=True, stop=False),
                        reads=[self.gPM, self.bwR],
                        writes=[bS])
                    self.T_(lambda t, h=h, hi=hi, ch=ch, bS=bS: t.matmul(
                        bS.ap[:, hi * 128:(hi + 1) * 128],
                        self.qc.ap[:, h, ch * 128:(ch + 1) * 128],
                        self.gSR.ap[:, hi, :], start=False, stop=True),
                        reads=[self.qc, self.gSR], writes=[bS])
                self.V_(lambda v, bS=bS, hlo=hlo: v.tensor_tensor(
                    self.sgrp.ap[:, :], bS.ap[:, :],
                    self.expBp.ap[:, hlo * 128:(hlo + 4) * 128], ALU.mult),
                    reads=[bS, self.expBp], writes=[self.sgrp])
                self.V_(lambda v: v.tensor_reduce(
                    self.smx.ap[:, 0:4], self.sgrp.ap[:, :].rearrange(
                        "p (a b) -> p a b", a=4), AX.X, ALU.max, negate=True),
                    reads=[self.sgrp], writes=[self.smx])
                for hi in range(4):
                    self.S_(lambda s, hi=hi: s.activation(
                        self.sgrp.ap[:, hi * 128:(hi + 1) * 128],
                        self.sgrp.ap[:, hi * 128:(hi + 1) * 128], ACT.Exp,
                        bias=self.smx.ap[:, hi:hi + 1]),
                        reads=[self.smx], writes=[self.sgrp])
                self.V_(lambda v: v.tensor_reduce(
                    self.smx.ap[:, 4:8], self.sgrp.ap[:, :].rearrange(
                        "p (a b) -> p a b", a=4), AX.X, ALU.add),
                    reads=[self.sgrp], writes=[self.smx])
                self.V_(lambda v: v.reciprocal(self.smx.ap[:, 4:8],
                                               self.smx.ap[:, 4:8]),
                        writes=[self.smx])
                self.V_(lambda v: v.tensor_tensor(
                    self.sgrp.ap[:, :].rearrange("p (a b) -> p a b", a=4),
                    self.sgrp.ap[:, :].rearrange("p (a b) -> p a b", a=4),
                    bc_in(self.smx.ap[:, 4:8], 128), ALU.mult),
                    reads=[self.smx], writes=[self.sgrp])
                self.V_(lambda v, hlo=hlo: v.tensor_tensor(
                    self.gAB.ap[:, :, :].rearrange("p a b -> p (a b)"),
                    self.sgrp.ap[:, :],
                    self.expBp.ap[:, hlo * 128:(hlo + 4) * 128], ALU.mult),
                    reads=[self.sgrp, self.expBp], writes=[self.gAB])
                for hi in range(4):
                    self.T_(lambda t, hi=hi, bTR=bTR: t.transpose(
                        bTR.bf[:, hi * 128:(hi + 1) * 128], self.gAB.ap[:, hi, :],
                        self.identb.ap[:, :]),
                        reads=[self.gAB, self.identb],
                        writes=[bTR])
                self.V_(lambda v, bTR=bTR: v.tensor_copy(
                    self.gAT.ap[:, :, :],
                    bTR.bf[:, 0:512].rearrange("p (a b) -> p a b", a=4)),
                    reads=[bTR], writes=[self.gAT])
                for hi in range(4):
                    self.T_(lambda t, hi=hi, bPT=bPT: t.matmul(
                        bPT.ap[:, hi * 128:(hi + 1) * 128], self.gRT.ap[:, hi, :],
                        self.gAT.ap[:, hi, :], start=True, stop=True),
                        reads=[self.gRT, self.gAT],
                        writes=[bPT])
                self.V_(lambda v, bPT=bPT: v.tensor_tensor(
                    self.gP2.ap[:, :, :],
                    bPT.ap[:, :].rearrange("p (a b) -> p a b", a=4),
                    bc_mid(self.tril1.ap[:, :], 4), ALU.mult),
                    reads=[bPT, self.tril1], writes=[self.gP2])
                self.V_(lambda v, hlo=hlo: v.tensor_tensor(
                    self.svr.ap[:, :, :], self.Svs.ap[:, hlo:hlo + 4, :],
                    bc_in(self.ecol.ap[:, H + hlo:H + hlo + 4], 128), ALU.mult),
                    reads=[self.Svs, self.ecol], writes=[self.svr])
                for hi in range(4):
                    h = hlo + hi
                    self.T_(lambda t, h=h, hi=hi, ch=ch, bO=bO: t.matmul(
                        bO.ap[:, hi * 128:(hi + 1) * 128], self.gP2.ap[:, hi, :],
                        self.vc.ap[:, ch, h * 128:(h + 1) * 128],
                        start=True, stop=False),
                        reads=[self.gP2, self.vc],
                        writes=[bO])
                    self.T_(lambda t, hi=hi, bO=bO: t.matmul(
                        bO.ap[:, hi * 128:(hi + 1) * 128], self.gAT.ap[:, hi, :],
                        self.svr.ap[:, hi, :], start=False, stop=True),
                        reads=[self.gAT, self.svr], writes=[bO])
                self.S_(lambda s, bO=bO: s.activation(
                    self.ogrp.ap[:, :], bO.ap[:, :], ACT.Square),
                    reads=[bO], writes=[self.ogrp])
                self.V_(lambda v: v.tensor_reduce(
                    self.rst.ap[:, 0:4], self.ogrp.ap[:, :].rearrange(
                        "p (a b) -> p a b", a=4), AX.X, ALU.add),
                    reads=[self.ogrp], writes=[self.rst])
                self.S_(lambda s: s.activation(self.rst.ap[:, 0:4],
                                               self.rst.ap[:, 0:4], ACT.Ln,
                                               scale=1.0 / 128.0,
                                               bias=self.cpool.ap[:, 2:3]),
                        reads=[self.cpool], writes=[self.rst])
                self.S_(lambda s: s.activation(self.rst.ap[:, 0:4],
                                               self.rst.ap[:, 0:4], ACT.Exp,
                                               scale=-0.5), writes=[self.rst])
                self.V_(lambda v, bO=bO: v.tensor_tensor(
                    self.ogrp.ap[:, :].rearrange("p (a b) -> p a b", a=4),
                    bO.ap[:, :].rearrange("p (a b) -> p a b", a=4),
                    bc_in(self.rst.ap[:, 0:4], 128), ALU.mult),
                    reads=[bO, self.rst], writes=[self.ogrp])
                self.V_(lambda v, hlo=hlo: v.tensor_tensor(
                    self.gOF.ap[:, :, :].rearrange("p a b -> p (a b)"),
                    self.ogrp.ap[:, :],
                    self.gsig.ap[:, hlo * 128:(hlo + 4) * 128], ALU.mult),
                    reads=[self.ogrp, self.gsig], writes=[self.gOF])
                for hi in range(4):
                    self.T_(lambda t, hi=hi, bTR=bTR: t.transpose(
                        bTR.bf[:, 512 + hi * 128:512 + (hi + 1) * 128],
                        self.gOF.ap[:, hi, :], self.identb.ap[:, :]),
                        reads=[self.gOF, self.identb],
                        writes=[bTR])
                self.V_(lambda v, bTR=bTR: v.tensor_copy(
                    self.oTst.ap[:, :, :],
                    bTR.bf[:, 512:1024].rearrange("p (a b) -> p a b", a=4)),
                    reads=[bTR], writes=[self.oTst])
                self.dma_store(
                    self.o_scr.rearrange("(s p) t -> p s t", p=128)[
                        :, hlo:hlo + 4, ch * 128:(ch + 1) * 128],
                    self.oTst.ap[:, :, :], [self.oTst])
            self.emit_ktm(self.ktm, ch, bank0=4)
            self.emit_states(self.ktm, self.bwd, self.diagB, ch, bank0=4)

    def b3_wo_ln1(self):
        T_, V_, S_ = self.T_, self.V_, self.S_
        I = self.I
        self.emit_warmup()
        st_toks = [(f"st{i}", self.P.cnt[f"st{i}"])
                   for i in range(self.P.LANES["st"]) if self.P.cnt[f"st{i}"]]
        self.dma("sync", self.oT.ap[:, :, :],
                 self.o_scr.rearrange("(s p) t -> p s t", p=128),
                 writes=[self.oT], extra=st_toks)
        for band in range(2):
            for i in range(8):
                sl = band * 8 + i
                wb = self.wo_b[i]
                self.dma("gpsimd", wb.ap[:, :],
                         I["Wo"][sl * 128:(sl + 1) * 128, :], writes=[wb])
                self.V_(lambda v, wb=wb: v.tensor_scalar(
                    wb.ap[:, :], wb.ap[:, :], self.nwcol.ap[:, 0:1], None,
                    ALU.mult), reads=[self.nwcol], writes=[wb])
            for tc in range(NCH):
                par = tc % 2
                if band == 0:
                    xb = self.xs[0]
                    self.dma("sync", xb.ap[:, :],
                             I["h_nat"][tc * 128:(tc + 1) * 128, :], writes=[xb])
                for j in range(4):
                    pb = self.banks[par * 4 + j]
                    for i in range(8):
                        sl = band * 8 + i
                        wb = self.wo_b[i]
                        self.T_(lambda t, i=i, j=j, tc=tc, sl=sl, pb=pb, wb=wb:
                                t.matmul(pb.ap[:, :],
                                         self.oT.ap[:, sl, tc * 128:(tc + 1) * 128],
                                         wb.ap[:, j * 512:(j + 1) * 512],
                                         start=(i == 0), stop=(i == 7)),
                                reads=[self.oT, wb],
                                writes=[pb])
                for j in range(4):
                    pb = self.banks[par * 4 + j]
                    if band == 0:
                        xb = self.xs[0]
                        self.V_(lambda v, j=j, tc=tc, pb=pb, xb=xb: v.tensor_tensor(
                            self.r1a.ap[:, tc, j * 512:(j + 1) * 512], pb.ap[:, :],
                            xb.ap[:, j * 512:(j + 1) * 512], ALU.add),
                            reads=[pb, xb], writes=[self.r1a])
                    else:
                        self.V_(lambda v, j=j, tc=tc, pb=pb: v.tensor_tensor(
                            self.r1a.ap[:, tc, j * 512:(j + 1) * 512],
                            self.r1a.ap[:, tc, j * 512:(j + 1) * 512],
                            pb.ap[:, :], ALU.add),
                            reads=[pb], writes=[self.r1a])
        for tc in range(NCH):
            self.emit_ln(tc, to_xT=True)
        self.tap("x1", self.r1a, [128, NCH, HID], F32)

    def emit_ln(self, tc, to_xT):
        V_, S_, T_ = self.V_, self.S_, self.T_
        acc = self.r1a
        sq = self.sqscr if to_xT else self.sqscr2
        self.V_(lambda v, tc=tc: v.tensor_reduce(
            self.lnc.ap[:, 0:1], acc.ap[:, tc, :], AX.X, ALU.add),
            reads=[acc], writes=[self.lnc])
        self.V_(lambda v: v.tensor_scalar(self.lnc.ap[:, 0:1], self.lnc.ap[:, 0:1],
                                          1.0 / HID, None, ALU.mult),
                writes=[self.lnc])
        self.S_(lambda s, tc=tc, sq=sq: s.activation(
            sq.ap[:, :], acc.ap[:, tc, :], ACT.Square,
            accum_out=self.lnc.ap[:, 1:2]),
            reads=[acc], writes=[sq, self.lnc])
        self.V_(lambda v: v.tensor_scalar(self.lnc.ap[:, 1:2], self.lnc.ap[:, 1:2],
                                          1.0 / HID, None, ALU.mult),
                writes=[self.lnc])
        self.V_(lambda v: v.tensor_mul(self.lnc.ap[:, 2:3], self.lnc.ap[:, 0:1],
                                       self.lnc.ap[:, 0:1]), writes=[self.lnc])
        self.V_(lambda v: v.tensor_sub(self.lnc.ap[:, 1:2], self.lnc.ap[:, 1:2],
                                       self.lnc.ap[:, 2:3]), writes=[self.lnc])
        self.S_(lambda s: s.activation(self.lnc.ap[:, 3:4], self.lnc.ap[:, 1:2],
                                       ACT.Ln, bias=self.cpool.ap[:, 2:3]),
                reads=[self.cpool], writes=[self.lnc])
        self.S_(lambda s: s.activation(self.lnc.ap[:, 3:4], self.lnc.ap[:, 3:4],
                                       ACT.Exp, scale=-0.5), writes=[self.lnc])
        self.V_(lambda v, tc=tc: v.tensor_scalar(
            acc.ap[:, tc, :], acc.ap[:, tc, :], self.lnc.ap[:, 0:1], None,
            ALU.subtract), reads=[self.lnc], writes=[acc])
        self.V_(lambda v, tc=tc: v.tensor_scalar(
            acc.ap[:, tc, :], acc.ap[:, tc, :], self.lnc.ap[:, 3:4], None,
            ALU.mult), reads=[self.lnc], writes=[acc])
        if not to_xT:
            return
        self.V_(lambda v, tc=tc, sq=sq: v.tensor_copy(sq.ap[:, :],
                                                      acc.ap[:, tc, :]),
                reads=[acc], writes=[sq])
        for g8 in range(2):
            pbT = self.banks[4 + g8]
            for hi in range(8):
                sl = g8 * 8 + hi
                self.T_(lambda t, sl=sl, hi=hi, pbT=pbT, sq=sq: t.transpose(
                    pbT.bf[:, hi * 128:(hi + 1) * 128],
                    sq.ap[:, sl * 128:(sl + 1) * 128], self.identb.ap[:, :]),
                    reads=[sq, self.identb], writes=[pbT])
            self.V_(lambda v, g8=g8, tc=tc, pbT=pbT: v.tensor_copy(
                self.xT.ap[:, g8 * 8:(g8 + 1) * 8, tc * 128:(tc + 1) * 128],
                pbT.bf[:, 0:1024].rearrange("p (a b) -> p a b", a=8)),
                reads=[pbT], writes=[self.xT])

    def b4_ffn(self):
        T_, V_, S_ = self.T_, self.V_, self.S_
        I = self.I

        def emit_gateup(hb, hs):
            hidx = hb * 8 + hs
            par = hs % 2
            fg, fu = self.ffg[par], self.ffu[par]
            gbb = self.gb[hb % 2]
            self.dma("gpsimd", fg.ap[:, :, :],
                     I["wgT"][:, hidx * 128:(hidx + 1) * 128].rearrange(
                         "(a p) c -> p a c", p=128), writes=[fg])
            self.dma("gpsimd", fu.ap[:, :, :],
                     I["wuT"][:, hidx * 128:(hidx + 1) * 128].rearrange(
                         "(a p) c -> p a c", p=128), writes=[fu])
            bg = self.banks[4 + par]
            bu = self.banks[6 + par]
            for half in range(2):
                for (bk, fw) in ((bg, fg), (bu, fu)):
                    for a in range(NSL):
                        self.T_(lambda t, a=a, half=half, bk=bk, fw=fw: t.matmul(
                            bk.ap[:, :], fw.ap[:, a, :],
                            self.xT.ap[:, a, half * 512:(half + 1) * 512],
                            start=(a == 0), stop=(a == NSL - 1)),
                            reads=[fw, self.xT], writes=[bk])
                self.S_(lambda s, bg=bg, half=half: s.activation(
                    self.sgf.ap[:, half * 512:(half + 1) * 512], bg.ap[:, :],
                    ACT.Silu), reads=[bg], writes=[self.sgf])
                self.V_(lambda v, bu=bu, hs=hs, half=half, gbb=gbb:
                        v.tensor_tensor(
                    gbb.ap[:, hs, half * 512:(half + 1) * 512], bu.ap[:, :],
                    self.sgf.ap[:, half * 512:(half + 1) * 512], ALU.mult),
                    reads=[bu, self.sgf], writes=[gbb])

        def emit_down_unit(hb, unit):
            tc, jh = unit // 2, unit % 2
            pair = unit % 2
            gbb = self.gb[hb % 2]
            for jj in range(2):
                j = jh * 2 + jj
                pb = self.banks[pair * 2 + jj]
                for i in range(8):
                    wd = self.wd_all[i]
                    self.T_(lambda t, i=i, j=j, tc=tc, pb=pb, wd=wd, gbb=gbb:
                            t.matmul(
                        pb.ap[:, :], gbb.ap[:, i, tc * 128:(tc + 1) * 128],
                        wd.ap[:, j * 512:(j + 1) * 512],
                        start=(i == 0), stop=(i == 7)),
                        reads=[gbb, wd], writes=[pb])
            for jj in range(2):
                j = jh * 2 + jj
                pb = self.banks[pair * 2 + jj]
                self.V_(lambda v, j=j, tc=tc, pb=pb: v.tensor_tensor(
                    self.r1a.ap[:, tc, j * 512:(j + 1) * 512],
                    self.r1a.ap[:, tc, j * 512:(j + 1) * 512],
                    pb.ap[:, :], ALU.add),
                    reads=[pb], writes=[self.r1a])

        def emit_wd_loads(hb):
            for i in range(8):
                hidx = hb * 8 + i
                wd = self.wd_all[i]
                self.dma("gpsimd", wd.ap[:, :],
                         I["wdT"][hidx * 128:(hidx + 1) * 128, :], writes=[wd])

        self.emit_warmup()
        emit_wd_loads(0)
        for hs in range(8):
            emit_gateup(0, hs)
        for hb in range(1, 8):
            for hs in range(8):
                emit_down_unit(hb - 1, 2 * hs)
                emit_down_unit(hb - 1, 2 * hs + 1)
                emit_gateup(hb, hs)
            emit_wd_loads(hb)
        for unit in range(16):
            emit_down_unit(7, unit)
        for tc in range(NCH):
            self.emit_ln(tc, to_xT=False)
            self.dma_store(self.out[tc * 128:(tc + 1) * 128, :],
                           self.r1a.ap[:, tc, :], [self.r1a])

    def finish(self):
        toks = [(f"st{i}", self.P.cnt[f"st{i}"])
                for i in range(self.P.LANES["st"]) if self.P.cnt[f"st{i}"]]
        if toks:
            self.P.emit("gpsimd", lambda g: g.engine_nop(), toks, "gs", 1)


def build_host_inputs(inputs):
    h = np.ascontiguousarray(np.asarray(inputs["L0_attn_hidden_states"],
                                        np.float32)[0])
    hT_full = np.ascontiguousarray(h.T)
    per_core = []
    for c in range(NCORES):
        lo = c * Tc
        hT_ext = np.zeros((HID, Tc + 3), np.float32)
        if c == 0:
            hT_ext[:, 3:] = hT_full[:, 0:Tc]
        else:
            hT_ext[:, :] = hT_full[:, lo - 3:lo + Tc]
        Mm = np.zeros((9, 8), np.float32)
        for cp in range(8):
            if cp < c:
                for cm in range(cp + 1, c):
                    Mm[cm, cp] = 1.0
            else:
                Mm[8, cp] = 1e5
        per_core.append({
            "hT_ext": hT_ext,
            "h_nat": np.ascontiguousarray(h[lo:lo + Tc]),
            "Mmat": Mm,
        })
    f32 = lambda k: np.ascontiguousarray(np.asarray(inputs[k], np.float32))
    shared = {
        "Wq": f32("L0_attn_Wq"), "Wk": f32("L0_attn_Wk"),
        "Wv": f32("L0_attn_Wv"), "Ww": f32("L0_attn_Ww"),
        "Wo": f32("L0_attn_Wo"),
        "cq": f32("L0_attn_cq"), "ck": f32("L0_attn_ck"),
        "cv": f32("L0_attn_cv"),
        "Wf1": f32("L0_attn_Wf1"), "Wf2": f32("L0_attn_Wf2"),
        "Wb": f32("L0_attn_Wb"),
        "Wg1": f32("L0_attn_Wg1"), "Wg2": f32("L0_attn_Wg2"),
        "bg2": f32("L0_attn_bg2"), "normw": f32("L0_attn_norm_w"),
        "wgT": np.ascontiguousarray(np.asarray(inputs["L0_ffn_w_gate"],
                                               np.float32).T),
        "wuT": np.ascontiguousarray(np.asarray(inputs["L0_ffn_w_up"],
                                               np.float32).T),
        "wdT": np.ascontiguousarray(np.asarray(inputs["L0_ffn_w_down"],
                                               np.float32).T),
    }
    return [{**shared, **pc} for pc in per_core]


_BUILD_CACHE = {}


def kernel(**inputs):
    in_maps = build_host_inputs(inputs)
    key = (STAGES, WANT_DBG)
    if key not in _BUILD_CACHE:
        b = Builder()
        nc = b.build()
        _BUILD_CACHE[key] = (nc, b)
    nc, b = _BUILD_CACHE[key]
    res = run_bass_kernel_spmd(nc, in_maps, core_ids=list(range(NCORES)))
    out = np.concatenate([r["out"] for r in res.results], axis=0)
    return out[None]
